# version 03
# baseline (speedup 1.0000x reference)
import contextlib
import numpy as np
import concourse.bass as bass
import concourse.mybir as mybir
from concourse.bass_utils import run_bass_kernel_spmd

F32 = mybir.dt.float32
BF16 = mybir.dt.bfloat16
U32 = mybir.dt.uint32
AF = mybir.ActivationFunctionType
ALU = mybir.AluOpType
AX = mybir.AxisListType

T = 4096
D = 1024
NT = 32
FT = 8
NH = 16
HD = 64
EPS0 = 1e-6

ENGS = ("tensor", "vector", "scalar", "gpsimd", "sync")


class Prog:
    def __init__(self, nc):
        self.nc = nc
        self.ops = []
        self.last_w = {}
        self.readers = {}

    def op(self, eng, emit, r=(), w=(), dma=False, tag=""):
        idx = len(self.ops)
        deps = set()
        for k in r:
            if k in self.last_w:
                deps.add(self.last_w[k])
        for k in w:
            if k in self.last_w:
                deps.add(self.last_w[k])
            for rd in self.readers.get(k, ()):
                deps.add(rd)
        deps.discard(idx)
        self.ops.append(dict(eng=eng, emit=emit, deps=deps, dma=dma,
                             idx=idx, tag=tag))
        for k in r:
            self.readers.setdefault(k, []).append(idx)
        for k in w:
            self.last_w[k] = idx
            self.readers[k] = [idx]
        return idx

    def marker(self, ctx, r, w, tag="marker"):
        self.op("vector", lambda: ctx.nc.vector.memset(
            ctx.sb["stats"][:, 127:128], 0.0), r=r, w=list(w) + [("st", 127)],
            tag=tag)

    def final_barrier(self):
        deps = set()
        for i, o in enumerate(self.ops):
            if o["dma"]:
                deps.add(i)
        lastof = {}
        for i, o in enumerate(self.ops):
            lastof[o["eng"]] = i
        deps.update(lastof.values())
        self.ops.append(dict(eng="sync", emit=None, deps=deps, dma=False,
                             idx=len(self.ops), tag="final"))

    SERIALIZE_DMA = True

    def emit(self, sems, block):
        ops = self.ops
        if self.SERIALIZE_DMA:
            prev = {}
            for o in ops:
                if o["dma"]:
                    if o["eng"] in prev:
                        o["deps"].add(prev[o["eng"]])
                    prev[o["eng"]] = o["idx"]
        need_inc = [False] * len(ops)

        def edge_needs_sem(od, o):
            if od["eng"] != o["eng"] or od["dma"]:
                return True
            return od["eng"] != "tensor"

        def chan(o):
            return o["eng"] + "_dma" if o["dma"] else o["eng"]

        for o in ops:
            for d in o["deps"]:
                od = ops[d]
                if edge_needs_sem(od, o):
                    need_inc[d] = True
        cum = {}
        for o in ops:
            c = chan(o)
            if need_inc[o["idx"]]:
                cum[c] = cum.get(c, 0) + (16 if o["dma"] else 1)
            o["cum"] = cum.get(c, 0)
        for o in ops:
            wv = {}
            for d in o["deps"]:
                od = ops[d]
                if not edge_needs_sem(od, o):
                    continue
                if not need_inc[od["idx"]]:
                    continue
                c = chan(od)
                wv[c] = max(wv.get(c, 0), od["cum"])
            o["waits"] = wv
        stats = {e: sum(1 for o in ops if o["eng"] == e) for e in ENGS}

        def run_engine(handle, name):
            for o in ops:
                if o["eng"] != name:
                    continue
                for e, v in sorted(o["waits"].items()):
                    handle.wait_ge(sems[e], v)
                if o["emit"] is None:
                    continue
                inst = o["emit"]()
                if need_inc[o["idx"]]:
                    inst.then_inc(sems[chan(o)], 16 if o["dma"] else 1)

        @block.sync
        def _(h):
            run_engine(h, "sync")

        @block.tensor
        def _(h):
            run_engine(h, "tensor")

        @block.vector
        def _(h):
            run_engine(h, "vector")

        @block.scalar
        def _(h):
            run_engine(h, "scalar")

        @block.gpsimd
        def _(h):
            run_engine(h, "gpsimd")
        return stats


def make_const_arrays():
    c = {}
    c["c_ident_f32"] = np.eye(128, dtype=np.float32)
    c["c_ident_bf"] = np.eye(128, dtype=np.float32)
    c["c_ones_row"] = np.ones((1, 128), dtype=np.float32)
    s_i = np.arange(128)
    c["c_triu_incl"] = -np.triu(np.ones((128, 128), np.float32))
    e127 = np.zeros((128, 128), np.float32)
    e127[127, :] = 1.0
    c["c_e127"] = e127
    c["c_maskT"] = np.where(s_i[:, None] < s_i[None, :], 0.0,
                            -1e30).astype(np.float32)
    blk = s_i // 32
    c["c_m32bd"] = (blk[:, None] == blk[None, :]).astype(np.float32)
    c["c_m32off"] = (blk[:, None] < blk[None, :]).astype(np.float32)
    return c


class Ctx:
    def __init__(self, nc, prog, sb, ps, ins, consts):
        self.nc = nc
        self.prog = prog
        self.sb = sb
        self.ps = ps
        self.ins = ins
        self.c = consts


def alloc_all(stack, nc, need_l1=True):
    sb = {}
    ps = {}

    def S(name, shape, dtype):
        sb[name] = stack.enter_context(nc.sbuf_tensor(name, shape, dtype))

    def P(name, shape, dtype=F32):
        ps[name] = stack.enter_context(nc.psum_tensor(name, shape, dtype))

    S("x_res", [128, NT * D], F32)
    S("wbuf", [128, 16 * 1024], BF16)
    S("work_bf", [128, 4096], BF16)
    S("work_b", [128, 2048], F32)
    S("l1big", [128, 2048], BF16)
    S("modt", [128, 3 * D], BF16)
    S("biast", [128, 2 * D], BF16)
    S("mrows", [1, D], F32)
    S("stats", [128, 128], F32)
    S("kvsb", [128, 512], BF16)
    S("xstage", [128, D], F32)
    S("gwT", [128, 128], BF16)
    S("gb16", [128, 16], F32)
    S("bemat", [128, 256], F32)
    S("cmb", [128, 17 * NT], F32)
    S("ident_f32", [128, 128], F32)
    S("ident_bf", [128, 128], BF16)
    S("ones_row", [1, 128], F32)
    S("ctrilu", [128, 128], F32)
    S("ce127", [128, 128], F32)
    S("cmaskT", [128, 128], F32)
    S("cm32bd", [128, 128], BF16)
    S("cm32off", [128, 128], BF16)
    sb["hTw"] = sb["l1big"][:, 0:1024]
    sb["acttrash"] = sb["l1big"][:, 1024:2048]
    if need_l1:
        pass
    P("psA", [128, 512], F32)
    P("psB", [128, 512], F32)
    P("psC", [128, 512], F32)
    P("psD", [128, 512], F32)
    P("psKV", [128, 512], F32)
    P("psT", [128, 1024], BF16)
    P("psE", [128, 512], F32)
    P("psF", [128, 512], F32)
    return sb, ps



def load_x(ctx):
    nc, p = ctx.nc, ctx.prog
    x = ctx.ins["x"].rearrange("(t p) f -> p t f", p=128)
    dst = ctx.sb["x_res"][:, :].rearrange("p (t f) -> p t f", f=D)
    p.op("sync", lambda: nc.sync.dma_start(out=dst, in_=x),
         w=[("x", t) for t in range(NT)], dma=True, tag="ldx")


def store_x(ctx, out_name="out"):
    nc, p = ctx.nc, ctx.prog
    dst = ctx.ins[out_name].rearrange("(t p) f -> p t f", p=128)
    src = ctx.sb["x_res"][:, :].rearrange("p (t f) -> p t f", f=D)
    p.op("sync", lambda: nc.sync.dma_start(out=dst, in_=src),
         r=[("x", t) for t in range(NT)], dma=True, tag="stx")


def load_consts(ctx, names=("c_ident_f32", "c_ones_row")):
    nc, p = ctx.nc, ctx.prog
    key = {"c_ident_f32": "ident_f32", "c_ones_row": "ones_row"}
    for n in names:
        k = key[n]
        p.op("sync", lambda n=n, k=k: nc.sync.dma_start(
            out=ctx.sb[k][0:ctx.ins[n].shape[0], :], in_=ctx.ins[n]),
            w=[k], dma=True, tag="ldconst")
    p.op("gpsimd", lambda: nc.gpsimd.dma_start(
        out=ctx.sb["ident_bf"][:, :], in_=ctx.ins["c_ident_bf"]),
        w=["ident_bf"], dma=True, tag="ldconst")


WSLOT = [0, 8 * 1024]


def stream_weight(ctx, in_name, slot, rows=1024, tag="ldw", half=None):
    nc, p = ctx.nc, ctx.prog
    nt = rows // 128
    base = WSLOT[slot] + (0 if half is None else half * 4 * 1024)
    src = ctx.ins[in_name].rearrange("(c p) f -> p c f", p=128)
    dst = ctx.sb["wbuf"][:, base:base + nt * 1024].rearrange(
        "p (c f) -> p c f", f=1024)
    wk = [("wslot", slot) if half is None else ("wslot", slot, half)]
    if slot == 1 and rows == 1024:
        wk.append("ctTregion")
    p.op("gpsimd", lambda: nc.gpsimd.dma_start(out=dst, in_=src),
         w=wk, dma=True, tag=tag)


def wsl(ctx, slot, ci, f0, fn, half=None):
    base = WSLOT[slot] + (0 if half is None else half * 4 * 1024)
    o = base + ci * 1024 + f0
    return ctx.sb["wbuf"][:, o:o + fn]


def bcast_row(ctx, src_ap, dst_ap, keys_r, key_w, add_const=None,
              bank="psC", tag="bcast"):
    nc, p = ctx.nc, ctx.prog
    n = src_ap.shape[-1]
    b = ctx.ps[bank]
    for h in range((n + 511) // 512):
        w0 = h * 512
        wn = min(512, n - w0)
        p.op("tensor", lambda w0=w0, wn=wn: nc.tensor.matmul(
            b[0:128, 0:wn], ctx.sb["ones_row"][0:1, :],
            src_ap[0:1, w0:w0 + wn], start=True, stop=True),
            r=list(keys_r) + ["ones_row"], w=[bank], tag=tag)
        if add_const is None:
            p.op("vector", lambda w0=w0, wn=wn: nc.vector.tensor_copy(
                dst_ap[:, w0:w0 + wn], b[0:128, 0:wn]),
                r=[bank], w=[key_w], tag=tag + "e")
        else:
            p.op("vector", lambda w0=w0, wn=wn, a=float(add_const):
                 nc.vector.tensor_scalar(
                     out=dst_ap[:, w0:w0 + wn], in0=b[0:128, 0:wn],
                     scalar1=a, scalar2=None, op0=ALU.add),
                 r=[bank], w=[key_w], tag=tag + "e")


def emit_mod(ctx, layer, half):
    nc, p = ctx.nc, ctx.prog
    pref = "L0" if layer == 0 else "L1"
    st = ctx.sb["stats"]
    stage = ctx.sb["mrows"]
    banks = ["psA", "psB", "psC", "psD", "psE", "psF"]
    kc = ("cfm", layer)
    c_ap = ctx.ins[f"{pref}_c"].rearrange("(k p) -> p k", p=128)
    p.op("sync", lambda: nc.sync.dma_start(
        out=st[:, 96:104], in_=c_ap, allow_slow_non_contiguous=True),
        w=[kc], dma=True, tag="ldc")
    p.op("scalar", lambda: nc.scalar.activation(
        out=st[:, 104:112], in_=st[:, 96:104], func=AF.Silu),
        r=[kc], w=[("cfs", layer)], tag="silc")
    wst = ctx.sb["wbuf"].bitcast(F32)
    for ci in range(8):
        wm = ctx.ins[f"{pref}_wmod"]
        j0 = half * 6
        p.op("sync", lambda ci=ci, wm=wm, j0=j0: nc.sync.dma_start(
            out=wst[:, 0:3072],
            in_=wm[128 * ci:128 * ci + 128, 512 * j0:512 * j0 + 3072]),
            w=[("wslot", 0), ("wslot", 1)], dma=True, tag="ldwmod")
        for jj in range(6):
            b = ctx.ps[banks[jj]]
            bkey = banks[jj]
            p.op("tensor", lambda ci=ci, jj=jj, b=b: nc.tensor.matmul(
                b[0:1, 0:512], st[:, 104 + ci:105 + ci],
                wst[:, 512 * jj:512 * jj + 512],
                start=(ci == 0), stop=(ci == 7)),
                r=[("cfs", layer), ("wslot", 0), ("wslot", 1)],
                w=[bkey], tag="modmm")
    for r in range(3):
        for chunk in range(2):
            jj = r * 2 + chunk
            b = ctx.ps[banks[jj]]
            bkey = banks[jj]
            p.op("vector", lambda chunk=chunk, b=b: nc.vector.tensor_copy(
                stage[0:1, 512 * chunk:512 * chunk + 512], b[0:1, 0:512]),
                r=[bkey], w=["mstage"], tag="modevac")
        if r == 2:
            p.op("scalar", lambda: nc.scalar.activation(
                out=ctx.sb["acttrash"][0:1, 0:D], in_=stage[0:1, :],
                func=AF.Sign),
                r=["mstage"], w=["acttrash", "acttrash2"], tag="gsign")
            p.op("scalar", lambda: nc.scalar.activation(
                out=ctx.sb["mrows"][0:1, 0:D], in_=stage[0:1, :],
                func=AF.Abs),
                r=["mstage"], w=["mstage"], tag="gabs0")
            p.op("vector", lambda: nc.vector.tensor_scalar(
                out=stage[0:1, :], in0=ctx.sb["mrows"][0:1, 0:D],
                scalar1=0.25, scalar2=None, op0=ALU.max),
                r=["mstage"], w=["mstage"], tag="gabs")
            p.op("vector", lambda: nc.vector.tensor_mul(
                stage[0:1, :], stage[0:1, :], ctx.sb["acttrash"][0:1, 0:D]),
                r=["acttrash", "mstage"], w=["mstage"], tag="gmul")
        dst = ctx.sb["modt"][:, r * D:(r + 1) * D]
        add = 1.0 if r == 1 else None
        bcast_row(ctx, stage[0:1, :], dst, ["mstage"], ("modt", r),
                  add_const=add, bank=banks[r], tag="modt")


MOD_SLOT = {"sa": 0, "sb": 0, "ca1p": 1, "cb1p": 1, "ga": 2, "gb": 2}


def emit_bias_tile(ctx, in_name, dst_slot, tag="biast"):
    nc, p = ctx.nc, ctx.prog
    row = ctx.sb["mrows"][0:1, 0:D]
    p.op("sync", lambda: nc.sync.dma_start(
        out=row, in_=ctx.ins[in_name][None, :]),
        w=["mstage"], dma=True, tag="ldb")
    dst = ctx.sb["biast"][:, dst_slot * D:(dst_slot + 1) * D]
    bcast_row(ctx, row, dst, ["mstage"], ("biast", dst_slot),
              bank="psD", tag=tag)


def emit_h_tile(ctx, t, mul_slot, add_slot, dst, dkey, transpose=True,
                xt_ap=None, xt_key=None, dst_col=None, tag="h"):
    nc, p = ctx.nc, ctx.prog
    st = ctx.sb["stats"]
    work = ctx.sb["work_b"]
    if xt_ap is None:
        xt = ctx.sb["x_res"][:, t * D:(t + 1) * D]
        xk = ("x", t)
    else:
        xt, xk = xt_ap, xt_key
    out_f32 = (transpose == "f32")
    if out_f32:
        transpose = False
    mm = ctx.sb["modt"][:, mul_slot * D:(mul_slot + 1) * D]
    ma = ctx.sb["modt"][:, add_slot * D:(add_slot + 1) * D]
    psT = ctx.ps["psT"]
    hbf = ctx.sb["work_bf"][:, 0:D] if transpose else dst
    hbk = ("wbf", 0) if transpose else (dkey, "tm")
    p.op("vector", lambda: nc.vector.tensor_reduce(
        st[:, 0:1], xt, AX.X, ALU.add),
        r=[xk], w=[("st", 0)], tag=tag + "s")
    p.op("scalar", lambda: nc.scalar.activation(
        out=ctx.sb["acttrash"][:, 0:D], in_=xt, func=AF.Square),
        r=[xk], w=["acttrash", "acttrash2"], tag=tag + "q0")
    p.op("vector", lambda: nc.vector.tensor_reduce(
        st[:, 1:2], ctx.sb["acttrash"][:, 0:D], AX.X, ALU.add),
        r=["acttrash"], w=[("st", 1)], tag=tag + "q")
    p.op("vector", lambda: nc.vector.tensor_scalar(
        out=st[:, 2:3], in0=st[:, 0:1], scalar1=1.0 / D, scalar2=None,
        op0=ALU.mult), r=[("st", 0)], w=[("st", 2)], tag=tag)
    p.op("vector", lambda: nc.vector.tensor_scalar(
        out=st[:, 3:4], in0=st[:, 1:2], scalar1=1.0 / D, scalar2=None,
        op0=ALU.mult), r=[("st", 1)], w=[("st", 3)], tag=tag)
    p.op("vector", lambda: nc.vector.tensor_mul(
        st[:, 4:5], st[:, 2:3], st[:, 2:3]),
        r=[("st", 2)], w=[("st", 4)], tag=tag)
    p.op("vector", lambda: nc.vector.tensor_sub(
        st[:, 5:6], st[:, 3:4], st[:, 4:5]),
        r=[("st", 3), ("st", 4)], w=[("st", 5)], tag=tag)
    p.op("vector", lambda: nc.vector.tensor_scalar(
        out=st[:, 5:6], in0=st[:, 5:6], scalar1=1e-5, scalar2=None,
        op0=ALU.add),
        r=[("st", 5)], w=[("st", 5)], tag=tag)
    p.op("scalar", lambda: nc.scalar.activation(
        out=st[:, 6:7], in_=st[:, 5:6], func=AF.Ln),
        r=[("st", 5)], w=[("st", 6)], tag=tag)
    p.op("scalar", lambda: nc.scalar.activation(
        out=st[:, 7:8], in_=st[:, 6:7], func=AF.Exp, scale=-0.5),
        r=[("st", 6)], w=[("st", 7)], tag=tag)
    p.op("vector", lambda: nc.vector.tensor_scalar(
        out=st[:, 8:9], in0=st[:, 2:3], scalar1=st[:, 7:8], scalar2=-1.0,
        op0=ALU.mult, op1=ALU.mult),
        r=[("st", 2), ("st", 7)], w=[("st", 8)], tag=tag)
    p.op("vector", lambda: nc.vector.scalar_tensor_tensor(
        out=work[:, 0:D], in0=mm, scalar=st[:, 8:9], in1=ma,
        op0=ALU.mult, op1=ALU.add),
        r=[("st", 8), ("modt", mul_slot), ("modt", add_slot)],
        w=["wb01"], tag=tag + "t2")
    p.op("vector", lambda: nc.vector.scalar_tensor_tensor(
        out=work[:, D:2 * D], in0=xt, scalar=st[:, 7:8], in1=mm,
        op0=ALU.mult, op1=ALU.mult),
        r=[xk, ("st", 7), ("modt", mul_slot)], w=["wb23"], tag=tag + "xm")
    if out_f32:
        hbf, hbk = dst, dkey
    p.op("gpsimd", lambda: nc.gpsimd.tensor_add(
        hbf, work[:, D:2 * D], work[:, 0:D]),
        r=["wb01", "wb23"], w=[hbk], tag=tag + "a")
    if not transpose:
        return
    for c in range(FT):
        p.op("tensor", lambda c=c: nc.tensor.transpose(
            psT[0:128, 128 * (c % 4):128 * (c % 4) + 128],
            hbf[:, 128 * c:128 * c + 128], ctx.sb["ident_bf"][:, :]),
            r=[hbk, "ident_bf"], w=["psT"], tag=tag + "tr")
        eng = "vector" if c % 2 == 0 else "scalar"
        fn = nc.vector.tensor_copy if c % 2 == 0 else nc.scalar.copy
        c0 = 128 * c if dst_col is None else dst_col(c)
        wk = (dkey, c) if dst_col is None else (dkey, c, t)
        p.op(eng, lambda c=c, fn=fn, c0=c0: fn(
            dst[:, c0:c0 + 128],
            psT[0:128, 128 * (c % 4):128 * (c % 4) + 128]),
            r=["psT"], w=[wk], tag=tag + "te")



def posfeat_ops(ctx, bank, bkey, sb0, out_ap, outk, bias_slot, with_den,
                tag="pf"):
    nc, p = ctx.nc, ctx.prog
    st = ctx.sb["stats"]
    work = ctx.sb["work_b"]
    wreg = "wb01" if sb0 < 48 else "wb23"
    woff = 0 if sb0 < 48 else 1024
    bt = ctx.sb["biast"][:, bias_slot * D:(bias_slot + 1) * D]
    half = 0 if sb0 < 48 else 1
    bth = bt[:, half * 512:half * 512 + 512]
    p.op("vector", lambda: nc.vector.tensor_add(
        work[:, woff:woff + 512], bank[0:128, 0:512], bth),
        r=[bkey, ("biast", bias_slot)], w=[wreg], tag=tag + "b")
    p.op("vector", lambda: nc.vector.tensor_reduce(
        st[:, sb0:sb0 + 8],
        work[:, woff:woff + 512].rearrange("p (h d) -> p h d", d=HD),
        AX.X, ALU.max),
        r=[wreg], w=[("st", sb0)], tag=tag + "m")
    p.op("gpsimd", lambda: nc.gpsimd.tensor_sub(
        work[:, woff + 512:woff + 1024].rearrange("p (h d) -> p h d", d=HD),
        work[:, woff:woff + 512].rearrange("p (h d) -> p h d", d=HD),
        st[:, sb0:sb0 + 8].rearrange("p (h o) -> p h o", o=1).broadcast_to(
            [128, 8, HD])),
        r=[wreg, ("st", sb0)], w=[wreg], tag=tag + "s")
    p.op("scalar", lambda: nc.scalar.activation(
        out=work[:, woff:woff + 512], in_=work[:, woff + 512:woff + 1024],
        func=AF.Exp),
        r=[wreg], w=[wreg], tag=tag + "e")
    p.op("vector", lambda: nc.vector.tensor_reduce(
        st[:, sb0 + 8:sb0 + 16],
        work[:, woff:woff + 512].rearrange("p (h d) -> p h d", d=HD),
        AX.X, ALU.add),
        r=[wreg], w=[("st", sb0 + 8)], tag=tag + "p")
    p.op("vector", lambda: nc.vector.tensor_scalar(
        out=st[:, sb0 + 16:sb0 + 24], in0=st[:, sb0 + 8:sb0 + 16],
        scalar1=EPS0, scalar2=None, op0=ALU.add),
        r=[("st", sb0 + 8)], w=[("st", sb0 + 16)], tag=tag)
    p.op("vector", lambda: nc.vector.reciprocal(
        st[:, sb0 + 24:sb0 + 32], st[:, sb0 + 16:sb0 + 24]),
        r=[("st", sb0 + 16)], w=[("st", sb0 + 24)], tag=tag)
    if with_den:
        p.op("vector", lambda: nc.vector.tensor_mul(
            st[:, sb0 + 16:sb0 + 24], st[:, sb0 + 8:sb0 + 16],
            st[:, sb0 + 24:sb0 + 32]),
            r=[("st", sb0 + 8), ("st", sb0 + 24)], w=[("st", sb0 + 16)],
            tag=tag + "d")
        p.op("vector", lambda: nc.vector.tensor_scalar(
            out=st[:, sb0 + 16:sb0 + 24], in0=st[:, sb0 + 16:sb0 + 24],
            scalar1=EPS0, scalar2=None, op0=ALU.add),
            r=[("st", sb0 + 16)], w=[("st", sb0 + 16)], tag=tag + "d2")
        p.op("vector", lambda: nc.vector.reciprocal(
            st[:, sb0 + 8:sb0 + 16], st[:, sb0 + 16:sb0 + 24]),
            r=[("st", sb0 + 16)], w=[("st", sb0 + 8)], tag=tag)
        p.op("vector", lambda: nc.vector.tensor_mul(
            st[:, sb0 + 24:sb0 + 32], st[:, sb0 + 24:sb0 + 32],
            st[:, sb0 + 8:sb0 + 16]),
            r=[("st", sb0 + 8)], w=[("st", sb0 + 24)], tag=tag)
    p.op("gpsimd", lambda: nc.gpsimd.tensor_mul(
        out_ap.rearrange("p (h d) -> p h d", d=HD),
        work[:, woff:woff + 512].rearrange("p (h d) -> p h d", d=HD),
        st[:, sb0 + 24:sb0 + 32].rearrange(
            "p (h o) -> p h o", o=1).broadcast_to([128, 8, HD])),
        r=[wreg, ("st", sb0 + 24)], w=[outk], tag=tag + "o")


def emit_l0_attn(ctx):
    nc, p = ctx.nc, ctx.prog
    wbf = ctx.sb["work_bf"]
    psA, psB, psC, psD = (ctx.ps[k] for k in ("psA", "psB", "psC", "psD"))
    psKV, psT = ctx.ps["psKV"], ctx.ps["psT"]
    hTw = ctx.sb["hTw"]

    emit_bias_tile(ctx, "L0_attn_bk", 0)
    emit_bias_tile(ctx, "L0_attn_bv", 1)
    stream_weight(ctx, "L0_attn_Wk", 0, tag="ldWk")
    stream_weight(ctx, "L0_attn_Wv", 1, tag="ldWv")

    kn = wbf[:, 2048:3072]
    vt = wbf[:, 3072:4096]

    for t in range(NT):
        emit_h_tile(ctx, t, MOD_SLOT["ca1p"], MOD_SLOT["sa"], hTw, "hTw",
                    tag=f"hA{t}_")
        for half, (bank, bkey) in enumerate((((psA, "psA")), (psB, "psB"))):
            for ci in range(FT):
                p.op("tensor", lambda half=half, ci=ci, bank=bank:
                     nc.tensor.matmul(
                         bank[0:128, 0:512],
                         hTw[:, 128 * ci:128 * ci + 128],
                         wsl(ctx, 0, ci, half * 512, 512),
                         start=(ci == 0), stop=(ci == FT - 1)),
                     r=[("hTw", ci), ("wslot", 0)], w=[bkey], tag="kproj")
            posfeat_ops(ctx, bank, bkey, 16 + 32 * half,
                        kn[:, half * 512:half * 512 + 512], ("wbf", 2),
                        0, False, tag="pfk")
        for half, (bank, bkey) in enumerate((((psC, "psC")), (psD, "psD"))):
            for ci in range(FT):
                p.op("tensor", lambda half=half, ci=ci, bank=bank:
                     nc.tensor.matmul(
                         bank[0:128, 0:512],
                         hTw[:, 128 * ci:128 * ci + 128],
                         wsl(ctx, 1, ci, half * 512, 512),
                         start=(ci == 0), stop=(ci == FT - 1)),
                     r=[("hTw", ci), ("wslot", 1)], w=[bkey], tag="vproj")
            p.op("vector", lambda half=half, bank=bank: nc.vector.tensor_add(
                vt[:, half * 512:half * 512 + 512], bank[0:128, 0:512],
                ctx.sb["biast"][:, D + half * 512:D + half * 512 + 512]),
                r=["psC" if half == 0 else "psD", ("biast", 1)],
                w=[("wbf", 3)], tag="vbias")
        for h in range(NH):
            pb = 64 * (h % 2)
            fo = 64 * (h // 2)
            p.op("tensor", lambda h=h, pb=pb, fo=fo, t=t: nc.tensor.matmul(
                psKV[pb:pb + 64, fo:fo + 64],
                kn[:, 64 * h:64 * h + 64], vt[:, 64 * h:64 * h + 64],
                start=(t == 0 and h < 2),
                stop=(t == NT - 1 and h >= NH - 2)),
                r=[("wbf", 2), ("wbf", 3)], w=["psKV"], tag="kvmm")

    kvsb = ctx.sb["kvsb"]
    p.op("vector", lambda: nc.vector.tensor_copy(
        kvsb[:, :], psKV[0:128, 0:512]),
        r=["psKV"], w=["kvsb"], tag="kvevac")

    emit_bias_tile(ctx, "L0_attn_bq", 0)
    emit_bias_tile(ctx, "L0_attn_bo", 1)
    stream_weight(ctx, "L0_attn_Wq", 0, tag="ldWq")
    stream_weight(ctx, "L0_attn_Wo", 1, tag="ldWo")

    qn = wbf[:, 2048:3072]
    qnT = wbf[:, 0:1024]
    cfm = wbf[:, 1024:2048]
    for t in range(NT):
        emit_h_tile(ctx, t, MOD_SLOT["ca1p"], MOD_SLOT["sa"], hTw, "hTw",
                    tag=f"hA2_{t}_")
        for half, (bank, bkey) in enumerate((((psA, "psA")), (psB, "psB"))):
            for ci in range(FT):
                p.op("tensor", lambda half=half, ci=ci, bank=bank:
                     nc.tensor.matmul(
                         bank[0:128, 0:512],
                         hTw[:, 128 * ci:128 * ci + 128],
                         wsl(ctx, 0, ci, half * 512, 512),
                         start=(ci == 0), stop=(ci == FT - 1)),
                     r=[("hTw", ci), ("wslot", 0)], w=[bkey], tag="qproj")
            posfeat_ops(ctx, bank, bkey, 16 + 32 * half,
                        qn[:, half * 512:half * 512 + 512], ("wbf", 2),
                        0, True, tag="pfq")
        for c in range(FT):
            p.op("tensor", lambda c=c: nc.tensor.transpose(
                psT[0:128, 128 * (c % 4):128 * (c % 4) + 128],
                qn[:, 128 * c:128 * c + 128], ctx.sb["ident_bf"][:, :]),
                r=[("wbf", 2), "ident_bf"], w=["psT"], tag="qtr")
            eng = "vector" if c % 2 == 0 else "scalar"
            fn = nc.vector.tensor_copy if c % 2 == 0 else nc.scalar.copy
            p.op(eng, lambda c=c, fn=fn: fn(
                qnT[:, 128 * c:128 * c + 128],
                psT[0:128, 128 * (c % 4):128 * (c % 4) + 128]),
                r=["psT"], w=[("wbf", 0)], tag="qtre")
        for h in range(NH):
            pb = 64 * (h % 2)
            fo = 64 * (h // 2)
            pair = h // 2
            bank = psC if pair < 4 else psD
            bkey = "psC" if pair < 4 else "psD"
            fslot = pair % 4
            p.op("tensor", lambda h=h, pb=pb, fo=fo, bank=bank, fslot=fslot:
                 nc.tensor.matmul(
                     bank[pb:pb + 64, 128 * fslot:128 * fslot + 128],
                     kvsb[pb:pb + 64, fo:fo + 64],
                     qnT[pb:pb + 64, 128 * (h // 2):128 * (h // 2) + 128],
                     start=True, stop=True),
                 r=["kvsb", ("wbf", 0)], w=[bkey], tag="ctxmm")
        for c in range(FT):
            bank = psC if c < 4 else psD
            bkey = "psC" if c < 4 else "psD"
            eng = "vector" if c % 2 == 0 else "scalar"
            fn = nc.vector.tensor_copy if c % 2 == 0 else nc.scalar.copy
            p.op(eng, lambda c=c, fn=fn, bank=bank: fn(
                cfm[:, 128 * c:128 * c + 128],
                bank[0:128, 128 * (c % 4):128 * (c % 4) + 128]),
                r=[bkey], w=[("wbf", 1)], tag="ctxevac")
        for half in range(2):
            bank = psA if half == 0 else psB
            bkey = "psA" if half == 0 else "psB"
            for ci in range(FT):
                p.op("tensor", lambda half=half, ci=ci, bank=bank:
                     nc.tensor.matmul(
                         bank[0:128, 0:512],
                         cfm[:, 128 * ci:128 * ci + 128],
                         wsl(ctx, 1, ci, half * 512, 512),
                         start=(ci == 0), stop=(ci == FT - 1)),
                     r=[("wbf", 1), ("wslot", 1)], w=[bkey], tag="womm")
            wreg = "wb01" if half == 0 else "wb23"
            woff = 0 if half == 0 else 1024
            work = ctx.sb["work_b"]
            p.op("vector", lambda half=half, bank=bank, woff=woff:
                 nc.vector.tensor_add(
                     work[:, woff:woff + 512], bank[0:128, 0:512],
                     ctx.sb["biast"][:, D + half * 512:
                                     D + half * 512 + 512]),
                 r=[bkey, ("biast", 1)], w=[wreg], tag="obias")
            p.op("gpsimd", lambda half=half, woff=woff:
                 nc.gpsimd.tensor_mul(
                     work[:, woff + 512:woff + 1024], work[:, woff:woff + 512],
                     ctx.sb["modt"][:, 2 * D + half * 512:
                                    2 * D + half * 512 + 512]),
                 r=[wreg, ("modt", 2)], w=[wreg], tag="oga")
            xt = ctx.sb["x_res"][:, t * D + half * 512:t * D + half * 512 + 512]
            p.op("vector", lambda xt=xt, woff=woff: nc.vector.tensor_add(
                xt, xt, ctx.sb["work_b"][:, woff + 512:woff + 1024]),
                r=[wreg, ("x", t)], w=[("x", t)], tag="oresid")



def build_program(ins_spec, emit_fn, need_l1=False):
    nc = bass.Bass("TRN2", target_bir_lowering=False)
    ins = {}
    for name, (shape, dtype, kind) in ins_spec.items():
        ins[name] = nc.dram_tensor(name, list(shape), dtype, kind=kind).ap()
    with contextlib.ExitStack() as stack:
        sb, ps = alloc_all(stack, nc, need_l1=need_l1)
        prog = Prog(nc)
        ctx = Ctx(nc, prog, sb, ps, ins, None)
        emit_fn(ctx)
        prog.final_barrier()
        sems = {}
        chans = set()
        for o in prog.ops:
            chans.add(o["eng"] + "_dma" if o["dma"] else o["eng"])
        for e in sorted(chans):
            sems[e] = stack.enter_context(nc.semaphore("sem_" + e))
        block = stack.enter_context(nc.Block())
        stats = prog.emit(sems, block)
    return nc, stats


def const_ins_spec():
    spec = {}
    for k, v in make_const_arrays().items():
        dt = F32
        spec[k] = (v.shape, dt, "ExternalInput")
    return spec



def emit_moe(ctx, stop_after=None, experts=None, dbg=None):
    nc, p = ctx.nc, ctx.prog
    st = ctx.sb["stats"]
    x_res = ctx.sb["x_res"]
    hTf = x_res.bitcast(BF16)[:, 0:FT * T]
    yac = x_res.bitcast(BF16)[:, FT * T:2 * FT * T]
    ctT = ctx.sb["wbuf"][0:17, 12288:16384]
    psA, psB, psC, psD = (ctx.ps[k] for k in ("psA", "psB", "psC", "psD"))
    psE, psF = ctx.ps["psE"], ctx.ps["psF"]

    p.op("sync", lambda: nc.sync.dma_start(
        out=ctx.ins["xspill"].rearrange("(t p) f -> p t f", p=128),
        in_=x_res[:, :].rearrange("p (t f) -> p t f", f=D)),
        r=[("x", t) for t in range(NT)], w=["xspilled"], dma=True, tag="spill")

    p.op("vector", lambda: nc.vector.memset(yac, 0.0),
         r=["xspilled"], w=[("y", t) for t in range(NT)], tag="yzero")

    wb = ctx.sb["work_b"]
    p.op("sync", lambda: nc.sync.dma_start(
        out=wb[0:16, 0:1024], in_=ctx.ins["L0_ffn_gate_w"]),
        w=["wb01", "wb23"], dma=True, tag="ldgw")
    for ci in range(8):
        p.op("tensor", lambda ci=ci: nc.tensor.transpose(
            psF[0:128, 0:16], wb[0:16, 128 * ci:128 * ci + 128],
            ctx.sb["ident_f32"][0:16, 0:16]),
            r=["wb01", "wb23", "ident_f32"], w=["psF"], tag="gwtr")
        p.op("vector", lambda ci=ci: nc.vector.tensor_copy(
            ctx.sb["gwT"][:, 16 * ci:16 * ci + 16], psF[0:128, 0:16]),
            r=["psF"], w=["gwT"], tag="gwtre")
    p.op("sync", lambda: nc.sync.dma_start(
        out=ctx.sb["mrows"][0:1, 0:16], in_=ctx.ins["L0_ffn_gate_b"][None, :]),
        w=["mstage"], dma=True, tag="ldgb")
    bcast_row(ctx, ctx.sb["mrows"][0:1, 0:16], ctx.sb["gb16"], ["mstage"],
              "gb16", bank="psE", tag="gbt")
    p.op("sync", lambda: nc.sync.dma_start(
        out=wb[0:16, 0:512], in_=ctx.ins["L0_ffn_be1"]),
        w=["wb01"], dma=True, tag="ldbe1")
    p.op("sync", lambda: nc.sync.dma_start(
        out=wb[0:16, 512:1024], in_=ctx.ins["L0_ffn_be3"]),
        w=["wb23"], dma=True, tag="ldbe3")
    for ci in range(4):
        for which, base in ((0, 0), (1, 64)):
            p.op("tensor", lambda ci=ci, which=which: nc.tensor.transpose(
                psF[0:128, 0:16],
                wb[0:16, 512 * which + 128 * ci:512 * which + 128 * ci + 128],
                ctx.sb["ident_f32"][0:16, 0:16]),
                r=["wb01", "wb23", "ident_f32"], w=["psF"], tag="betr")
            p.op("vector", lambda ci=ci, base=base: nc.vector.tensor_copy(
                ctx.sb["bemat"][:, base + 16 * ci:base + 16 * ci + 16],
                psF[0:128, 0:16]),
                r=["psF"], w=["bemat"], tag="betre")
    p.op("sync", lambda: nc.sync.dma_start(
        out=wb[0:1, 0:1024], in_=ctx.ins["L0_ffn_bs1"][None, :]),
        w=["wb01", "wb23"], dma=True, tag="ldbs1")
    p.op("sync", lambda: nc.sync.dma_start(
        out=wb[1:2, 0:1024], in_=ctx.ins["L0_ffn_bs3"][None, :]),
        w=["wb01", "wb23"], dma=True, tag="ldbs3")
    for ci in range(8):
        p.op("tensor", lambda ci=ci: nc.tensor.transpose(
            psF[0:128, 0:2], wb[0:2, 128 * ci:128 * ci + 128],
            ctx.sb["ident_f32"][0:2, 0:2]),
            r=["wb01", "wb23", "ident_f32"], w=["psF"], tag="bstr")
        p.op("vector", lambda ci=ci: nc.vector.tensor_copy(
            ctx.sb["bemat"][:, 128 + 2 * ci:128 + 2 * ci + 2],
            psF[0:128, 0:2]),
            r=["psF"], w=["bemat"], tag="bstre")

    for t in range(NT):
        p.op("sync", lambda t=t: nc.sync.dma_start(
            out=ctx.sb["xstage"][:, :],
            in_=ctx.ins["xspill"][128 * t:128 * t + 128, :]),
            r=["xspilled", "xstage"], w=["xstage"], dma=True, tag="ldxs")
        emit_h_tile(ctx, t, MOD_SLOT["cb1p"], MOD_SLOT["sb"], hTf, "hTf",
                    xt_ap=ctx.sb["xstage"][:, :], xt_key="xstage",
                    dst_col=lambda c, t=t: T * c + 128 * t, tag=f"hB{t}_")

    for t in range(NT):
        for ci in range(8):
            p.op("tensor", lambda ci=ci, t=t: nc.tensor.matmul(
                psA[0:128, 0:16],
                hTf[:, T * ci + 128 * t: T * ci + 128 * t + 128],
                ctx.sb["gwT"][:, 16 * ci:16 * ci + 16],
                start=(ci == 0), stop=(ci == 7)),
                r=[("hTf", ci, t), "gwT"], w=["psA"], tag="gatemm")
        p.op("vector", lambda: nc.vector.tensor_copy(
            st[:, 16:32], psA[0:128, 0:16]),
            r=["psA"], w=[("st", 16)], tag="gev")
        p.op("vector", lambda: nc.vector.tensor_reduce(
            st[:, 32:33], st[:, 16:32], AX.X, ALU.max),
            r=[("st", 16)], w=[("st", 32)], tag="gmax")
        p.op("vector", lambda: nc.vector.tensor_scalar(
            out=st[:, 16:32], in0=st[:, 16:32], scalar1=st[:, 32:33],
            scalar2=None, op0=ALU.subtract),
            r=[("st", 16), ("st", 32)], w=[("st", 16)], tag="gsub")
        p.op("scalar", lambda: nc.scalar.activation(
            out=st[:, 16:32], in_=st[:, 16:32], func=AF.Exp),
            r=[("st", 16)], w=[("st", 16)], tag="gexp")
        p.op("vector", lambda: nc.vector.tensor_reduce(
            st[:, 32:33], st[:, 16:32], AX.X, ALU.add),
            r=[("st", 16)], w=[("st", 32)], tag="gsum")
        p.op("vector", lambda: nc.vector.reciprocal(
            st[:, 33:34], st[:, 32:33]),
            r=[("st", 32)], w=[("st", 33)], tag="grec")
        p.op("vector", lambda: nc.vector.tensor_scalar(
            out=st[:, 16:32], in0=st[:, 16:32], scalar1=st[:, 33:34],
            scalar2=None, op0=ALU.mult),
            r=[("st", 16), ("st", 33)], w=[("st", 16)], tag="gnorm")
        p.op("vector", lambda: nc.vector.tensor_add(
            st[:, 48:64], st[:, 16:32], ctx.sb["gb16"][:, :]),
            r=[("st", 16), "gb16"], w=[("st", 48)], tag="gsb")
        sbv = st[:, 48:64].rearrange("p (g k) -> p g k", k=4)
        p.op("vector", lambda: nc.vector.tensor_max(
            st[:, 64:68], sbv[:, :, 0], sbv[:, :, 1]),
            r=[("st", 48)], w=[("st", 64)], tag="g1")
        p.op("vector", lambda: nc.vector.tensor_max(
            st[:, 68:72], sbv[:, :, 2], sbv[:, :, 3]),
            r=[("st", 48)], w=[("st", 68)], tag="g2")
        p.op("vector", lambda: nc.vector.tensor_add(
            st[:, 72:76], sbv[:, :, 0], sbv[:, :, 1]),
            r=[("st", 48)], w=[("st", 72)], tag="g3")
        p.op("vector", lambda: nc.vector.tensor_add(
            st[:, 76:80], sbv[:, :, 2], sbv[:, :, 3]),
            r=[("st", 48)], w=[("st", 76)], tag="g4")
        p.op("vector", lambda: nc.vector.tensor_add(
            st[:, 64:68], st[:, 64:68], st[:, 68:72]),
            r=[("st", 64), ("st", 68)], w=[("st", 64)], tag="g5")
        p.op("vector", lambda: nc.vector.tensor_max(
            st[:, 64:68], st[:, 64:68], st[:, 72:76]),
            r=[("st", 64), ("st", 72)], w=[("st", 64)], tag="g6")
        p.op("vector", lambda: nc.vector.tensor_max(
            st[:, 64:68], st[:, 64:68], st[:, 76:80]),
            r=[("st", 64), ("st", 76)], w=[("st", 64)], tag="g7")
        gv = st[:, 64:68]
        p.op("vector", lambda: nc.vector.tensor_max(
            st[:, 72:73], gv[:, 0:1], gv[:, 1:2]),
            r=[("st", 64)], w=[("st", 72)], tag="s3")
        p.op("vector", lambda: nc.vector.tensor_max(
            st[:, 73:74], gv[:, 2:3], gv[:, 3:4]),
            r=[("st", 64)], w=[("st", 73)], tag="s4")
        p.op("vector", lambda: nc.vector.tensor_tensor(
            out=st[:, 74:75], in0=gv[:, 0:1], in1=gv[:, 1:2], op=ALU.min),
            r=[("st", 64)], w=[("st", 74)], tag="s5")
        p.op("vector", lambda: nc.vector.tensor_tensor(
            out=st[:, 75:76], in0=gv[:, 2:3], in1=gv[:, 3:4], op=ALU.min),
            r=[("st", 64)], w=[("st", 75)], tag="s6")
        p.op("vector", lambda: nc.vector.tensor_tensor(
            out=st[:, 76:77], in0=st[:, 72:73], in1=st[:, 73:74], op=ALU.min),
            r=[("st", 72), ("st", 73)], w=[("st", 76)], tag="s7")
        p.op("vector", lambda: nc.vector.tensor_max(
            st[:, 77:78], st[:, 74:75], st[:, 75:76]),
            r=[("st", 74), ("st", 75)], w=[("st", 77)], tag="s8")
        p.op("vector", lambda: nc.vector.tensor_max(
            st[:, 78:79], st[:, 76:77], st[:, 77:78]),
            r=[("st", 76), ("st", 77)], w=[("st", 78)], tag="s9")
        p.op("vector", lambda: nc.vector.tensor_scalar(
            out=st[:, 80:84], in0=st[:, 64:68], scalar1=st[:, 78:79],
            scalar2=None, op0=ALU.is_ge),
            r=[("st", 64), ("st", 78)], w=[("st", 80)], tag="keep")
        p.op("vector", lambda: nc.vector.tensor_scalar(
            out=st[:, 80:84], in0=st[:, 80:84], scalar1=-1.0, scalar2=1e30,
            op0=ALU.add, op1=ALU.mult),
            r=[("st", 80)], w=[("st", 80)], tag="pen")
        p.op("vector", lambda: nc.vector.tensor_tensor(
            out=st[:, 48:64].rearrange("p (g k) -> p g k", k=4),
            in0=sbv,
            in1=st[:, 80:84].rearrange("p (g o) -> p g o", o=1).broadcast_to(
                [128, 4, 4]),
            op=ALU.add),
            r=[("st", 48), ("st", 80)], w=[("st", 48)], tag="gmask")
        p.op("vector", lambda: nc.vector.tensor_reduce(
            st[:, 32:33], st[:, 48:64], AX.X, ALU.max),
            r=[("st", 48)], w=[("st", 32)], tag="t2m")
        p.op("vector", lambda: nc.vector.tensor_scalar(
            out=st[:, 84:100], in0=st[:, 48:64], scalar1=st[:, 32:33],
            scalar2=None, op0=ALU.is_ge),
            r=[("st", 48), ("st", 32)], w=[("st", 84)], tag="eqm")
        p.op("vector", lambda: nc.vector.tensor_scalar(
            out=st[:, 84:100], in0=st[:, 84:100], scalar1=-2e30,
            scalar2=None, op0=ALU.mult),
            r=[("st", 84)], w=[("st", 84)], tag="eqp")
        p.op("vector", lambda: nc.vector.tensor_add(
            st[:, 84:100], st[:, 84:100], st[:, 48:64]),
            r=[("st", 84), ("st", 48)], w=[("st", 84)], tag="m2")
        p.op("vector", lambda: nc.vector.tensor_reduce(
            st[:, 33:34], st[:, 84:100], AX.X, ALU.max),
            r=[("st", 84)], w=[("st", 33)], tag="snd")
        p.op("vector", lambda: nc.vector.tensor_scalar(
            out=st[:, 84:100], in0=st[:, 48:64], scalar1=st[:, 33:34],
            scalar2=None, op0=ALU.is_ge),
            r=[("st", 48), ("st", 33)], w=[("st", 84)], tag="sel")
        p.op("vector", lambda: nc.vector.tensor_mul(
            st[:, 100:116], st[:, 16:32], st[:, 84:100]),
            r=[("st", 16), ("st", 84)], w=[("st", 100)], tag="comb")
        p.op("vector", lambda: nc.vector.memset(st[:, 116:117], 1.0),
             w=[("st", 116)], tag="c16")
        p.op("gpsimd", lambda t=t: nc.gpsimd.tensor_copy(
            ctx.sb["cmb"][:, 17 * t:17 * t + 17], st[:, 100:117]),
            r=[("st", 100), ("st", 116)], w=[("cmb", t)], tag="cmbcp")
        p.op("tensor", lambda t=t: nc.tensor.transpose(
            psF[0:17, 0:128], st[:, 100:117], ctx.sb["ident_f32"][:, :]),
            r=[("st", 100), ("st", 116), "ident_f32"], w=["psF"], tag="ctr")
        p.op("vector", lambda t=t: nc.vector.tensor_copy(
            ctT[0:17, 128 * t:128 * t + 128], psF[0:17, 0:128]),
            r=["psF"], w=[("ctT", t), "ctTregion"], tag="ctre")

    for ei, e in enumerate(range(18) if experts is None else experts):
        if ei == 1 and stop_after == "experts" and "d_ymid" in ctx.ins:
            p.op("gpsimd", lambda: nc.gpsimd.dma_start(
                out=ctx.ins["d_ymid"], in_=yac[:, 0:1024]),
                r=[("y", 0)], dma=True, tag="dymid")
        if e < 16:
            src1 = ctx.ins["L0_ffn_We1"][e]
            src3 = ctx.ins["L0_ffn_We3"][e]
            src2 = ctx.ins["L0_ffn_We2"][e]
            cbrow = e
        else:
            H = e - 16
            src1 = ctx.ins["L0_ffn_ws1"][:, 512 * H:512 * H + 512]
            src3 = ctx.ins["L0_ffn_ws3"][:, 512 * H:512 * H + 512]
            src2 = ctx.ins["L0_ffn_ws2"][512 * H:512 * H + 512, :]
            cbrow = 16
        if dbg and ei == 1:
            if "w1" in dbg:
                src1 = ctx.ins["L0_ffn_ws1"][:, 0:512]
            if "w3" in dbg:
                src3 = ctx.ins["L0_ffn_ws3"][:, 0:512]
            if "w2" in dbg:
                src2 = ctx.ins["L0_ffn_ws2"][0:512, :]
        p.op("gpsimd", lambda src1=src1: nc.gpsimd.dma_start(
            out=ctx.sb["wbuf"][:, 0:8 * 1024].rearrange(
                "p (c f) -> p c f", f=1024)[:, :, 0:512],
            in_=src1.rearrange("(c p) f -> p c f", p=128)),
            w=[("wslot", 0)], dma=True, tag="ldw1")
        p.op("gpsimd", lambda src3=src3: nc.gpsimd.dma_start(
            out=ctx.sb["wbuf"][:, 0:8 * 1024].rearrange(
                "p (c f) -> p c f", f=1024)[:, :, 512:1024],
            in_=src3.rearrange("(c p) f -> p c f", p=128)),
            w=[("wslot", 0)], dma=True, tag="ldw3")
        p.op("gpsimd", lambda src2=src2: nc.gpsimd.dma_start(
            out=ctx.sb["wbuf"][:, 8 * 1024:12 * 1024].rearrange(
                "p (c f) -> p c f", f=1024),
            in_=src2.rearrange("(c p) f -> p c f", p=128)),
            w=[("wslot", 1)], dma=True, tag="ldw2")

        def bias_ap(which, jp, e=e, ei=ei):
            bm = ctx.sb["bemat"]
            if dbg and ei == 1 and "bias" in dbg:
                e = 16
            if e < 16:
                return bm[:, which * 64 + 16 * jp + e:
                          which * 64 + 16 * jp + e + 1]
            H = e - 16
            c0 = 128 + 2 * (4 * H + jp) + which
            return bm[:, c0:c0 + 1]

        for k in range(8):
            hm = ctx.sb["work_bf"]
            for jp in range(4):
                for ci in range(8):
                    p.op("tensor", lambda ci=ci, jp=jp, k=k: nc.tensor.matmul(
                        psA[0:128, 0:512],
                        wsl(ctx, 0, ci, 128 * jp, 128),
                        hTf[:, T * ci + 512 * k: T * ci + 512 * k + 512],
                        start=(ci == 0), stop=(ci == 7)),
                        r=[("wslot", 0)] + [("hTf", ci, 4 * k + j)
                                            for j in range(4)],
                        w=["psA"], tag="up1")
                for ci in range(8):
                    p.op("tensor", lambda ci=ci, jp=jp, k=k: nc.tensor.matmul(
                        psB[0:128, 0:512],
                        wsl(ctx, 0, ci, 512 + 128 * jp, 128),
                        hTf[:, T * ci + 512 * k: T * ci + 512 * k + 512],
                        start=(ci == 0), stop=(ci == 7)),
                        r=[("wslot", 0)] + [("hTf", ci, 4 * k + j)
                                            for j in range(4)],
                        w=["psB"], tag="up3")
                b1ap = bias_ap(0, jp)
                b3ap = bias_ap(1, jp)
                p.op("vector", lambda jp=jp, b1ap=b1ap: nc.vector.tensor_scalar(
                    out=ctx.sb["acttrash"][:, 0:512], in0=psA[0:128, 0:512],
                    scalar1=b1ap, scalar2=None, op0=ALU.add),
                    r=["psA", "bemat"], w=["acttrash"], tag="a1")
                p.op("scalar", lambda jp=jp: nc.scalar.activation(
                    out=hm[:, 512 * jp:512 * jp + 512],
                    in_=ctx.sb["acttrash"][:, 0:512], func=AF.Silu),
                    r=["acttrash"], w=[("wbf", jp // 2)], tag="silu")
                p.op("vector", lambda jp=jp, b3ap=b3ap: nc.vector.tensor_scalar(
                    out=ctx.sb["acttrash"][:, 512:1024], in0=psB[0:128, 0:512],
                    scalar1=b3ap, scalar2=None, op0=ALU.add),
                    r=["psB", "bemat"], w=["acttrash2"], tag="u3")
                p.op("gpsimd", lambda jp=jp: nc.gpsimd.tensor_mul(
                    hm[:, 512 * jp:512 * jp + 512],
                    hm[:, 512 * jp:512 * jp + 512],
                    ctx.sb["acttrash"][:, 512:1024]),
                    r=["acttrash2", ("wbf", jp // 2)], w=[("wbf", jp // 2)],
                    tag="hmul")
            if stop_after == "experts" and k == 0 and e == 17 and \
                    "d_hm" in ctx.ins:
                p.op("gpsimd", lambda: nc.gpsimd.dma_start(
                    out=ctx.ins["d_hm"], in_=hm[:, 0:2048]),
                    r=[("wbf", 0), ("wbf", 1)], dma=True, tag="dhm")
            for m in range(4):
                yt = 4 * k + m
                for half in range(2):
                    bank = psE if half == 0 else psF
                    bkey = "psE" if half == 0 else "psF"
                    for jp in range(4):
                        p.op("tensor", lambda jp=jp, m=m, half=half, bank=bank:
                             nc.tensor.matmul(
                                 bank[0:128, 0:512],
                                 hm[:, 512 * jp + 128 * m:
                                    512 * jp + 128 * m + 128],
                                 wsl(ctx, 1, jp, half * 512, 512),
                                 start=(jp == 0), stop=(jp == 3)),
                             r=[("wbf", jp // 2), ("wslot", 1)], w=[bkey],
                             tag="down")
                    ysl = yac[:, 1024 * yt + 512 * half:
                              1024 * yt + 512 * half + 512]
                    cbap = ctx.sb["cmb"][:, 17 * yt + cbrow:17 * yt + cbrow + 1]
                    wsc = ctx.sb["work_b"][:, 0:512] if half == 0 else \
                        ctx.sb["work_b"][:, 1024:1536]
                    wk = "wb01" if half == 0 else "wb23"
                    p.op("vector", lambda wsc=wsc, bank=bank, cbap=cbap:
                         nc.vector.tensor_scalar(
                             out=wsc, in0=bank[0:128, 0:512], scalar1=cbap,
                             scalar2=None, op0=ALU.mult),
                         r=[bkey, ("cmb", yt)], w=[wk], tag="ycb")
                    p.op("vector", lambda ysl=ysl, wsc=wsc:
                         nc.vector.tensor_add(ysl, wsc, ysl),
                         r=[wk, ("y", yt)], w=[("y", yt)], tag="yadd")

    if stop_after == "experts":
        if "d_w2" in ctx.ins:
            p.op("gpsimd", lambda: nc.gpsimd.dma_start(
                out=ctx.ins["d_w2"], in_=ctx.sb["wbuf"][:, 8192:12288]),
                r=[("wslot", 1)], dma=True, tag="dw2")
        return
    p.marker(ctx, r=[("hTf", c, t) for c in range(FT) for t in range(NT)] +
             ["hm", "cbb"], w=["hT_dead"], tag="moedone")
    bes = ctx.sb["wbuf"][0:17, 0:1024]
    p.op("gpsimd", lambda: nc.gpsimd.dma_start(
        out=bes[0:16, :], in_=ctx.ins["L0_ffn_be2"]),
        r=["hT_dead"], w=["bes", ("wslot", 0)], dma=True, tag="ldbe2")
    p.op("gpsimd", lambda: nc.gpsimd.dma_start(
        out=bes[16:17, :], in_=ctx.ins["L0_ffn_bs2"][None, :]),
        w=["bes", ("wslot", 0)], dma=True, tag="ldbs2")
    wbx = ctx.sb["work_b"]
    for t in range(NT):
        p.op("sync", lambda t=t: nc.sync.dma_start(
            out=ctx.sb["xstage"][:, :],
            in_=ctx.ins["xspill"][128 * t:128 * t + 128, :]),
            r=["xspilled", "xstage"], w=["xstage"], dma=True, tag="ldxf")
        for half in range(2):
            bank = psA if half == 0 else psB
            bkey = "psA" if half == 0 else "psB"
            p.op("tensor", lambda t=t, half=half, bank=bank: nc.tensor.matmul(
                bank[0:128, 0:512], ctT[0:17, 128 * t:128 * t + 128],
                bes[0:17, 512 * half:512 * half + 512],
                start=True, stop=True),
                r=[("ctT", t), "ctTregion", "bes"], w=[bkey], tag="besmm")
            wreg = "wb01" if half == 0 else "wb23"
            woff = 0 if half == 0 else 1024
            p.op("vector", lambda t=t, half=half, bank=bank, woff=woff:
                 nc.vector.tensor_add(
                     wbx[:, woff:woff + 512], bank[0:128, 0:512],
                     yac[:, 1024 * t + 512 * half:1024 * t + 512 * half + 512]),
                 r=[bkey, ("y", t)], w=[wreg], tag="ybes")
            p.op("gpsimd", lambda half=half, woff=woff: nc.gpsimd.tensor_mul(
                wbx[:, woff + 512:woff + 1024], wbx[:, woff:woff + 512],
                ctx.sb["modt"][:, 2 * D + 512 * half:2 * D + 512 * half + 512]),
                r=[wreg, ("modt", 2)], w=[wreg], tag="ygb")
            p.op("vector", lambda t=t, half=half, woff=woff:
                 nc.vector.tensor_add(
                     x_res[:, 1024 * t + 512 * half:
                           1024 * t + 512 * half + 512],
                     ctx.sb["xstage"][:, 512 * half:512 * half + 512],
                     wbx[:, woff + 512:woff + 1024]),
                 r=[wreg, "xstage", "hT_dead"], w=[("x", t)], tag="xnew")



def _xov(ctx, c0, c1, f32=False):
    if f32:
        return ctx.sb["x_res"][:, c0:c1]
    return ctx.sb["x_res"].bitcast(BF16)[:, c0:c1]


def emit_l1_weights(ctx):
    nc, p = ctx.nc, ctx.prog
    p.op("sync", lambda: nc.sync.dma_start(
        out=ctx.ins["xspillC"].rearrange("(t p) f -> p t f", p=128),
        in_=ctx.sb["x_res"][:, :].rearrange("p (t f) -> p t f", f=D)),
        r=[("x", t) for t in range(NT)], w=["xspC"], dma=True, tag="spillC")
    psT = ctx.ps["psT"]
    names = ["L1_attn_Wq", "L1_attn_Wk", "L1_attn_Wv", "L1_attn_Wg",
             "L1_attn_Wo"]
    stg = _xov(ctx, 40960, 49152).rearrange("p (c f) -> p c f", f=1024)
    for wi, name in enumerate(names):
        p.op("gpsimd", lambda name=name: nc.gpsimd.dma_start(
            out=stg, in_=ctx.ins[name].rearrange("(c p) f -> p c f", p=128)),
            r=["xspC", "wstg"], w=["wstg"], dma=True, tag="ldL1w")
        dstv = _xov(ctx, 8192 * wi, 8192 * (wi + 1)).rearrange(
            "p (c f) -> p c f", f=1024)
        for ot in range(8):
            for it in range(8):
                sl = (2 * ot) % 4
                p.op("tensor", lambda ot=ot, it=it, sl=sl: nc.tensor.transpose(
                    psT[0:128, 128 * sl:128 * sl + 128],
                    stg[:, ot, 128 * it:128 * it + 128],
                    ctx.sb["ident_bf"][:, :]),
                    r=["wstg", "ident_bf", "xspC"], w=["psT"], tag="l1wtr")
                eng = "vector" if it % 2 == 0 else "scalar"
                fn = nc.vector.tensor_copy if it % 2 == 0 else nc.scalar.copy
                p.op(eng, lambda ot=ot, it=it, sl=sl, fn=fn, dstv=dstv: fn(
                    dstv[:, it, 128 * ot:128 * ot + 128],
                    psT[0:128, 128 * sl:128 * sl + 128]),
                    r=["psT"], w=[("l1w", min(ot, 7))], tag="l1wtre")
        p.marker(ctx, r=[("l1w", i) for i in range(8)], w=[("wT", wi)],
                 tag="l1wdone")


def emit_l1_consts(ctx):
    nc, p = ctx.nc, ctx.prog
    for cname, sname, key in (("c_triu_incl", "ctrilu", "trilu"),
                              ("c_e127", "ce127", "e127"),
                              ("c_maskT", "cmaskT", "maskT")):
        p.op("sync", lambda cname=cname, sname=sname: nc.sync.dma_start(
            out=ctx.sb[sname][:, :], in_=ctx.ins[cname]),
            w=[key], dma=True, tag="ldc1")
    for cname, sname, key in (("c_m32bd", "cm32bd", "m32bd"),
                              ("c_m32off", "cm32off", "m32off")):
        p.op("gpsimd", lambda cname=cname, sname=sname: nc.gpsimd.dma_start(
            out=ctx.sb[sname][:, :], in_=ctx.ins[cname]),
            w=[key], dma=True, tag="ldc2")


L1WS_KEYS = (["Dw", "NTh", "t1h", "TTm", "Pm", "Ptm", "Rm", "um", "zk", "ym",
              "Vp", "o32", "Sf", "gt16", "kqd", "bkq", "nbe", "xo"] +
             [("o", h) for h in range(NH)] +
             [("albe", t) for t in range(NT)] +
             [("fm", w, s) for w in range(2) for s in range(8)])


def emit_l1_attn(ctx, ntiles=NT):
    nc, p = ctx.nc, ctx.prog
    st = ctx.sb["stats"]
    psA, psB, psC, psD = (ctx.ps[k] for k in ("psA", "psB", "psC", "psD"))
    psE, psF, psT = ctx.ps["psE"], ctx.ps["psF"], ctx.ps["psT"]
    wbf = ctx.sb["work_bf"]
    hTw = ctx.sb["hTw"]
    F = lambda a, b: ctx.sb["x_res"][:, a:b]
    albeF = F(25600, 26624)
    Dw = F(26624, 28672)
    NTh = F(28672, 28800)
    t1h = F(28800, 28928)
    TTm = F(28928, 29056)
    Pm = F(29056, 29184)
    Ptm = F(29184, 29312)
    Rm = F(29312, 29440)
    um = F(29440, 29504)
    zk = F(29504, 29568)
    ym = F(29568, 29632)
    Vp = F(29632, 29696)
    o32 = F(29696, 30720)
    Sf = F(30720, 31232)
    gt16 = F(31232, 31360)
    kqd = F(31360, 31376)
    bkq = F(31376, 31392)
    nbe = F(31392, 31408)
    xo = F(31424, 32448)
    ywb = ctx.sb["work_b"].bitcast(BF16)[:, 0:64]
    wab = ctx.sb["bemat"].bitcast(BF16)[:, 0:256]
    qfmb = ctx.sb["biast"]
    Sb = ctx.sb["kvsb"]

    p.marker(ctx, r=["xspC", ("biast", 0), ("biast", 1), "gwT", "wab",
                     ("wT", 0), ("wT", 1), ("wT", 2), ("wT", 3), ("wT", 4)],
             w=list(L1WS_KEYS) + ["onesbf"], tag="cfence")

    p.op("vector", lambda: nc.vector.tensor_copy(
        ctx.sb["gwT"][0:1, 0:128], ctx.sb["ones_row"][0:1, :]),
        r=["ones_row", "onesbf"], w=["onesbf"], tag="onesbf")
    for which, wname in ((0, "L1_attn_Wa"), (1, "L1_attn_Wb")):
        p.op("sync", lambda wname=wname: nc.sync.dma_start(
            out=ctx.sb["work_b"][0:16, 0:1024], in_=ctx.ins[wname]),
            r=["wab"], w=["wb01", "wb23"], dma=True, tag="ldWab")
        for ci in range(8):
            p.op("tensor", lambda ci=ci: nc.tensor.transpose(
                psF[0:128, 0:16],
                ctx.sb["work_b"][0:16, 128 * ci:128 * ci + 128],
                ctx.sb["ident_f32"][0:16, 0:16]),
                r=["wb01", "wb23", "ident_f32"], w=["psF"], tag="wabtr")
            p.op("vector", lambda ci=ci, which=which: nc.vector.tensor_copy(
                wab[:, 128 * which + 16 * ci:128 * which + 16 * ci + 16],
                psF[0:128, 0:16]),
                r=["psF"], w=["wab"], tag="wabtre")
    p.op("sync", lambda: nc.sync.dma_start(
        out=ctx.sb["mrows"][0:1, 0:16], in_=ctx.ins["L1_attn_ba"][None, :]),
        w=["mstage"], dma=True, tag="ldba")
    bcast_row(ctx, ctx.sb["mrows"][0:1, 0:16], ctx.sb["gb16"][:, 0:16],
              ["mstage"], "bab", bank="psD", tag="bab")
    p.op("sync", lambda: nc.sync.dma_start(
        out=ctx.sb["mrows"][0:1, 0:16], in_=ctx.ins["L1_attn_bb"][None, :]),
        r=["bab"], w=["mstage"], dma=True, tag="ldbb")
    bcast_row(ctx, ctx.sb["mrows"][0:1, 0:16], ctx.sb["cmb"][:, 0:16],
              ["mstage"], "bbb", bank="psD", tag="bbb")
    p.op("vector", lambda: nc.vector.memset(Sf, 0.0), r=["Sf"], w=["Sf"],
         tag="S0")
    p.op("vector", lambda: nc.vector.memset(Sb[:, :], 0.0),
         r=["kvsb"], w=["Sb"], tag="S0b")

    WQ, WK, WV, WG, WO = 0, 8192, 16384, 24576, 32768

    def wsl1(base, ci, f0, fn):
        o = base + 1024 * ci + f0
        return ctx.sb["x_res"].bitcast(BF16)[:, o:o + fn]

    for t in range(ntiles):
        p.op("sync", lambda t=t: nc.sync.dma_start(
            out=ctx.sb["xstage"][:, :],
            in_=ctx.ins["xspillC"][128 * t:128 * t + 128, :]),
            r=["xspC", "xstage"], w=["xstage"], dma=True, tag="ldxc")
        emit_h_tile(ctx, t, MOD_SLOT["ca1p"], MOD_SLOT["sa"], hTw, "hTw",
                    xt_ap=ctx.sb["xstage"][:, :], xt_key="xstage",
                    tag=f"hC{t}_")
        for name, base, dstq, extra in (("q", WQ, 0, "l2"), ("k", WK, 1, "l2"),
                                        ("v", WV, 2, "v"), ("g", WG, 3, "og")):
            for half in range(2):
                bank, bkey = (psA, "psA") if half == 0 else (psB, "psB")
                for ci in range(FT):
                    p.op("tensor", lambda half=half, ci=ci, bank=bank,
                         base=base: nc.tensor.matmul(
                             bank[0:128, 0:512],
                             hTw[:, 128 * ci:128 * ci + 128],
                             wsl1(base, ci, half * 512, 512),
                             start=(ci == 0), stop=(ci == FT - 1)),
                         r=[("hTw", ci), ("wT", base // 8192)], w=[bkey],
                         tag="prj" + name)
                dsl = wbf[:, 1024 * dstq + 512 * half:
                          1024 * dstq + 512 * half + 512]
                if extra == "l2":
                    p.op("vector", lambda bank=bank, half=half:
                         nc.vector.tensor_copy(
                             Dw[:, 512 * half:512 * half + 512],
                             bank[0:128, 0:512]),
                         r=[bkey, "Dw"], w=["Dw"], tag="l2ev")
                    p.op("scalar", lambda half=half:
                         nc.scalar.activation(
                             out=Dw[:, 1024 + 512 * half:1536 + 512 * half],
                             in_=Dw[:, 512 * half:512 * half + 512],
                             func=AF.Square),
                         r=["Dw"], w=["Dw"], tag="l2sq")
                    p.op("vector", lambda half=half: nc.vector.tensor_reduce(
                        st[:, 16 + 8 * half:24 + 8 * half],
                        Dw[:, 1024 + 512 * half:1536 + 512 * half].rearrange(
                            "p (h d) -> p h d", d=HD), AX.X, ALU.add),
                        r=["Dw"], w=[("st", 16)], tag="l2ss")
                    p.op("scalar", lambda half=half: nc.scalar.activation(
                        out=st[:, 32 + 8 * half:40 + 8 * half],
                        in_=st[:, 16 + 8 * half:24 + 8 * half], func=AF.Ln),
                        r=[("st", 16)], w=[("st", 32)], tag="l2ln")
                    p.op("scalar", lambda half=half: nc.scalar.activation(
                        out=st[:, 16 + 8 * half:24 + 8 * half],
                        in_=st[:, 32 + 8 * half:40 + 8 * half], func=AF.Exp,
                        scale=0.5),
                        r=[("st", 32)], w=[("st", 16)], tag="l2ex")
                    p.op("vector", lambda half=half: nc.vector.tensor_scalar(
                        out=st[:, 16 + 8 * half:24 + 8 * half],
                        in0=st[:, 16 + 8 * half:24 + 8 * half],
                        scalar1=1e-6, scalar2=None, op0=ALU.max),
                        r=[("st", 16)], w=[("st", 16)], tag="l2mx")
                    p.op("vector", lambda half=half: nc.vector.reciprocal(
                        st[:, 32 + 8 * half:40 + 8 * half],
                        st[:, 16 + 8 * half:24 + 8 * half]),
                        r=[("st", 16)], w=[("st", 32)], tag="l2rc")
                    p.op("gpsimd", lambda half=half, dsl=dsl,
                         dstq=dstq: nc.gpsimd.tensor_tensor(
                             out=dsl.rearrange("p (h d) -> p h d", d=HD),
                             in0=Dw[:, 512 * half:512 * half + 512].rearrange(
                                 "p (h d) -> p h d", d=HD),
                             in1=st[:, 32 + 8 * half:40 + 8 * half].rearrange(
                                 "p (h o) -> p h o", o=1).broadcast_to(
                                 [128, 8, HD]),
                             op=ALU.mult),
                         r=["Dw", ("st", 32)], w=[("wbf", dstq)], tag="l2n")
                elif extra == "v":
                    p.op("vector", lambda bank=bank, dsl=dsl, bkey=bkey:
                         nc.vector.tensor_copy(dsl, bank[0:128, 0:512]),
                         r=[bkey], w=[("wbf", dstq)], tag="vev")
                else:
                    p.op("scalar", lambda bank=bank, dsl=dsl, bkey=bkey:
                         nc.scalar.activation(out=dsl, in_=bank[0:128, 0:512],
                                              func=AF.Silu),
                         r=[bkey], w=[("wbf", dstq)], tag="ogev")
        qtm = wbf[:, 0:1024]
        ktm = wbf[:, 1024:2048]
        vtm = wbf[:, 2048:3072]
        og = wbf[:, 3072:4096]
        p.op("vector", lambda: nc.vector.tensor_mul(
            Dw[:, 0:1024], qtm, ktm), r=[("wbf", 0), ("wbf", 1), "Dw"],
            w=["Dw"], tag="kq")
        p.op("vector", lambda: nc.vector.tensor_reduce(
            kqd, Dw[:, 0:1024].rearrange("p (h d) -> p h d", d=HD),
            AX.X, ALU.add), r=["Dw", "kqd"], w=["kqd"], tag="kqd")
        for which in range(2):
            for ci in range(FT):
                p.op("tensor", lambda which=which, ci=ci: nc.tensor.matmul(
                    psE[0:128, 16 * which:16 * which + 16],
                    hTw[:, 128 * ci:128 * ci + 128],
                    wab[:, 128 * which + 16 * ci:128 * which + 16 * ci + 16],
                    start=(ci == 0 and which == 0), stop=(ci == FT - 1)),
                    r=[("hTw", ci), "wab"], w=["psE"], tag="abmm")
        p.op("vector", lambda: nc.vector.tensor_add(
            Dw[:, 0:16], psE[0:128, 0:16], ctx.sb["gb16"][:, 0:16]),
            r=["psE", "bab", "Dw"], w=["Dw"], tag="aadd")
        p.op("vector", lambda: nc.vector.tensor_add(
            Dw[:, 16:32], psE[0:128, 16:32], ctx.sb["cmb"][:, 0:16]),
            r=["psE", "bbb", "Dw"], w=["Dw"], tag="badd")
        p.op("scalar", lambda t=t: nc.scalar.activation(
            out=albeF[:, 32 * t:32 * t + 32], in_=Dw[:, 0:32],
            func=AF.Sigmoid),
            r=["Dw"], w=[("albe", t)], tag="absig")
        al_t = albeF[:, 32 * t:32 * t + 16]
        be_t = albeF[:, 32 * t + 16:32 * t + 32]
        p.op("scalar", lambda: nc.scalar.activation(
            out=st[:, 16:32], in_=Dw[:, 0:16], func=AF.Exp, scale=-1.0),
            r=["Dw"], w=[("st", 16)], tag="lga1")
        p.op("vector", lambda: nc.vector.tensor_scalar(
            out=st[:, 16:32], in0=st[:, 16:32], scalar1=1.0, scalar2=None,
            op0=ALU.add), r=[("st", 16)], w=[("st", 16)], tag="lga2")
        p.op("scalar", lambda: nc.scalar.activation(
            out=st[:, 16:32], in_=st[:, 16:32], func=AF.Ln),
            r=[("st", 16)], w=[("st", 16)], tag="lga3")
        p.op("tensor", lambda: nc.tensor.matmul(
            psF[0:128, 0:16], ctx.sb["ctrilu"][:, :], st[:, 16:32],
            start=True, stop=True),
            r=[("st", 16), "trilu"], w=["psF"], tag="gmm")
        p.op("vector", lambda: nc.vector.tensor_copy(
            st[:, 80:96], psF[0:128, 0:16]),
            r=["psF"], w=[("st", 80)], tag="gev")
        p.op("scalar", lambda: nc.scalar.activation(
            out=st[:, 48:64], in_=st[:, 80:96], func=AF.Exp),
            r=[("st", 80)], w=[("st", 48)], tag="Gex")
        p.op("tensor", lambda: nc.tensor.matmul(
            psF[0:128, 16:32], ctx.sb["ce127"][:, :], st[:, 48:64],
            start=True, stop=True),
            r=[("st", 48), "e127"], w=["psF"], tag="gcmm")
        p.op("vector", lambda: nc.vector.tensor_copy(
            st[:, 96:112], psF[0:128, 16:32]),
            r=["psF"], w=[("st", 96)], tag="gcev")
        p.op("vector", lambda: nc.vector.reciprocal(
            st[:, 64:80], st[:, 48:64]),
            r=[("st", 48)], w=[("st", 64)], tag="recG")
        p.op("vector", lambda be_t=be_t, t=t: nc.vector.tensor_mul(
            st[:, 112:128], be_t, st[:, 96:112]),
            r=[("albe", t), ("st", 96)], w=[("st", 112)], tag="wv1")
        p.op("vector", lambda: nc.vector.tensor_mul(
            st[:, 112:128], st[:, 112:128], st[:, 64:80]),
            r=[("st", 112), ("st", 64)], w=[("st", 112)], tag="wv2")
        p.op("vector", lambda be_t=be_t, t=t: nc.vector.tensor_mul(
            bkq, be_t, kqd), r=[("albe", t), "kqd", "bkq"], w=["bkq"],
            tag="bkq")
        p.op("vector", lambda be_t=be_t, t=t: nc.vector.tensor_scalar(
            out=nbe, in0=be_t, scalar1=-1.0, scalar2=None, op0=ALU.mult),
            r=[("albe", t), "nbe"], w=["nbe"], tag="nbe")
        p.op("tensor", lambda: nc.tensor.transpose(
            psF[0:16, 32:160], st[:, 80:96], ctx.sb["ident_f32"][:, :]),
            r=[("st", 80), "ident_f32"], w=["psF"], tag="gtr")
        p.op("vector", lambda: nc.vector.tensor_copy(
            gt16[0:16, :], psF[0:16, 32:160]),
            r=["psF", "gt16"], w=["gt16"], tag="gtev")
        for qd in range(4):
            if qd % 2 == 0:
                gh = qd // 2
                p.op("sync", lambda gh=gh: nc.sync.dma_start(
                    out=ctx.sb["mrows"][0:1, 0:1024].rearrange(
                        "o (h i) -> o h i", i=128),
                    in_=gt16[8 * gh:8 * gh + 8, :]),
                    r=["gt16", "mstage"], w=["mstage"], dma=True,
                    tag="growdma")
            p.op("tensor", lambda qd=qd: nc.tensor.matmul(
                psC[0:128, 0:512], ctx.sb["ones_row"][0:1, :],
                ctx.sb["mrows"][0:1, 512 * (qd % 2):512 * (qd % 2) + 512],
                start=True, stop=True),
                r=["mstage", "ones_row"], w=["psC"], tag="xbc")
            dwq = Dw[:, 512 * qd:512 * qd + 512].rearrange(
                "p (h i) -> p h i", i=128)
            p.op("vector", lambda qd=qd, dwq=dwq: nc.vector.tensor_tensor(
                out=dwq,
                in0=psC[0:128, 0:512].rearrange("p (h i) -> p h i", i=128),
                in1=st[:, 80 + 4 * qd:84 + 4 * qd].rearrange(
                    "p (h o) -> p h o", o=1).broadcast_to([128, 4, 128]),
                op=ALU.subtract),
                r=["psC", ("st", 80), "Dw"], w=["Dw"], tag="m1a")
            p.op("gpsimd", lambda qd=qd, dwq=dwq: nc.gpsimd.tensor_tensor(
                out=dwq, in0=dwq,
                in1=ctx.sb["cmaskT"][:, :].rearrange(
                    "p (o i) -> p o i", o=1).broadcast_to([128, 4, 128]),
                op=ALU.add),
                r=["Dw", "maskT"], w=["Dw"], tag="m1c")
            p.op("scalar", lambda qd=qd: nc.scalar.activation(
                out=Dw[:, 512 * qd:512 * qd + 512],
                in_=Dw[:, 512 * qd:512 * qd + 512], func=AF.Exp),
                r=["Dw"], w=["Dw"], tag="m1e")
        p.op("gpsimd", lambda: nc.gpsimd.tensor_tensor(
            out=Dw[:, 0:2048].rearrange("p (h i) -> p h i", i=128),
            in0=Dw[:, 0:2048].rearrange("p (h i) -> p h i", i=128),
            in1=nbe.rearrange("p (h o) -> p h o", o=1).broadcast_to(
                [128, 16, 128]),
            op=ALU.mult),
            r=["Dw", "nbe"], w=["Dw"], tag="m1f")

        for h in range(NH):
            pb = 64 * (h % 2)
            sub = h // 2
            if h % 2 == 0:
                for which, srcq in ((0, 0), (1, 1)):
                    p.op("tensor", lambda sub=sub, which=which, srcq=srcq:
                         nc.tensor.transpose(
                             psT[0:128, 256 * which:256 * which + 128],
                             wbf[:, 1024 * srcq + 128 * sub:
                                 1024 * srcq + 128 * sub + 128],
                             ctx.sb["ident_bf"][:, :]),
                         r=[("wbf", srcq), "ident_bf"], w=["psT"],
                         tag="qktr")
                    p.op("scalar", lambda sub=sub, which=which:
                         nc.scalar.copy(
                             qfmb[:, 1024 * which + 128 * sub:
                                  1024 * which + 128 * sub + 128],
                             psT[0:128, 256 * which:256 * which + 128]),
                         r=["psT"], w=[("fm", which, sub)], tag="qktre")
            qfm = qfmb[pb:pb + 64, 128 * sub:128 * sub + 128]
            kfm = qfmb[pb:pb + 64, 1024 + 128 * sub:1024 + 128 * sub + 128]
            Dh = Dw[:, 128 * h:128 * h + 128]
            Ssl_b = Sb[pb:pb + 64, 64 * sub:64 * sub + 64]
            p.op("tensor", lambda kfm=kfm, Ssl_b=Ssl_b: nc.tensor.matmul(
                psA[0:128, 0:64], kfm, Ssl_b, start=True, stop=True),
                r=[("fm", 1, sub), "Sb"], w=["psA"], tag="ks0")
            p.op("vector", lambda h=h: nc.vector.tensor_scalar(
                out=um, in0=psA[0:128, 0:64], scalar1=st[:, 48 + h:49 + h],
                scalar2=None, op0=ALU.mult),
                r=["psA", ("st", 48), "um"], w=["um"], tag="vp1")
            p.op("vector", lambda h=h: nc.vector.tensor_sub(
                Vp, vtm[:, 64 * h:64 * h + 64], um),
                r=[("wbf", 2), "um", "Vp"], w=["Vp"], tag="vp2")
            p.op("tensor", lambda kfm=kfm: nc.tensor.matmul(
                psC[0:128, 0:128], kfm, kfm, start=True, stop=True),
                r=[("fm", 1, sub)], w=["psC"], tag="kk")
            p.op("tensor", lambda kfm=kfm, qfm=qfm: nc.tensor.matmul(
                psD[0:128, 0:128], kfm, qfm, start=True, stop=True),
                r=[("fm", 0, sub), ("fm", 1, sub)], w=["psD"], tag="kq2")
            p.op("vector", lambda Dh=Dh: nc.vector.tensor_mul(
                NTh, psC[0:128, 0:128], Dh),
                r=["psC", "Dw", "NTh"], w=["NTh"], tag="nth")
            p.op("vector", lambda Dh=Dh: nc.vector.tensor_mul(
                t1h, psD[0:128, 0:128], Dh),
                r=["psD", "Dw", "t1h"], w=["t1h"], tag="t1h")
            p.op("gpsimd", lambda: nc.gpsimd.tensor_mul(
                Pm, NTh, ctx.sb["cm32bd"][:, :]),
                r=["NTh", "m32bd", "Pm"], w=["Pm"], tag="p0")
            p.op("gpsimd", lambda: nc.gpsimd.tensor_mul(
                Rm, NTh, ctx.sb["cm32off"][:, :]),
                r=["NTh", "m32off", "Rm"], w=["Rm"], tag="rmask")
            p.op("vector", lambda: nc.vector.tensor_add(
                TTm, Pm, ctx.sb["ident_f32"][:, :]),
                r=["Pm", "ident_f32", "TTm"], w=["TTm"], tag="tt0")
            for lev in range(4):
                p.op("vector", lambda: nc.vector.transpose(Ptm, Pm),
                     r=["Pm", "Ptm"], w=["Ptm"], tag="ptr")
                p.op("tensor", lambda: nc.tensor.matmul(
                    psE[0:128, 128:256], Ptm, Pm, start=True, stop=True),
                    r=["Ptm", "Pm"], w=["psE"], tag="psq")
                p.op("vector", lambda: nc.vector.tensor_copy(
                    Pm, psE[0:128, 128:256]),
                    r=["psE", "Pm"], w=["Pm"], tag="pev")
                p.op("vector", lambda: nc.vector.transpose(Ptm, Pm),
                     r=["Pm", "Ptm"], w=["Ptm"], tag="ptr2")
                p.op("tensor", lambda: nc.tensor.matmul(
                    psE[0:128, 128:256], Ptm, TTm, start=True, stop=True),
                    r=["Ptm", "TTm"], w=["psE"], tag="ttmm")
                p.op("vector", lambda: nc.vector.tensor_add(
                    TTm, TTm, psE[0:128, 128:256]),
                    r=["TTm", "psE"], w=["TTm"], tag="ttadd")
            p.op("tensor", lambda: nc.tensor.matmul(
                psB[0:128, 0:64], TTm, Vp, start=True, stop=True),
                r=["TTm", "Vp"], w=["psB"], tag="umm")
            p.op("vector", lambda: nc.vector.tensor_copy(
                um, psB[0:128, 0:64]), r=["psB", "um"], w=["um"], tag="uev")
            p.op("vector", lambda: nc.vector.tensor_copy(ym, um),
                 r=["um", "ym"], w=["ym"], tag="y0")
            for it in range(3):
                p.op("tensor", lambda: nc.tensor.matmul(
                    psB[0:128, 64:128], Rm, um, start=True, stop=True),
                    r=["Rm", "um"], w=["psB"], tag="rmm")
                p.op("vector", lambda: nc.vector.tensor_copy(
                    zk, psB[0:128, 64:128]), r=["psB", "zk"], w=["zk"],
                    tag="zev")
                p.op("tensor", lambda: nc.tensor.matmul(
                    psB[0:128, 0:64], TTm, zk, start=True, stop=True),
                    r=["TTm", "zk"], w=["psB"], tag="tzmm")
                p.op("vector", lambda: nc.vector.tensor_copy(
                    um, psB[0:128, 0:64]), r=["psB", "um"], w=["um"],
                    tag="zev2")
                p.op("vector", lambda: nc.vector.tensor_add(ym, ym, um),
                     r=["ym", "um"], w=["ym"], tag="yacc")
            osl = o32[:, 64 * h:64 * h + 64]
            p.op("tensor", lambda qfm=qfm, Ssl_b=Ssl_b: nc.tensor.matmul(
                psA[0:128, 64:128], qfm, Ssl_b, start=True, stop=True),
                r=[("fm", 0, sub), "Sb"], w=["psA"], tag="qs0")
            p.op("vector", lambda osl=osl, h=h: nc.vector.tensor_scalar(
                out=osl, in0=psA[0:128, 64:128], scalar1=st[:, 48 + h:49 + h],
                scalar2=None, op0=ALU.mult),
                r=["psA", ("st", 48), ("o", h)], w=[("o", h)], tag="ocr")
            p.op("tensor", lambda: nc.tensor.matmul(
                psB[0:128, 64:128], t1h, ym, start=True, stop=True),
                r=["t1h", "ym"], w=["psB"], tag="atty")
            p.op("vector", lambda osl=osl: nc.vector.tensor_sub(
                osl, osl, psB[0:128, 64:128]),
                r=[("o", h), "psB"], w=[("o", h)], tag="oatt")
            p.op("vector", lambda osl=osl, h=h: nc.vector.scalar_tensor_tensor(
                out=osl, in0=ym, scalar=bkq[:, h:h + 1], in1=osl,
                op0=ALU.mult, op1=ALU.add),
                r=["ym", "bkq", ("o", h)], w=[("o", h)], tag="odiag")
            p.op("vector", lambda h=h: nc.vector.tensor_scalar(
                out=zk, in0=ym, scalar1=st[:, 112 + h:113 + h], scalar2=None,
                op0=ALU.mult),
                r=["ym", ("st", 112), "zk"], w=["zk"], tag="yw")
            p.op("vector", lambda: nc.vector.tensor_copy(ywb, zk),
                 r=["zk", "wb01"], w=["wb01"], tag="ywb")
            p.op("tensor", lambda h=h, pb=pb: nc.tensor.matmul(
                psE[pb:pb + 64, 0:64], wbf[:, 1024 + 64 * h:1024 + 64 * h + 64],
                ywb, start=True, stop=True),
                r=[("wbf", 1), "wb01"], w=["psE"], tag="smm")
            Ssl = Sf[pb:pb + 64, 64 * sub:64 * sub + 64]
            p.op("vector", lambda Ssl=Ssl, h=h, pb=pb: nc.vector.tensor_scalar(
                out=Ssl, in0=Ssl, scalar1=st[pb:pb + 64, 96 + h:97 + h],
                scalar2=None, op0=ALU.mult),
                r=["Sf", ("st", 96)], w=["Sf"], tag="sdec")
            p.op("vector", lambda Ssl=Ssl, pb=pb: nc.vector.tensor_add(
                Ssl, Ssl, psE[pb:pb + 64, 0:64]),
                r=["Sf", "psE"], w=["Sf"], tag="supd")
            p.op("vector", lambda Ssl=Ssl, Ssl_b=Ssl_b: nc.vector.tensor_copy(
                Ssl_b, Ssl), r=["Sf", "Sb"], w=["Sb"], tag="sbf")
        p.op("vector", lambda: nc.vector.tensor_mul(
            wbf[:, 0:1024], o32[:, 0:1024], og),
            r=[("o", h2) for h2 in range(NH)] + [("wbf", 3), ("wbf", 0)],
            w=[("wbf", 0)], tag="oog")
        for c in range(FT):
            p.op("tensor", lambda c=c: nc.tensor.transpose(
                psT[0:128, 128 * (c % 4):128 * (c % 4) + 128],
                wbf[:, 128 * c:128 * c + 128], ctx.sb["ident_bf"][:, :]),
                r=[("wbf", 0), "ident_bf"], w=["psT"], tag="otr")
            eng = "vector" if c % 2 == 0 else "scalar"
            fn = nc.vector.tensor_copy if c % 2 == 0 else nc.scalar.copy
            p.op(eng, lambda c=c, fn=fn: fn(
                wbf[:, 1024 + 128 * c:1024 + 128 * c + 128],
                psT[0:128, 128 * (c % 4):128 * (c % 4) + 128]),
                r=["psT"], w=[("wbf", 1)], tag="otre")
        for half in range(2):
            bank, bkey = (psA, "psA") if half == 0 else (psB, "psB")
            for ci in range(FT):
                p.op("tensor", lambda half=half, ci=ci, bank=bank:
                     nc.tensor.matmul(
                         bank[0:128, 0:512],
                         wbf[:, 1024 + 128 * ci:1024 + 128 * ci + 128],
                         wsl1(WO, ci, half * 512, 512),
                         start=(ci == 0), stop=(ci == FT - 1)),
                     r=[("wbf", 1), ("wT", 4)], w=[bkey], tag="womm")
            p.op("vector", lambda half=half, bank=bank, bkey=bkey:
                 nc.vector.tensor_mul(
                     xo[:, 512 * half:512 * half + 512], bank[0:128, 0:512],
                     ctx.sb["modt"][:, 2 * D + 512 * half:
                                    2 * D + 512 * half + 512]),
                 r=[bkey, ("modt", 2), "xo"], w=["xo"], tag="xga")
            p.op("vector", lambda half=half: nc.vector.tensor_add(
                xo[:, 512 * half:512 * half + 512],
                xo[:, 512 * half:512 * half + 512],
                ctx.sb["xstage"][:, 512 * half:512 * half + 512]),
                r=["xo", "xstage"], w=["xo"], tag="xadd")
        p.op("sync", lambda t=t: nc.sync.dma_start(
            out=ctx.ins["x3"][128 * t:128 * t + 128, :], in_=xo),
            r=["xo"], dma=True, tag="stx3")
        if t == 0 and "d_st" in ctx.ins:
            p.op("sync", lambda: nc.sync.dma_start(
                out=ctx.ins["d_st"], in_=st[:, :]),
                r=[("st", 16), ("st", 32), ("st", 48), ("st", 64), ("st", 80),
                   ("st", 96), ("st", 112)], dma=True, tag="dbg")
            p.op("sync", lambda: nc.sync.dma_start(
                out=ctx.ins["d_Dw"], in_=Dw), r=["Dw"], dma=True, tag="dbg")
            p.op("gpsimd", lambda: nc.gpsimd.dma_start(
                out=ctx.ins["d_qk"], in_=wbf[:, 0:4096]),
                r=[("wbf", 0), ("wbf", 1), ("wbf", 2), ("wbf", 3)],
                dma=True, tag="dbg")
            p.op("sync", lambda: nc.sync.dma_start(
                out=ctx.ins["d_o"], in_=o32), r=[("o", hh) for hh in range(NH)],
                dma=True, tag="dbg")
            p.op("sync", lambda: nc.sync.dma_start(
                out=ctx.ins["d_S"], in_=Sf), r=["Sf"], dma=True, tag="dbg")
            p.op("gpsimd", lambda: nc.gpsimd.dma_start(
                out=ctx.ins["d_grow"],
                in_=ctx.sb["mrows"].bitcast(BF16)[0:1, 0:2048]),
                r=["mstage"], dma=True, tag="dbg")



def emit_quant(ctx, src_ap, src_key, dst_ap, dst_key, work0, wkey, tag="qz"):
    nc, p = ctx.nc, ctx.prog
    v3 = lambda ap: ap.rearrange("p (b k) -> p b k", k=32)
    mx = work0[:, 1024:1056]
    rs = work0[:, 1056:1088]
    u = work0[:, 0:1024]
    p.op("vector", lambda: nc.vector.tensor_reduce(
        mx, v3(src_ap), AX.X, ALU.max, apply_absolute_value=True),
        r=[src_key, wkey], w=[wkey], tag=tag + "mx")
    p.op("vector", lambda: nc.vector.tensor_scalar(
        out=mx, in0=mx, scalar1=1.0 / 7.5, scalar2=1e-30,
        op0=ALU.mult, op1=ALU.max),
        r=[wkey], w=[wkey], tag=tag + "sc")
    p.op("vector", lambda: nc.vector.reciprocal(rs, mx),
         r=[wkey], w=[wkey], tag=tag + "rc")
    p.op("gpsimd", lambda: nc.gpsimd.tensor_tensor(
        out=v3(u), in0=v3(src_ap),
        in1=rs.rearrange("p (b o) -> p b o", o=1).broadcast_to([128, 32, 32]),
        op=ALU.mult),
        r=[src_key, wkey], w=[wkey], tag=tag + "u")
    ex = work0[:, 1088:2112]
    p.op("vector", lambda: nc.vector.tensor_single_scalar(
        out=ex.bitcast(U32), in_=u.bitcast(U32), scalar=0x7F800000,
        op=ALU.bitwise_and),
        r=[wkey], w=[wkey], tag=tag + "ex")
    p.op("vector", lambda: nc.vector.tensor_scalar(
        out=ex, in0=ex, scalar1=1.0, scalar2=float(3 * 2 ** 19),
        op0=ALU.max, op1=ALU.mult),
        r=[wkey], w=[wkey], tag=tag + "S")
    p.op("vector", lambda: nc.vector.tensor_add(u, u, ex),
         r=[wkey], w=[wkey], tag=tag + "t")
    p.op("vector", lambda: nc.vector.tensor_sub(u, u, ex),
         r=[wkey], w=[wkey], tag=tag + "q")
    p.op("gpsimd", lambda: nc.gpsimd.tensor_tensor(
        out=v3(dst_ap), in0=v3(u),
        in1=mx.rearrange("p (b o) -> p b o", o=1).broadcast_to([128, 32, 32]),
        op=ALU.mult),
        r=[wkey], w=[dst_key], tag=tag + "d")


def emit_l1_ffn(ctx):
    nc, p = ctx.nc, ctx.prog
    psA, psB, psT = ctx.ps["psA"], ctx.ps["psB"], ctx.ps["psT"]
    F = lambda a, b: ctx.sb["x_res"][:, a:b]
    hF = F(0, 1024)
    qw = F(1024, 3136)
    wstage = F(4096, 5120)
    wq = ctx.sb["x_res"].bitcast(BF16)[:, 16384:24576]
    wqT = ctx.sb["x_res"].bitcast(BF16)[:, 24576:32768]
    xd = ctx.sb["work_bf"][:, 0:1024]
    xdT = ctx.sb["work_bf"][:, 1024:2048]
    xo = F(31424, 32448)
    wbx = ctx.sb["work_b"]
    p.marker(ctx, r=[],
             w=list(L1WS_KEYS) + ["hF", "qw", "wstage", "wq", "wqT",
                ("wT", 0), ("wT", 1), ("wT", 2), ("wT", 3),
                ("biast", 0), ("biast", 1)], tag="dfence")
    emit_bias_tile(ctx, "L1_ffn_bias", 0)
    for ci in range(8):
        p.op("sync", lambda ci=ci: nc.sync.dma_start(
            out=wstage, in_=ctx.ins["L1_ffn_weight"][128 * ci:128 * ci + 128, :]),
            r=["wstage"], w=["wstage"], dma=True, tag="ldWd")
        emit_quant(ctx, wstage, "wstage", wq[:, 1024 * ci:1024 * ci + 1024],
                   "wq", qw, "qw", tag=f"qw{ci}")
    for ot in range(8):
        for it in range(8):
            sl = (2 * it) % 4
            p.op("tensor", lambda ot=ot, it=it, sl=sl: nc.tensor.transpose(
                psT[0:128, 128 * sl:128 * sl + 128],
                wq[:, 1024 * ot + 128 * it:1024 * ot + 128 * it + 128],
                ctx.sb["ident_bf"][:, :]),
                r=["wq", "ident_bf"], w=["psT"], tag="wdtr")
            eng = "vector" if it % 2 == 0 else "scalar"
            fn = nc.vector.tensor_copy if it % 2 == 0 else nc.scalar.copy
            p.op(eng, lambda ot=ot, it=it, sl=sl, fn=fn: fn(
                wqT[:, 1024 * it + 128 * ot:1024 * it + 128 * ot + 128],
                psT[0:128, 128 * sl:128 * sl + 128]),
                r=["psT"], w=["wqT"], tag="wdtre")
    for t in range(NT):
        p.op("sync", lambda t=t: nc.sync.dma_start(
            out=ctx.sb["xstage"][:, :],
            in_=ctx.ins["x3"][128 * t:128 * t + 128, :]),
            r=["xstage"], w=["xstage"], dma=True, tag="ldxd")
        emit_h_tile(ctx, t, MOD_SLOT["cb1p"], MOD_SLOT["sb"], hF, "hF",
                    transpose="f32", xt_ap=ctx.sb["xstage"][:, :],
                    xt_key="xstage", tag=f"hD{t}_")
        emit_quant(ctx, hF, "hF", xd, ("wbf", 0), qw, "qw", tag="qx")
        for c in range(FT):
            p.op("tensor", lambda c=c: nc.tensor.transpose(
                psT[0:128, 128 * (c % 4):128 * (c % 4) + 128],
                xd[:, 128 * c:128 * c + 128], ctx.sb["ident_bf"][:, :]),
                r=[("wbf", 0), "ident_bf"], w=["psT"], tag="xdtr")
            eng = "vector" if c % 2 == 0 else "scalar"
            fn = nc.vector.tensor_copy if c % 2 == 0 else nc.scalar.copy
            p.op(eng, lambda c=c, fn=fn: fn(
                xdT[:, 128 * c:128 * c + 128],
                psT[0:128, 128 * (c % 4):128 * (c % 4) + 128]),
                r=["psT"], w=[("wbf", 1)], tag="xdtre")
        for half in range(2):
            bank, bkey = (psA, "psA") if half == 0 else (psB, "psB")
            for ci in range(FT):
                p.op("tensor", lambda half=half, ci=ci, bank=bank:
                     nc.tensor.matmul(
                         bank[0:128, 0:512],
                         xdT[:, 128 * ci:128 * ci + 128],
                         wqT[:, 1024 * ci + 512 * half:
                             1024 * ci + 512 * half + 512],
                         start=(ci == 0), stop=(ci == FT - 1)),
                     r=[("wbf", 1), "wqT"], w=[bkey], tag="ffmm")
            woff = 0 if half == 0 else 1024
            wreg = "wb01" if half == 0 else "wb23"
            p.op("vector", lambda half=half, bank=bank, woff=woff:
                 nc.vector.tensor_add(
                     wbx[:, woff:woff + 512], bank[0:128, 0:512],
                     ctx.sb["biast"][:, 512 * half:512 * half + 512]),
                 r=[bkey, ("biast", 0)], w=[wreg], tag="ffb")
            p.op("gpsimd", lambda half=half, woff=woff: nc.gpsimd.tensor_mul(
                wbx[:, woff + 512:woff + 1024], wbx[:, woff:woff + 512],
                ctx.sb["modt"][:, 2 * D + 512 * half:2 * D + 512 * half + 512]),
                r=[wreg, ("modt", 2)], w=[wreg], tag="ffg")
            p.op("vector", lambda half=half, woff=woff: nc.vector.tensor_add(
                xo[:, 512 * half:512 * half + 512],
                ctx.sb["xstage"][:, 512 * half:512 * half + 512],
                wbx[:, woff + 512:woff + 1024]),
                r=[wreg, "xstage", "xo"], w=["xo"], tag="ffx")
        p.op("sync", lambda t=t: nc.sync.dma_start(
            out=ctx.ins["out"][128 * t:128 * t + 128, :], in_=xo),
            r=["xo"], dma=True, tag="stout")



FUSED = False
NCORES = 8


def emit_global_fence(ctx, tag="gfence"):
    p = ctx.prog
    keys = set(p.last_w) | set(p.readers)
    p.marker(ctx, r=[], w=list(keys), tag=tag)


def _wspec():
    s = {}
    for w in ("Wq", "Wk", "Wv", "Wo"):
        s[f"L0_attn_{w}"] = ((1024, 1024), F32, "ExternalInput")
    for b in ("bq", "bk", "bv", "bo"):
        s[f"L0_attn_{b}"] = ((1024,), F32, "ExternalInput")
    s.update({
        "L0_ffn_gate_w": ((16, 1024), F32, "ExternalInput"),
        "L0_ffn_gate_b": ((16,), F32, "ExternalInput"),
        "L0_ffn_We1": ((16, 1024, 512), F32, "ExternalInput"),
        "L0_ffn_be1": ((16, 512), F32, "ExternalInput"),
        "L0_ffn_We2": ((16, 512, 1024), F32, "ExternalInput"),
        "L0_ffn_be2": ((16, 1024), F32, "ExternalInput"),
        "L0_ffn_We3": ((16, 1024, 512), F32, "ExternalInput"),
        "L0_ffn_be3": ((16, 512), F32, "ExternalInput"),
        "L0_ffn_ws1": ((1024, 1024), F32, "ExternalInput"),
        "L0_ffn_bs1": ((1024,), F32, "ExternalInput"),
        "L0_ffn_ws2": ((1024, 1024), F32, "ExternalInput"),
        "L0_ffn_bs2": ((1024,), F32, "ExternalInput"),
        "L0_ffn_ws3": ((1024, 1024), F32, "ExternalInput"),
        "L0_ffn_bs3": ((1024,), F32, "ExternalInput"),
    })
    for w in ("Wq", "Wk", "Wv", "Wg", "Wo"):
        s[f"L1_attn_{w}"] = ((1024, 1024), F32, "ExternalInput")
    for w in ("Wa", "Wb"):
        s[f"L1_attn_{w}"] = ((16, 1024), F32, "ExternalInput")
    for b in ("ba", "bb"):
        s[f"L1_attn_{b}"] = ((16,), F32, "ExternalInput")
    s["L1_ffn_weight"] = ((1024, 1024), F32, "ExternalInput")
    s["L1_ffn_bias"] = ((1024,), F32, "ExternalInput")
    s["L0_c"] = ((1024,), F32, "ExternalInput")
    s["L0_wmod"] = ((1024, 6144), F32, "ExternalInput")
    s["L1_c"] = ((1024,), F32, "ExternalInput")
    s["L1_wmod"] = ((1024, 6144), F32, "ExternalInput")
    return s


_WS = _wspec()


def _pick(*names):
    s = {}
    for n in names:
        s[n] = _WS[n]
    return s


def _stageA_spec():
    s = {"x": ((T, D), F32, "ExternalInput"),
         "out": ((T, D), F32, "ExternalOutput")}
    s.update(_pick("L0_c", "L0_wmod", *[f"L0_attn_{w}" for w in
                                        ("Wq", "Wk", "Wv", "Wo",
                                         "bq", "bk", "bv", "bo")]))
    s.update(const_ins_spec())
    return s


def _stageB_spec():
    s = {"x": ((T, D), F32, "ExternalInput"),
         "out": ((T, D), F32, "ExternalOutput"),
         "xspill": ((T, D), F32, "Internal")}
    s.update(_pick("L0_c", "L0_wmod",
                   *[k for k in _WS if k.startswith("L0_ffn")]))
    s.update(const_ins_spec())
    return s


def _stageC_spec():
    s = {"x": ((T, D), F32, "ExternalInput"),
         "x3": ((T, D), F32, "ExternalOutput"),
         "xspillC": ((T, D), F32, "Internal")}
    s.update(_pick("L1_c", "L1_wmod",
                   *[k for k in _WS if k.startswith("L1_attn")]))
    s.update(const_ins_spec())
    return s


def _stageD_spec():
    s = {"x3": ((T, D), F32, "ExternalInput"),
         "out": ((T, D), F32, "ExternalOutput")}
    s.update(_pick("L1_c", "L1_wmod", "L1_ffn_weight", "L1_ffn_bias"))
    s.update(const_ins_spec())
    return s


def _fused_spec():
    s = {"x": ((T, D), F32, "ExternalInput"),
         "out": ((T, D), F32, "ExternalOutput"),
         "xspill": ((T, D), F32, "Internal"),
         "xspillC": ((T, D), F32, "Internal"),
         "x3": ((T, D), F32, "Internal")}
    s.update(_WS)
    s.update(const_ins_spec())
    return s


def _emitA(ctx):
    load_consts(ctx)
    load_x(ctx)
    emit_mod(ctx, 0, 0)
    emit_l0_attn(ctx)
    store_x(ctx)


def _emitB(ctx):
    load_consts(ctx)
    load_x(ctx)
    emit_mod(ctx, 0, 1)
    emit_moe(ctx)
    store_x(ctx)


def _emitC(ctx):
    load_consts(ctx)
    emit_l1_consts(ctx)
    load_x(ctx)
    emit_mod(ctx, 1, 0)
    emit_l1_weights(ctx)
    emit_l1_attn(ctx)


def _emitD(ctx):
    load_consts(ctx)
    emit_mod(ctx, 1, 1)
    emit_l1_ffn(ctx)


def _emitF(ctx):
    load_consts(ctx)
    load_x(ctx)
    emit_mod(ctx, 0, 0)
    emit_l0_attn(ctx)
    emit_mod(ctx, 0, 1)
    emit_moe(ctx)
    emit_global_fence(ctx)
    emit_l1_consts(ctx)
    emit_mod(ctx, 1, 0)
    emit_l1_weights(ctx)
    emit_l1_attn(ctx)
    emit_mod(ctx, 1, 1)
    emit_l1_ffn(ctx)


_PROG_CACHE = {}


def _get_prog(name, specf, emitf):
    if name not in _PROG_CACHE:
        _PROG_CACHE[name] = build_program(specf(), emitf)
    return _PROG_CACHE[name][0]


def _launch(nc, spec, per_core):
    res = run_bass_kernel_spmd(nc, per_core, core_ids=list(range(NCORES)))
    return res


def kernel(**inputs):
    ins = {k: np.asarray(v, np.float32) for k, v in inputs.items()}
    consts = make_const_arrays()
    x = np.ascontiguousarray(ins["L0_attn_hidden_states"].reshape(NCORES, T, D))

    def wmaps(prefix_keys, extra):
        maps = []
        for b in range(NCORES):
            m = {}
            for k in prefix_keys:
                v = ins[k]
                m[k] = np.ascontiguousarray(v[b]) if k in ("L0_c", "L1_c") \
                    else v
            m.update(extra(b))
            m.update(consts)
            maps.append(m)
        return maps

    if FUSED:
        nc = _get_prog("F", _fused_spec, _emitF)
        keys = [k for k in _WS]
        res = _launch(nc, None, wmaps(keys, lambda b: {"x": x[b]}))
        out = np.stack([np.asarray(res.results[b]["out"])
                        for b in range(NCORES)])
        return out.reshape(NCORES, T, D).astype(np.float32)

    ncA = _get_prog("A", _stageA_spec, _emitA)
    kA = ["L0_c", "L0_wmod"] + [f"L0_attn_{w}" for w in
                                ("Wq", "Wk", "Wv", "Wo", "bq", "bk", "bv",
                                 "bo")]
    resA = _launch(ncA, None, wmaps(kA, lambda b: {"x": x[b]}))
    x1 = [np.asarray(resA.results[b]["out"]) for b in range(NCORES)]

    ncB = _get_prog("B", _stageB_spec, _emitB)
    kB = ["L0_c", "L0_wmod"] + [k for k in _WS if k.startswith("L0_ffn")]
    resB = _launch(ncB, None, wmaps(kB, lambda b: {"x": x1[b]}))
    x2 = [np.asarray(resB.results[b]["out"]) for b in range(NCORES)]

    ncC = _get_prog("C", _stageC_spec, _emitC)
    kC = ["L1_c", "L1_wmod"] + [k for k in _WS if k.startswith("L1_attn")]
    resC = _launch(ncC, None, wmaps(kC, lambda b: {"x": x2[b]}))
    x3 = [np.asarray(resC.results[b]["x3"]) for b in range(NCORES)]

    ncD = _get_prog("D", _stageD_spec, _emitD)
    kD = ["L1_c", "L1_wmod", "L1_ffn_weight", "L1_ffn_bias"]
    resD = _launch(ncD, None, wmaps(kD, lambda b: {"x3": x3[b]}))
    out = np.stack([np.asarray(resD.results[b]["out"])
                    for b in range(NCORES)])
    return out.reshape(NCORES, T, D).astype(np.float32)


# version 04
# speedup vs baseline: 1.0095x; 1.0095x over previous
import contextlib
import numpy as np
import concourse.bass as bass
import concourse.mybir as mybir
from concourse.bass_utils import run_bass_kernel_spmd

F32 = mybir.dt.float32
BF16 = mybir.dt.bfloat16
U32 = mybir.dt.uint32
AF = mybir.ActivationFunctionType
ALU = mybir.AluOpType
AX = mybir.AxisListType

T = 4096
D = 1024
NT = 32
FT = 8
NH = 16
HD = 64
EPS0 = 1e-6

ENGS = ("tensor", "vector", "scalar", "gpsimd", "sync")


class Prog:
    def __init__(self, nc):
        self.nc = nc
        self.ops = []
        self.last_w = {}
        self.readers = {}

    def op(self, eng, emit, r=(), w=(), dma=False, tag=""):
        idx = len(self.ops)
        deps = set()
        for k in r:
            if k in self.last_w:
                deps.add(self.last_w[k])
        for k in w:
            if k in self.last_w:
                deps.add(self.last_w[k])
            for rd in self.readers.get(k, ()):
                deps.add(rd)
        deps.discard(idx)
        self.ops.append(dict(eng=eng, emit=emit, deps=deps, dma=dma,
                             idx=idx, tag=tag))
        for k in r:
            self.readers.setdefault(k, []).append(idx)
        for k in w:
            self.last_w[k] = idx
            self.readers[k] = [idx]
        return idx

    def marker(self, ctx, r, w, tag="marker"):
        self.op("vector", lambda: ctx.nc.vector.memset(
            ctx.sb["stats"][:, 127:128], 0.0), r=r, w=list(w) + [("st", 127)],
            tag=tag)

    def final_barrier(self):
        deps = set()
        for i, o in enumerate(self.ops):
            if o["dma"]:
                deps.add(i)
        lastof = {}
        for i, o in enumerate(self.ops):
            lastof[o["eng"]] = i
        deps.update(lastof.values())
        self.ops.append(dict(eng="sync", emit=None, deps=deps, dma=False,
                             idx=len(self.ops), tag="final"))

    SERIALIZE_DMA = True

    def emit(self, sems, block):
        ops = self.ops
        if self.SERIALIZE_DMA:
            prev = {}
            for o in ops:
                if o["dma"]:
                    if o["eng"] in prev:
                        o["deps"].add(prev[o["eng"]])
                    prev[o["eng"]] = o["idx"]
        need_inc = [False] * len(ops)

        def edge_needs_sem(od, o):
            if od["eng"] != o["eng"] or od["dma"]:
                return True
            return od["eng"] != "tensor"

        def chan(o):
            return o["eng"] + "_dma" if o["dma"] else o["eng"]

        for o in ops:
            for d in o["deps"]:
                od = ops[d]
                if edge_needs_sem(od, o):
                    need_inc[d] = True
        cum = {}
        for o in ops:
            c = chan(o)
            if need_inc[o["idx"]]:
                cum[c] = cum.get(c, 0) + (16 if o["dma"] else 1)
            o["cum"] = cum.get(c, 0)
        for o in ops:
            wv = {}
            for d in o["deps"]:
                od = ops[d]
                if not edge_needs_sem(od, o):
                    continue
                if not need_inc[od["idx"]]:
                    continue
                c = chan(od)
                wv[c] = max(wv.get(c, 0), od["cum"])
            o["waits"] = wv
        stats = {e: sum(1 for o in ops if o["eng"] == e) for e in ENGS}

        def run_engine(handle, name):
            for o in ops:
                if o["eng"] != name:
                    continue
                for e, v in sorted(o["waits"].items()):
                    handle.wait_ge(sems[e], v)
                if o["emit"] is None:
                    continue
                inst = o["emit"]()
                if need_inc[o["idx"]]:
                    inst.then_inc(sems[chan(o)], 16 if o["dma"] else 1)

        @block.sync
        def _(h):
            run_engine(h, "sync")

        @block.tensor
        def _(h):
            run_engine(h, "tensor")

        @block.vector
        def _(h):
            run_engine(h, "vector")

        @block.scalar
        def _(h):
            run_engine(h, "scalar")

        @block.gpsimd
        def _(h):
            run_engine(h, "gpsimd")
        return stats


def make_const_arrays():
    c = {}
    c["c_ident_f32"] = np.eye(128, dtype=np.float32)
    c["c_ident_bf"] = np.eye(128, dtype=np.float32)
    c["c_ones_row"] = np.ones((1, 128), dtype=np.float32)
    s_i = np.arange(128)
    c["c_triu_incl"] = -np.triu(np.ones((128, 128), np.float32))
    e127 = np.zeros((128, 128), np.float32)
    e127[127, :] = 1.0
    c["c_e127"] = e127
    c["c_maskT"] = np.where(s_i[:, None] < s_i[None, :], 0.0,
                            -1e30).astype(np.float32)
    blk = s_i // 32
    c["c_m32bd"] = (blk[:, None] == blk[None, :]).astype(np.float32)
    c["c_m32off"] = (blk[:, None] < blk[None, :]).astype(np.float32)
    return c


class Ctx:
    def __init__(self, nc, prog, sb, ps, ins, consts):
        self.nc = nc
        self.prog = prog
        self.sb = sb
        self.ps = ps
        self.ins = ins
        self.c = consts


def alloc_all(stack, nc, need_l1=True):
    sb = {}
    ps = {}

    def S(name, shape, dtype):
        sb[name] = stack.enter_context(nc.sbuf_tensor(name, shape, dtype))

    def P(name, shape, dtype=F32):
        ps[name] = stack.enter_context(nc.psum_tensor(name, shape, dtype))

    S("x_res", [128, NT * D], F32)
    S("wbuf", [128, 16 * 1024], BF16)
    S("work_bf", [128, 4096], BF16)
    S("work_b", [128, 2048], F32)
    S("l1big", [128, 2048], BF16)
    S("modt", [128, 3 * D], BF16)
    S("biast", [128, 2 * D], BF16)
    S("mrows", [1, D], F32)
    S("stats", [128, 128], F32)
    S("kvsb", [128, 512], BF16)
    S("xstage", [128, D], F32)
    S("gwT", [128, 128], BF16)
    S("gb16", [128, 16], F32)
    S("bemat", [128, 256], F32)
    S("cmb", [128, 17 * NT], F32)
    S("ident_f32", [128, 128], F32)
    S("ident_bf", [128, 128], BF16)
    S("ones_row", [1, 128], F32)
    S("ctrilu", [128, 128], F32)
    S("ce127", [128, 128], F32)
    S("cmaskT", [128, 128], F32)
    S("cm32bd", [128, 128], BF16)
    S("cm32off", [128, 128], BF16)
    sb["hTw"] = sb["l1big"][:, 0:1024]
    sb["acttrash"] = sb["l1big"][:, 1024:2048]
    if need_l1:
        pass
    P("psA", [128, 512], F32)
    P("psB", [128, 512], F32)
    P("psC", [128, 512], F32)
    P("psD", [128, 512], F32)
    P("psKV", [128, 512], F32)
    P("psT", [128, 1024], BF16)
    P("psE", [128, 512], F32)
    P("psF", [128, 512], F32)
    return sb, ps



def load_x(ctx):
    nc, p = ctx.nc, ctx.prog
    x = ctx.ins["x"].rearrange("(t p) f -> p t f", p=128)
    dst = ctx.sb["x_res"][:, :].rearrange("p (t f) -> p t f", f=D)
    p.op("sync", lambda: nc.sync.dma_start(out=dst, in_=x),
         w=[("x", t) for t in range(NT)], dma=True, tag="ldx")


def store_x(ctx, out_name="out"):
    nc, p = ctx.nc, ctx.prog
    dst = ctx.ins[out_name].rearrange("(t p) f -> p t f", p=128)
    src = ctx.sb["x_res"][:, :].rearrange("p (t f) -> p t f", f=D)
    p.op("sync", lambda: nc.sync.dma_start(out=dst, in_=src),
         r=[("x", t) for t in range(NT)], dma=True, tag="stx")


def load_consts(ctx, names=("c_ident_f32", "c_ones_row")):
    nc, p = ctx.nc, ctx.prog
    key = {"c_ident_f32": "ident_f32", "c_ones_row": "ones_row"}
    for n in names:
        k = key[n]
        p.op("sync", lambda n=n, k=k: nc.sync.dma_start(
            out=ctx.sb[k][0:ctx.ins[n].shape[0], :], in_=ctx.ins[n]),
            w=[k], dma=True, tag="ldconst")
    p.op("gpsimd", lambda: nc.gpsimd.dma_start(
        out=ctx.sb["ident_bf"][:, :], in_=ctx.ins["c_ident_bf"]),
        w=["ident_bf"], dma=True, tag="ldconst")


WSLOT = [0, 8 * 1024]


def stream_weight(ctx, in_name, slot, rows=1024, tag="ldw", half=None):
    nc, p = ctx.nc, ctx.prog
    nt = rows // 128
    base = WSLOT[slot] + (0 if half is None else half * 4 * 1024)
    src = ctx.ins[in_name].rearrange("(c p) f -> p c f", p=128)
    dst = ctx.sb["wbuf"][:, base:base + nt * 1024].rearrange(
        "p (c f) -> p c f", f=1024)
    wk = [("wslot", slot) if half is None else ("wslot", slot, half)]
    if slot == 1 and rows == 1024:
        wk.append("ctTregion")
    p.op("gpsimd", lambda: nc.gpsimd.dma_start(out=dst, in_=src),
         w=wk, dma=True, tag=tag)


def wsl(ctx, slot, ci, f0, fn, half=None):
    base = WSLOT[slot] + (0 if half is None else half * 4 * 1024)
    o = base + ci * 1024 + f0
    return ctx.sb["wbuf"][:, o:o + fn]


def bcast_row(ctx, src_ap, dst_ap, keys_r, key_w, add_const=None,
              bank="psC", tag="bcast"):
    nc, p = ctx.nc, ctx.prog
    n = src_ap.shape[-1]
    b = ctx.ps[bank]
    for h in range((n + 511) // 512):
        w0 = h * 512
        wn = min(512, n - w0)
        p.op("tensor", lambda w0=w0, wn=wn: nc.tensor.matmul(
            b[0:128, 0:wn], ctx.sb["ones_row"][0:1, :],
            src_ap[0:1, w0:w0 + wn], start=True, stop=True),
            r=list(keys_r) + ["ones_row"], w=[bank], tag=tag)
        if add_const is None:
            p.op("vector", lambda w0=w0, wn=wn: nc.vector.tensor_copy(
                dst_ap[:, w0:w0 + wn], b[0:128, 0:wn]),
                r=[bank], w=[key_w], tag=tag + "e")
        else:
            p.op("vector", lambda w0=w0, wn=wn, a=float(add_const):
                 nc.vector.tensor_scalar(
                     out=dst_ap[:, w0:w0 + wn], in0=b[0:128, 0:wn],
                     scalar1=a, scalar2=None, op0=ALU.add),
                 r=[bank], w=[key_w], tag=tag + "e")


def emit_mod(ctx, layer, half):
    nc, p = ctx.nc, ctx.prog
    pref = "L0" if layer == 0 else "L1"
    st = ctx.sb["stats"]
    stage = ctx.sb["mrows"]
    banks = ["psA", "psB", "psC", "psD", "psE", "psF"]
    kc = ("cfm", layer)
    c_ap = ctx.ins[f"{pref}_c"].rearrange("(k p) -> p k", p=128)
    p.op("sync", lambda: nc.sync.dma_start(
        out=st[:, 96:104], in_=c_ap, allow_slow_non_contiguous=True),
        w=[kc], dma=True, tag="ldc")
    p.op("scalar", lambda: nc.scalar.activation(
        out=st[:, 104:112], in_=st[:, 96:104], func=AF.Silu),
        r=[kc], w=[("cfs", layer)], tag="silc")
    wst = ctx.sb["wbuf"].bitcast(F32)
    for ci in range(8):
        wm = ctx.ins[f"{pref}_wmod"]
        j0 = half * 6
        p.op("sync", lambda ci=ci, wm=wm, j0=j0: nc.sync.dma_start(
            out=wst[:, 0:3072],
            in_=wm[128 * ci:128 * ci + 128, 512 * j0:512 * j0 + 3072]),
            w=[("wslot", 0), ("wslot", 1)], dma=True, tag="ldwmod")
        for jj in range(6):
            b = ctx.ps[banks[jj]]
            bkey = banks[jj]
            p.op("tensor", lambda ci=ci, jj=jj, b=b: nc.tensor.matmul(
                b[0:1, 0:512], st[:, 104 + ci:105 + ci],
                wst[:, 512 * jj:512 * jj + 512],
                start=(ci == 0), stop=(ci == 7)),
                r=[("cfs", layer), ("wslot", 0), ("wslot", 1)],
                w=[bkey], tag="modmm")
    for r in range(3):
        for chunk in range(2):
            jj = r * 2 + chunk
            b = ctx.ps[banks[jj]]
            bkey = banks[jj]
            p.op("vector", lambda chunk=chunk, b=b: nc.vector.tensor_copy(
                stage[0:1, 512 * chunk:512 * chunk + 512], b[0:1, 0:512]),
                r=[bkey], w=["mstage"], tag="modevac")
        if r == 2:
            p.op("scalar", lambda: nc.scalar.activation(
                out=ctx.sb["acttrash"][0:1, 0:D], in_=stage[0:1, :],
                func=AF.Sign),
                r=["mstage"], w=["acttrash", "acttrash2"], tag="gsign")
            p.op("scalar", lambda: nc.scalar.activation(
                out=ctx.sb["mrows"][0:1, 0:D], in_=stage[0:1, :],
                func=AF.Abs),
                r=["mstage"], w=["mstage"], tag="gabs0")
            p.op("vector", lambda: nc.vector.tensor_scalar(
                out=stage[0:1, :], in0=ctx.sb["mrows"][0:1, 0:D],
                scalar1=0.25, scalar2=None, op0=ALU.max),
                r=["mstage"], w=["mstage"], tag="gabs")
            p.op("vector", lambda: nc.vector.tensor_mul(
                stage[0:1, :], stage[0:1, :], ctx.sb["acttrash"][0:1, 0:D]),
                r=["acttrash", "mstage"], w=["mstage"], tag="gmul")
        dst = ctx.sb["modt"][:, r * D:(r + 1) * D]
        add = 1.0 if r == 1 else None
        bcast_row(ctx, stage[0:1, :], dst, ["mstage"], ("modt", r),
                  add_const=add, bank=banks[r], tag="modt")


MOD_SLOT = {"sa": 0, "sb": 0, "ca1p": 1, "cb1p": 1, "ga": 2, "gb": 2}


def emit_bias_tile(ctx, in_name, dst_slot, tag="biast"):
    nc, p = ctx.nc, ctx.prog
    row = ctx.sb["mrows"][0:1, 0:D]
    p.op("sync", lambda: nc.sync.dma_start(
        out=row, in_=ctx.ins[in_name][None, :]),
        w=["mstage"], dma=True, tag="ldb")
    dst = ctx.sb["biast"][:, dst_slot * D:(dst_slot + 1) * D]
    bcast_row(ctx, row, dst, ["mstage"], ("biast", dst_slot),
              bank="psD", tag=tag)


def emit_h_tile(ctx, t, mul_slot, add_slot, dst, dkey, transpose=True,
                xt_ap=None, xt_key=None, dst_col=None, tag="h"):
    nc, p = ctx.nc, ctx.prog
    st = ctx.sb["stats"]
    work = ctx.sb["work_b"]
    if xt_ap is None:
        xt = ctx.sb["x_res"][:, t * D:(t + 1) * D]
        xk = ("x", t)
    else:
        xt, xk = xt_ap, xt_key
    out_f32 = (transpose == "f32")
    if out_f32:
        transpose = False
    mm = ctx.sb["modt"][:, mul_slot * D:(mul_slot + 1) * D]
    ma = ctx.sb["modt"][:, add_slot * D:(add_slot + 1) * D]
    psT = ctx.ps["psT"]
    hbf = ctx.sb["work_bf"][:, 0:D] if transpose else dst
    hbk = ("wbf", 0) if transpose else (dkey, "tm")
    p.op("vector", lambda: nc.vector.tensor_reduce(
        st[:, 0:1], xt, AX.X, ALU.add),
        r=[xk], w=[("st", 0)], tag=tag + "s")
    p.op("scalar", lambda: nc.scalar.activation(
        out=ctx.sb["acttrash"][:, 0:D], in_=xt, func=AF.Square),
        r=[xk], w=["acttrash", "acttrash2"], tag=tag + "q0")
    p.op("vector", lambda: nc.vector.tensor_reduce(
        st[:, 1:2], ctx.sb["acttrash"][:, 0:D], AX.X, ALU.add),
        r=["acttrash"], w=[("st", 1)], tag=tag + "q")
    p.op("vector", lambda: nc.vector.tensor_scalar(
        out=st[:, 2:3], in0=st[:, 0:1], scalar1=1.0 / D, scalar2=None,
        op0=ALU.mult), r=[("st", 0)], w=[("st", 2)], tag=tag)
    p.op("vector", lambda: nc.vector.tensor_scalar(
        out=st[:, 3:4], in0=st[:, 1:2], scalar1=1.0 / D, scalar2=None,
        op0=ALU.mult), r=[("st", 1)], w=[("st", 3)], tag=tag)
    p.op("vector", lambda: nc.vector.tensor_mul(
        st[:, 4:5], st[:, 2:3], st[:, 2:3]),
        r=[("st", 2)], w=[("st", 4)], tag=tag)
    p.op("vector", lambda: nc.vector.tensor_sub(
        st[:, 5:6], st[:, 3:4], st[:, 4:5]),
        r=[("st", 3), ("st", 4)], w=[("st", 5)], tag=tag)
    p.op("vector", lambda: nc.vector.tensor_scalar(
        out=st[:, 5:6], in0=st[:, 5:6], scalar1=1e-5, scalar2=None,
        op0=ALU.add),
        r=[("st", 5)], w=[("st", 5)], tag=tag)
    p.op("scalar", lambda: nc.scalar.activation(
        out=st[:, 6:7], in_=st[:, 5:6], func=AF.Ln),
        r=[("st", 5)], w=[("st", 6)], tag=tag)
    p.op("scalar", lambda: nc.scalar.activation(
        out=st[:, 7:8], in_=st[:, 6:7], func=AF.Exp, scale=-0.5),
        r=[("st", 6)], w=[("st", 7)], tag=tag)
    p.op("vector", lambda: nc.vector.tensor_scalar(
        out=st[:, 8:9], in0=st[:, 2:3], scalar1=st[:, 7:8], scalar2=-1.0,
        op0=ALU.mult, op1=ALU.mult),
        r=[("st", 2), ("st", 7)], w=[("st", 8)], tag=tag)
    p.op("vector", lambda: nc.vector.scalar_tensor_tensor(
        out=work[:, 0:D], in0=mm, scalar=st[:, 8:9], in1=ma,
        op0=ALU.mult, op1=ALU.add),
        r=[("st", 8), ("modt", mul_slot), ("modt", add_slot)],
        w=["wb01"], tag=tag + "t2")
    p.op("vector", lambda: nc.vector.scalar_tensor_tensor(
        out=work[:, D:2 * D], in0=xt, scalar=st[:, 7:8], in1=mm,
        op0=ALU.mult, op1=ALU.mult),
        r=[xk, ("st", 7), ("modt", mul_slot)], w=["wb23"], tag=tag + "xm")
    if out_f32:
        hbf, hbk = dst, dkey
    p.op("gpsimd", lambda: nc.gpsimd.tensor_add(
        hbf, work[:, D:2 * D], work[:, 0:D]),
        r=["wb01", "wb23"], w=[hbk], tag=tag + "a")
    if not transpose:
        return
    for c in range(FT):
        p.op("tensor", lambda c=c: nc.tensor.transpose(
            psT[0:128, 128 * (c % 4):128 * (c % 4) + 128],
            hbf[:, 128 * c:128 * c + 128], ctx.sb["ident_bf"][:, :]),
            r=[hbk, "ident_bf"], w=["psT"], tag=tag + "tr")
        eng = "vector" if c % 2 == 0 else "scalar"
        fn = nc.vector.tensor_copy if c % 2 == 0 else nc.scalar.copy
        c0 = 128 * c if dst_col is None else dst_col(c)
        wk = (dkey, c) if dst_col is None else (dkey, c, t)
        p.op(eng, lambda c=c, fn=fn, c0=c0: fn(
            dst[:, c0:c0 + 128],
            psT[0:128, 128 * (c % 4):128 * (c % 4) + 128]),
            r=["psT"], w=[wk], tag=tag + "te")



def posfeat_ops(ctx, bank, bkey, sb0, out_ap, outk, bias_slot, with_den,
                tag="pf"):
    nc, p = ctx.nc, ctx.prog
    st = ctx.sb["stats"]
    work = ctx.sb["work_b"]
    wreg = "wb01" if sb0 < 48 else "wb23"
    woff = 0 if sb0 < 48 else 1024
    bt = ctx.sb["biast"][:, bias_slot * D:(bias_slot + 1) * D]
    half = 0 if sb0 < 48 else 1
    bth = bt[:, half * 512:half * 512 + 512]
    p.op("vector", lambda: nc.vector.tensor_add(
        work[:, woff:woff + 512], bank[0:128, 0:512], bth),
        r=[bkey, ("biast", bias_slot)], w=[wreg], tag=tag + "b")
    p.op("vector", lambda: nc.vector.tensor_reduce(
        st[:, sb0:sb0 + 8],
        work[:, woff:woff + 512].rearrange("p (h d) -> p h d", d=HD),
        AX.X, ALU.max),
        r=[wreg], w=[("st", sb0)], tag=tag + "m")
    p.op("gpsimd", lambda: nc.gpsimd.tensor_sub(
        work[:, woff + 512:woff + 1024].rearrange("p (h d) -> p h d", d=HD),
        work[:, woff:woff + 512].rearrange("p (h d) -> p h d", d=HD),
        st[:, sb0:sb0 + 8].rearrange("p (h o) -> p h o", o=1).broadcast_to(
            [128, 8, HD])),
        r=[wreg, ("st", sb0)], w=[wreg], tag=tag + "s")
    p.op("scalar", lambda: nc.scalar.activation(
        out=work[:, woff:woff + 512], in_=work[:, woff + 512:woff + 1024],
        func=AF.Exp),
        r=[wreg], w=[wreg], tag=tag + "e")
    p.op("vector", lambda: nc.vector.tensor_reduce(
        st[:, sb0 + 8:sb0 + 16],
        work[:, woff:woff + 512].rearrange("p (h d) -> p h d", d=HD),
        AX.X, ALU.add),
        r=[wreg], w=[("st", sb0 + 8)], tag=tag + "p")
    p.op("vector", lambda: nc.vector.tensor_scalar(
        out=st[:, sb0 + 16:sb0 + 24], in0=st[:, sb0 + 8:sb0 + 16],
        scalar1=EPS0, scalar2=None, op0=ALU.add),
        r=[("st", sb0 + 8)], w=[("st", sb0 + 16)], tag=tag)
    p.op("vector", lambda: nc.vector.reciprocal(
        st[:, sb0 + 24:sb0 + 32], st[:, sb0 + 16:sb0 + 24]),
        r=[("st", sb0 + 16)], w=[("st", sb0 + 24)], tag=tag)
    if with_den:
        p.op("vector", lambda: nc.vector.tensor_mul(
            st[:, sb0 + 16:sb0 + 24], st[:, sb0 + 8:sb0 + 16],
            st[:, sb0 + 24:sb0 + 32]),
            r=[("st", sb0 + 8), ("st", sb0 + 24)], w=[("st", sb0 + 16)],
            tag=tag + "d")
        p.op("vector", lambda: nc.vector.tensor_scalar(
            out=st[:, sb0 + 16:sb0 + 24], in0=st[:, sb0 + 16:sb0 + 24],
            scalar1=EPS0, scalar2=None, op0=ALU.add),
            r=[("st", sb0 + 16)], w=[("st", sb0 + 16)], tag=tag + "d2")
        p.op("vector", lambda: nc.vector.reciprocal(
            st[:, sb0 + 8:sb0 + 16], st[:, sb0 + 16:sb0 + 24]),
            r=[("st", sb0 + 16)], w=[("st", sb0 + 8)], tag=tag)
        p.op("vector", lambda: nc.vector.tensor_mul(
            st[:, sb0 + 24:sb0 + 32], st[:, sb0 + 24:sb0 + 32],
            st[:, sb0 + 8:sb0 + 16]),
            r=[("st", sb0 + 8)], w=[("st", sb0 + 24)], tag=tag)
    p.op("gpsimd", lambda: nc.gpsimd.tensor_mul(
        out_ap.rearrange("p (h d) -> p h d", d=HD),
        work[:, woff:woff + 512].rearrange("p (h d) -> p h d", d=HD),
        st[:, sb0 + 24:sb0 + 32].rearrange(
            "p (h o) -> p h o", o=1).broadcast_to([128, 8, HD])),
        r=[wreg, ("st", sb0 + 24)], w=[outk], tag=tag + "o")


def emit_l0_attn(ctx):
    nc, p = ctx.nc, ctx.prog
    wbf = ctx.sb["work_bf"]
    psA, psB, psC, psD = (ctx.ps[k] for k in ("psA", "psB", "psC", "psD"))
    psKV, psT = ctx.ps["psKV"], ctx.ps["psT"]
    hTw = ctx.sb["hTw"]

    emit_bias_tile(ctx, "L0_attn_bk", 0)
    emit_bias_tile(ctx, "L0_attn_bv", 1)
    stream_weight(ctx, "L0_attn_Wk", 0, tag="ldWk")
    stream_weight(ctx, "L0_attn_Wv", 1, tag="ldWv")

    kn = wbf[:, 2048:3072]
    vt = wbf[:, 3072:4096]

    for t in range(NT):
        emit_h_tile(ctx, t, MOD_SLOT["ca1p"], MOD_SLOT["sa"], hTw, "hTw",
                    tag=f"hA{t}_")
        for half, (bank, bkey) in enumerate((((psA, "psA")), (psB, "psB"))):
            for ci in range(FT):
                p.op("tensor", lambda half=half, ci=ci, bank=bank:
                     nc.tensor.matmul(
                         bank[0:128, 0:512],
                         hTw[:, 128 * ci:128 * ci + 128],
                         wsl(ctx, 0, ci, half * 512, 512),
                         start=(ci == 0), stop=(ci == FT - 1)),
                     r=[("hTw", ci), ("wslot", 0)], w=[bkey], tag="kproj")
            posfeat_ops(ctx, bank, bkey, 16 + 32 * half,
                        kn[:, half * 512:half * 512 + 512], ("wbf", 2),
                        0, False, tag="pfk")
        for half, (bank, bkey) in enumerate((((psC, "psC")), (psD, "psD"))):
            for ci in range(FT):
                p.op("tensor", lambda half=half, ci=ci, bank=bank:
                     nc.tensor.matmul(
                         bank[0:128, 0:512],
                         hTw[:, 128 * ci:128 * ci + 128],
                         wsl(ctx, 1, ci, half * 512, 512),
                         start=(ci == 0), stop=(ci == FT - 1)),
                     r=[("hTw", ci), ("wslot", 1)], w=[bkey], tag="vproj")
            p.op("vector", lambda half=half, bank=bank: nc.vector.tensor_add(
                vt[:, half * 512:half * 512 + 512], bank[0:128, 0:512],
                ctx.sb["biast"][:, D + half * 512:D + half * 512 + 512]),
                r=["psC" if half == 0 else "psD", ("biast", 1)],
                w=[("wbf", 3)], tag="vbias")
        for h in range(NH):
            pb = 64 * (h % 2)
            fo = 64 * (h // 2)
            p.op("tensor", lambda h=h, pb=pb, fo=fo, t=t: nc.tensor.matmul(
                psKV[pb:pb + 64, fo:fo + 64],
                kn[:, 64 * h:64 * h + 64], vt[:, 64 * h:64 * h + 64],
                start=(t == 0 and h < 2),
                stop=(t == NT - 1 and h >= NH - 2)),
                r=[("wbf", 2), ("wbf", 3)], w=["psKV"], tag="kvmm")

    kvsb = ctx.sb["kvsb"]
    p.op("vector", lambda: nc.vector.tensor_copy(
        kvsb[:, :], psKV[0:128, 0:512]),
        r=["psKV"], w=["kvsb"], tag="kvevac")

    emit_bias_tile(ctx, "L0_attn_bq", 0)
    emit_bias_tile(ctx, "L0_attn_bo", 1)
    stream_weight(ctx, "L0_attn_Wq", 0, tag="ldWq")
    stream_weight(ctx, "L0_attn_Wo", 1, tag="ldWo")

    qn = wbf[:, 2048:3072]
    qnT = wbf[:, 0:1024]
    cfm = wbf[:, 1024:2048]
    for t in range(NT):
        emit_h_tile(ctx, t, MOD_SLOT["ca1p"], MOD_SLOT["sa"], hTw, "hTw",
                    tag=f"hA2_{t}_")
        for half, (bank, bkey) in enumerate((((psA, "psA")), (psB, "psB"))):
            for ci in range(FT):
                p.op("tensor", lambda half=half, ci=ci, bank=bank:
                     nc.tensor.matmul(
                         bank[0:128, 0:512],
                         hTw[:, 128 * ci:128 * ci + 128],
                         wsl(ctx, 0, ci, half * 512, 512),
                         start=(ci == 0), stop=(ci == FT - 1)),
                     r=[("hTw", ci), ("wslot", 0)], w=[bkey], tag="qproj")
            posfeat_ops(ctx, bank, bkey, 16 + 32 * half,
                        qn[:, half * 512:half * 512 + 512], ("wbf", 2),
                        0, True, tag="pfq")
        for c in range(FT):
            p.op("tensor", lambda c=c: nc.tensor.transpose(
                psT[0:128, 128 * (c % 4):128 * (c % 4) + 128],
                qn[:, 128 * c:128 * c + 128], ctx.sb["ident_bf"][:, :]),
                r=[("wbf", 2), "ident_bf"], w=["psT"], tag="qtr")
            eng = "vector" if c % 2 == 0 else "scalar"
            fn = nc.vector.tensor_copy if c % 2 == 0 else nc.scalar.copy
            p.op(eng, lambda c=c, fn=fn: fn(
                qnT[:, 128 * c:128 * c + 128],
                psT[0:128, 128 * (c % 4):128 * (c % 4) + 128]),
                r=["psT"], w=[("wbf", 0)], tag="qtre")
        for h in range(NH):
            pb = 64 * (h % 2)
            fo = 64 * (h // 2)
            pair = h // 2
            bank = psC if pair < 4 else psD
            bkey = "psC" if pair < 4 else "psD"
            fslot = pair % 4
            p.op("tensor", lambda h=h, pb=pb, fo=fo, bank=bank, fslot=fslot:
                 nc.tensor.matmul(
                     bank[pb:pb + 64, 128 * fslot:128 * fslot + 128],
                     kvsb[pb:pb + 64, fo:fo + 64],
                     qnT[pb:pb + 64, 128 * (h // 2):128 * (h // 2) + 128],
                     start=True, stop=True),
                 r=["kvsb", ("wbf", 0)], w=[bkey], tag="ctxmm")
        for c in range(FT):
            bank = psC if c < 4 else psD
            bkey = "psC" if c < 4 else "psD"
            eng = "vector" if c % 2 == 0 else "scalar"
            fn = nc.vector.tensor_copy if c % 2 == 0 else nc.scalar.copy
            p.op(eng, lambda c=c, fn=fn, bank=bank: fn(
                cfm[:, 128 * c:128 * c + 128],
                bank[0:128, 128 * (c % 4):128 * (c % 4) + 128]),
                r=[bkey], w=[("wbf", 1)], tag="ctxevac")
        for half in range(2):
            bank = psA if half == 0 else psB
            bkey = "psA" if half == 0 else "psB"
            for ci in range(FT):
                p.op("tensor", lambda half=half, ci=ci, bank=bank:
                     nc.tensor.matmul(
                         bank[0:128, 0:512],
                         cfm[:, 128 * ci:128 * ci + 128],
                         wsl(ctx, 1, ci, half * 512, 512),
                         start=(ci == 0), stop=(ci == FT - 1)),
                     r=[("wbf", 1), ("wslot", 1)], w=[bkey], tag="womm")
            wreg = "wb01" if half == 0 else "wb23"
            woff = 0 if half == 0 else 1024
            work = ctx.sb["work_b"]
            p.op("vector", lambda half=half, bank=bank, woff=woff:
                 nc.vector.tensor_add(
                     work[:, woff:woff + 512], bank[0:128, 0:512],
                     ctx.sb["biast"][:, D + half * 512:
                                     D + half * 512 + 512]),
                 r=[bkey, ("biast", 1)], w=[wreg], tag="obias")
            p.op("gpsimd", lambda half=half, woff=woff:
                 nc.gpsimd.tensor_mul(
                     work[:, woff + 512:woff + 1024], work[:, woff:woff + 512],
                     ctx.sb["modt"][:, 2 * D + half * 512:
                                    2 * D + half * 512 + 512]),
                 r=[wreg, ("modt", 2)], w=[wreg], tag="oga")
            xt = ctx.sb["x_res"][:, t * D + half * 512:t * D + half * 512 + 512]
            p.op("vector", lambda xt=xt, woff=woff: nc.vector.tensor_add(
                xt, xt, ctx.sb["work_b"][:, woff + 512:woff + 1024]),
                r=[wreg, ("x", t)], w=[("x", t)], tag="oresid")



def build_program(ins_spec, emit_fn, need_l1=False):
    nc = bass.Bass("TRN2", target_bir_lowering=False)
    ins = {}
    for name, (shape, dtype, kind) in ins_spec.items():
        ins[name] = nc.dram_tensor(name, list(shape), dtype, kind=kind).ap()
    with contextlib.ExitStack() as stack:
        sb, ps = alloc_all(stack, nc, need_l1=need_l1)
        prog = Prog(nc)
        ctx = Ctx(nc, prog, sb, ps, ins, None)
        emit_fn(ctx)
        prog.final_barrier()
        sems = {}
        chans = set()
        for o in prog.ops:
            chans.add(o["eng"] + "_dma" if o["dma"] else o["eng"])
        for e in sorted(chans):
            sems[e] = stack.enter_context(nc.semaphore("sem_" + e))
        block = stack.enter_context(nc.Block())
        stats = prog.emit(sems, block)
    return nc, stats


def const_ins_spec():
    spec = {}
    for k, v in make_const_arrays().items():
        dt = F32
        spec[k] = (v.shape, dt, "ExternalInput")
    return spec



def emit_moe(ctx, stop_after=None, experts=None, dbg=None):
    nc, p = ctx.nc, ctx.prog
    st = ctx.sb["stats"]
    x_res = ctx.sb["x_res"]
    hTf = x_res.bitcast(BF16)[:, 0:FT * T]
    yac = x_res.bitcast(BF16)[:, FT * T:2 * FT * T]
    ctT = ctx.sb["wbuf"][0:17, 12288:16384]
    psA, psB, psC, psD = (ctx.ps[k] for k in ("psA", "psB", "psC", "psD"))
    psE, psF = ctx.ps["psE"], ctx.ps["psF"]

    p.op("sync", lambda: nc.sync.dma_start(
        out=ctx.ins["xspill"].rearrange("(t p) f -> p t f", p=128),
        in_=x_res[:, :].rearrange("p (t f) -> p t f", f=D)),
        r=[("x", t) for t in range(NT)], w=["xspilled"], dma=True, tag="spill")

    p.op("vector", lambda: nc.vector.memset(yac, 0.0),
         r=["xspilled"], w=[("y", t) for t in range(NT)], tag="yzero")

    wb = ctx.sb["work_b"]
    p.op("sync", lambda: nc.sync.dma_start(
        out=wb[0:16, 0:1024], in_=ctx.ins["L0_ffn_gate_w"]),
        w=["wb01", "wb23"], dma=True, tag="ldgw")
    for ci in range(8):
        p.op("tensor", lambda ci=ci: nc.tensor.transpose(
            psF[0:128, 0:16], wb[0:16, 128 * ci:128 * ci + 128],
            ctx.sb["ident_f32"][0:16, 0:16]),
            r=["wb01", "wb23", "ident_f32"], w=["psF"], tag="gwtr")
        p.op("vector", lambda ci=ci: nc.vector.tensor_copy(
            ctx.sb["gwT"][:, 16 * ci:16 * ci + 16], psF[0:128, 0:16]),
            r=["psF"], w=["gwT"], tag="gwtre")
    p.op("sync", lambda: nc.sync.dma_start(
        out=ctx.sb["mrows"][0:1, 0:16], in_=ctx.ins["L0_ffn_gate_b"][None, :]),
        w=["mstage"], dma=True, tag="ldgb")
    bcast_row(ctx, ctx.sb["mrows"][0:1, 0:16], ctx.sb["gb16"], ["mstage"],
              "gb16", bank="psE", tag="gbt")
    p.op("sync", lambda: nc.sync.dma_start(
        out=wb[0:16, 0:512], in_=ctx.ins["L0_ffn_be1"]),
        w=["wb01"], dma=True, tag="ldbe1")
    p.op("sync", lambda: nc.sync.dma_start(
        out=wb[0:16, 512:1024], in_=ctx.ins["L0_ffn_be3"]),
        w=["wb23"], dma=True, tag="ldbe3")
    for ci in range(4):
        for which, base in ((0, 0), (1, 64)):
            p.op("tensor", lambda ci=ci, which=which: nc.tensor.transpose(
                psF[0:128, 0:16],
                wb[0:16, 512 * which + 128 * ci:512 * which + 128 * ci + 128],
                ctx.sb["ident_f32"][0:16, 0:16]),
                r=["wb01", "wb23", "ident_f32"], w=["psF"], tag="betr")
            p.op("vector", lambda ci=ci, base=base: nc.vector.tensor_copy(
                ctx.sb["bemat"][:, base + 16 * ci:base + 16 * ci + 16],
                psF[0:128, 0:16]),
                r=["psF"], w=["bemat"], tag="betre")
    p.op("sync", lambda: nc.sync.dma_start(
        out=wb[0:1, 0:1024], in_=ctx.ins["L0_ffn_bs1"][None, :]),
        w=["wb01", "wb23"], dma=True, tag="ldbs1")
    p.op("sync", lambda: nc.sync.dma_start(
        out=wb[1:2, 0:1024], in_=ctx.ins["L0_ffn_bs3"][None, :]),
        w=["wb01", "wb23"], dma=True, tag="ldbs3")
    for ci in range(8):
        p.op("tensor", lambda ci=ci: nc.tensor.transpose(
            psF[0:128, 0:2], wb[0:2, 128 * ci:128 * ci + 128],
            ctx.sb["ident_f32"][0:2, 0:2]),
            r=["wb01", "wb23", "ident_f32"], w=["psF"], tag="bstr")
        p.op("vector", lambda ci=ci: nc.vector.tensor_copy(
            ctx.sb["bemat"][:, 128 + 2 * ci:128 + 2 * ci + 2],
            psF[0:128, 0:2]),
            r=["psF"], w=["bemat"], tag="bstre")

    for t in range(NT):
        p.op("sync", lambda t=t: nc.sync.dma_start(
            out=ctx.sb["xstage"][:, :],
            in_=ctx.ins["xspill"][128 * t:128 * t + 128, :]),
            r=["xspilled", "xstage"], w=["xstage"], dma=True, tag="ldxs")
        emit_h_tile(ctx, t, MOD_SLOT["cb1p"], MOD_SLOT["sb"], hTf, "hTf",
                    xt_ap=ctx.sb["xstage"][:, :], xt_key="xstage",
                    dst_col=lambda c, t=t: T * c + 128 * t, tag=f"hB{t}_")

    for t in range(NT):
        for ci in range(8):
            p.op("tensor", lambda ci=ci, t=t: nc.tensor.matmul(
                psA[0:128, 0:16],
                hTf[:, T * ci + 128 * t: T * ci + 128 * t + 128],
                ctx.sb["gwT"][:, 16 * ci:16 * ci + 16],
                start=(ci == 0), stop=(ci == 7)),
                r=[("hTf", ci, t), "gwT"], w=["psA"], tag="gatemm")
        p.op("vector", lambda: nc.vector.tensor_copy(
            st[:, 16:32], psA[0:128, 0:16]),
            r=["psA"], w=[("st", 16)], tag="gev")
        p.op("vector", lambda: nc.vector.tensor_reduce(
            st[:, 32:33], st[:, 16:32], AX.X, ALU.max),
            r=[("st", 16)], w=[("st", 32)], tag="gmax")
        p.op("vector", lambda: nc.vector.tensor_scalar(
            out=st[:, 16:32], in0=st[:, 16:32], scalar1=st[:, 32:33],
            scalar2=None, op0=ALU.subtract),
            r=[("st", 16), ("st", 32)], w=[("st", 16)], tag="gsub")
        p.op("scalar", lambda: nc.scalar.activation(
            out=st[:, 16:32], in_=st[:, 16:32], func=AF.Exp),
            r=[("st", 16)], w=[("st", 16)], tag="gexp")
        p.op("vector", lambda: nc.vector.tensor_reduce(
            st[:, 32:33], st[:, 16:32], AX.X, ALU.add),
            r=[("st", 16)], w=[("st", 32)], tag="gsum")
        p.op("vector", lambda: nc.vector.reciprocal(
            st[:, 33:34], st[:, 32:33]),
            r=[("st", 32)], w=[("st", 33)], tag="grec")
        p.op("vector", lambda: nc.vector.tensor_scalar(
            out=st[:, 16:32], in0=st[:, 16:32], scalar1=st[:, 33:34],
            scalar2=None, op0=ALU.mult),
            r=[("st", 16), ("st", 33)], w=[("st", 16)], tag="gnorm")
        p.op("vector", lambda: nc.vector.tensor_add(
            st[:, 48:64], st[:, 16:32], ctx.sb["gb16"][:, :]),
            r=[("st", 16), "gb16"], w=[("st", 48)], tag="gsb")
        sbv = st[:, 48:64].rearrange("p (g k) -> p g k", k=4)
        p.op("vector", lambda: nc.vector.tensor_max(
            st[:, 64:68], sbv[:, :, 0], sbv[:, :, 1]),
            r=[("st", 48)], w=[("st", 64)], tag="g1")
        p.op("vector", lambda: nc.vector.tensor_max(
            st[:, 68:72], sbv[:, :, 2], sbv[:, :, 3]),
            r=[("st", 48)], w=[("st", 68)], tag="g2")
        p.op("vector", lambda: nc.vector.tensor_add(
            st[:, 72:76], sbv[:, :, 0], sbv[:, :, 1]),
            r=[("st", 48)], w=[("st", 72)], tag="g3")
        p.op("vector", lambda: nc.vector.tensor_add(
            st[:, 76:80], sbv[:, :, 2], sbv[:, :, 3]),
            r=[("st", 48)], w=[("st", 76)], tag="g4")
        p.op("vector", lambda: nc.vector.tensor_add(
            st[:, 64:68], st[:, 64:68], st[:, 68:72]),
            r=[("st", 64), ("st", 68)], w=[("st", 64)], tag="g5")
        p.op("vector", lambda: nc.vector.tensor_max(
            st[:, 64:68], st[:, 64:68], st[:, 72:76]),
            r=[("st", 64), ("st", 72)], w=[("st", 64)], tag="g6")
        p.op("vector", lambda: nc.vector.tensor_max(
            st[:, 64:68], st[:, 64:68], st[:, 76:80]),
            r=[("st", 64), ("st", 76)], w=[("st", 64)], tag="g7")
        gv = st[:, 64:68]
        p.op("vector", lambda: nc.vector.tensor_max(
            st[:, 72:73], gv[:, 0:1], gv[:, 1:2]),
            r=[("st", 64)], w=[("st", 72)], tag="s3")
        p.op("vector", lambda: nc.vector.tensor_max(
            st[:, 73:74], gv[:, 2:3], gv[:, 3:4]),
            r=[("st", 64)], w=[("st", 73)], tag="s4")
        p.op("vector", lambda: nc.vector.tensor_tensor(
            out=st[:, 74:75], in0=gv[:, 0:1], in1=gv[:, 1:2], op=ALU.min),
            r=[("st", 64)], w=[("st", 74)], tag="s5")
        p.op("vector", lambda: nc.vector.tensor_tensor(
            out=st[:, 75:76], in0=gv[:, 2:3], in1=gv[:, 3:4], op=ALU.min),
            r=[("st", 64)], w=[("st", 75)], tag="s6")
        p.op("vector", lambda: nc.vector.tensor_tensor(
            out=st[:, 76:77], in0=st[:, 72:73], in1=st[:, 73:74], op=ALU.min),
            r=[("st", 72), ("st", 73)], w=[("st", 76)], tag="s7")
        p.op("vector", lambda: nc.vector.tensor_max(
            st[:, 77:78], st[:, 74:75], st[:, 75:76]),
            r=[("st", 74), ("st", 75)], w=[("st", 77)], tag="s8")
        p.op("vector", lambda: nc.vector.tensor_max(
            st[:, 78:79], st[:, 76:77], st[:, 77:78]),
            r=[("st", 76), ("st", 77)], w=[("st", 78)], tag="s9")
        p.op("vector", lambda: nc.vector.tensor_scalar(
            out=st[:, 80:84], in0=st[:, 64:68], scalar1=st[:, 78:79],
            scalar2=None, op0=ALU.is_ge),
            r=[("st", 64), ("st", 78)], w=[("st", 80)], tag="keep")
        p.op("vector", lambda: nc.vector.tensor_scalar(
            out=st[:, 80:84], in0=st[:, 80:84], scalar1=-1.0, scalar2=1e30,
            op0=ALU.add, op1=ALU.mult),
            r=[("st", 80)], w=[("st", 80)], tag="pen")
        p.op("vector", lambda: nc.vector.tensor_tensor(
            out=st[:, 48:64].rearrange("p (g k) -> p g k", k=4),
            in0=sbv,
            in1=st[:, 80:84].rearrange("p (g o) -> p g o", o=1).broadcast_to(
                [128, 4, 4]),
            op=ALU.add),
            r=[("st", 48), ("st", 80)], w=[("st", 48)], tag="gmask")
        p.op("vector", lambda: nc.vector.tensor_reduce(
            st[:, 32:33], st[:, 48:64], AX.X, ALU.max),
            r=[("st", 48)], w=[("st", 32)], tag="t2m")
        p.op("vector", lambda: nc.vector.tensor_scalar(
            out=st[:, 84:100], in0=st[:, 48:64], scalar1=st[:, 32:33],
            scalar2=None, op0=ALU.is_ge),
            r=[("st", 48), ("st", 32)], w=[("st", 84)], tag="eqm")
        p.op("vector", lambda: nc.vector.tensor_scalar(
            out=st[:, 84:100], in0=st[:, 84:100], scalar1=-2e30,
            scalar2=None, op0=ALU.mult),
            r=[("st", 84)], w=[("st", 84)], tag="eqp")
        p.op("vector", lambda: nc.vector.tensor_add(
            st[:, 84:100], st[:, 84:100], st[:, 48:64]),
            r=[("st", 84), ("st", 48)], w=[("st", 84)], tag="m2")
        p.op("vector", lambda: nc.vector.tensor_reduce(
            st[:, 33:34], st[:, 84:100], AX.X, ALU.max),
            r=[("st", 84)], w=[("st", 33)], tag="snd")
        p.op("vector", lambda: nc.vector.tensor_scalar(
            out=st[:, 84:100], in0=st[:, 48:64], scalar1=st[:, 33:34],
            scalar2=None, op0=ALU.is_ge),
            r=[("st", 48), ("st", 33)], w=[("st", 84)], tag="sel")
        p.op("vector", lambda: nc.vector.tensor_mul(
            st[:, 100:116], st[:, 16:32], st[:, 84:100]),
            r=[("st", 16), ("st", 84)], w=[("st", 100)], tag="comb")
        p.op("vector", lambda: nc.vector.memset(st[:, 116:117], 1.0),
             w=[("st", 116)], tag="c16")
        p.op("gpsimd", lambda t=t: nc.gpsimd.tensor_copy(
            ctx.sb["cmb"][:, 17 * t:17 * t + 17], st[:, 100:117]),
            r=[("st", 100), ("st", 116)], w=[("cmb", t)], tag="cmbcp")
        p.op("tensor", lambda t=t: nc.tensor.transpose(
            psF[0:17, 0:128], st[:, 100:117], ctx.sb["ident_f32"][:, :]),
            r=[("st", 100), ("st", 116), "ident_f32"], w=["psF"], tag="ctr")
        p.op("vector", lambda t=t: nc.vector.tensor_copy(
            ctT[0:17, 128 * t:128 * t + 128], psF[0:17, 0:128]),
            r=["psF"], w=[("ctT", t), "ctTregion"], tag="ctre")

    for ei, e in enumerate(range(18) if experts is None else experts):
        if ei == 1 and stop_after == "experts" and "d_ymid" in ctx.ins:
            p.op("gpsimd", lambda: nc.gpsimd.dma_start(
                out=ctx.ins["d_ymid"], in_=yac[:, 0:1024]),
                r=[("y", 0)], dma=True, tag="dymid")
        if e < 16:
            src1 = ctx.ins["L0_ffn_We1"][e]
            src3 = ctx.ins["L0_ffn_We3"][e]
            src2 = ctx.ins["L0_ffn_We2"][e]
            cbrow = e
        else:
            H = e - 16
            src1 = ctx.ins["L0_ffn_ws1"][:, 512 * H:512 * H + 512]
            src3 = ctx.ins["L0_ffn_ws3"][:, 512 * H:512 * H + 512]
            src2 = ctx.ins["L0_ffn_ws2"][512 * H:512 * H + 512, :]
            cbrow = 16
        if dbg and ei == 1:
            if "w1" in dbg:
                src1 = ctx.ins["L0_ffn_ws1"][:, 0:512]
            if "w3" in dbg:
                src3 = ctx.ins["L0_ffn_ws3"][:, 0:512]
            if "w2" in dbg:
                src2 = ctx.ins["L0_ffn_ws2"][0:512, :]
        p.op("gpsimd", lambda src1=src1: nc.gpsimd.dma_start(
            out=ctx.sb["wbuf"][:, 0:8 * 1024].rearrange(
                "p (c f) -> p c f", f=1024)[:, :, 0:512],
            in_=src1.rearrange("(c p) f -> p c f", p=128)),
            w=[("wslot", 0)], dma=True, tag="ldw1")
        p.op("gpsimd", lambda src3=src3: nc.gpsimd.dma_start(
            out=ctx.sb["wbuf"][:, 0:8 * 1024].rearrange(
                "p (c f) -> p c f", f=1024)[:, :, 512:1024],
            in_=src3.rearrange("(c p) f -> p c f", p=128)),
            w=[("wslot", 0)], dma=True, tag="ldw3")
        p.op("gpsimd", lambda src2=src2: nc.gpsimd.dma_start(
            out=ctx.sb["wbuf"][:, 8 * 1024:12 * 1024].rearrange(
                "p (c f) -> p c f", f=1024),
            in_=src2.rearrange("(c p) f -> p c f", p=128)),
            w=[("wslot", 1)], dma=True, tag="ldw2")

        def bias_ap(which, jp, e=e, ei=ei):
            bm = ctx.sb["bemat"]
            if dbg and ei == 1 and "bias" in dbg:
                e = 16
            if e < 16:
                return bm[:, which * 64 + 16 * jp + e:
                          which * 64 + 16 * jp + e + 1]
            H = e - 16
            c0 = 128 + 2 * (4 * H + jp) + which
            return bm[:, c0:c0 + 1]

        for k in range(8):
            hm = ctx.sb["work_bf"]
            for jp in range(4):
                for ci in range(8):
                    p.op("tensor", lambda ci=ci, jp=jp, k=k: nc.tensor.matmul(
                        psA[0:128, 0:512],
                        wsl(ctx, 0, ci, 128 * jp, 128),
                        hTf[:, T * ci + 512 * k: T * ci + 512 * k + 512],
                        start=(ci == 0), stop=(ci == 7)),
                        r=[("wslot", 0)] + [("hTf", ci, 4 * k + j)
                                            for j in range(4)],
                        w=["psA"], tag="up1")
                for ci in range(8):
                    p.op("tensor", lambda ci=ci, jp=jp, k=k: nc.tensor.matmul(
                        psB[0:128, 0:512],
                        wsl(ctx, 0, ci, 512 + 128 * jp, 128),
                        hTf[:, T * ci + 512 * k: T * ci + 512 * k + 512],
                        start=(ci == 0), stop=(ci == 7)),
                        r=[("wslot", 0)] + [("hTf", ci, 4 * k + j)
                                            for j in range(4)],
                        w=["psB"], tag="up3")
                b1ap = bias_ap(0, jp)
                b3ap = bias_ap(1, jp)
                p.op("vector", lambda jp=jp, b1ap=b1ap: nc.vector.tensor_scalar(
                    out=ctx.sb["acttrash"][:, 0:512], in0=psA[0:128, 0:512],
                    scalar1=b1ap, scalar2=None, op0=ALU.add),
                    r=["psA", "bemat"], w=["acttrash"], tag="a1")
                p.op("scalar", lambda jp=jp: nc.scalar.activation(
                    out=hm[:, 512 * jp:512 * jp + 512],
                    in_=ctx.sb["acttrash"][:, 0:512], func=AF.Silu),
                    r=["acttrash"], w=[("wbf", jp // 2)], tag="silu")
                p.op("vector", lambda jp=jp, b3ap=b3ap: nc.vector.tensor_scalar(
                    out=ctx.sb["acttrash"][:, 512:1024], in0=psB[0:128, 0:512],
                    scalar1=b3ap, scalar2=None, op0=ALU.add),
                    r=["psB", "bemat"], w=["acttrash2"], tag="u3")
                p.op("gpsimd", lambda jp=jp: nc.gpsimd.tensor_mul(
                    hm[:, 512 * jp:512 * jp + 512],
                    hm[:, 512 * jp:512 * jp + 512],
                    ctx.sb["acttrash"][:, 512:1024]),
                    r=["acttrash2", ("wbf", jp // 2)], w=[("wbf", jp // 2)],
                    tag="hmul")
            if stop_after == "experts" and k == 0 and e == 17 and \
                    "d_hm" in ctx.ins:
                p.op("gpsimd", lambda: nc.gpsimd.dma_start(
                    out=ctx.ins["d_hm"], in_=hm[:, 0:2048]),
                    r=[("wbf", 0), ("wbf", 1)], dma=True, tag="dhm")
            for m in range(4):
                yt = 4 * k + m
                for half in range(2):
                    bank = psE if half == 0 else psF
                    bkey = "psE" if half == 0 else "psF"
                    for jp in range(4):
                        p.op("tensor", lambda jp=jp, m=m, half=half, bank=bank:
                             nc.tensor.matmul(
                                 bank[0:128, 0:512],
                                 hm[:, 512 * jp + 128 * m:
                                    512 * jp + 128 * m + 128],
                                 wsl(ctx, 1, jp, half * 512, 512),
                                 start=(jp == 0), stop=(jp == 3)),
                             r=[("wbf", jp // 2), ("wslot", 1)], w=[bkey],
                             tag="down")
                    ysl = yac[:, 1024 * yt + 512 * half:
                              1024 * yt + 512 * half + 512]
                    cbap = ctx.sb["cmb"][:, 17 * yt + cbrow:17 * yt + cbrow + 1]
                    wsc = ctx.sb["work_b"][:, 0:512] if half == 0 else \
                        ctx.sb["work_b"][:, 1024:1536]
                    wk = "wb01" if half == 0 else "wb23"
                    p.op("vector", lambda wsc=wsc, bank=bank, cbap=cbap:
                         nc.vector.tensor_scalar(
                             out=wsc, in0=bank[0:128, 0:512], scalar1=cbap,
                             scalar2=None, op0=ALU.mult),
                         r=[bkey, ("cmb", yt)], w=[wk], tag="ycb")
                    p.op("vector", lambda ysl=ysl, wsc=wsc:
                         nc.vector.tensor_add(ysl, wsc, ysl),
                         r=[wk, ("y", yt)], w=[("y", yt)], tag="yadd")

    if stop_after == "experts":
        if "d_w2" in ctx.ins:
            p.op("gpsimd", lambda: nc.gpsimd.dma_start(
                out=ctx.ins["d_w2"], in_=ctx.sb["wbuf"][:, 8192:12288]),
                r=[("wslot", 1)], dma=True, tag="dw2")
        return
    p.marker(ctx, r=[("hTf", c, t) for c in range(FT) for t in range(NT)] +
             ["hm", "cbb"], w=["hT_dead"], tag="moedone")
    bes = ctx.sb["wbuf"][0:17, 0:1024]
    p.op("gpsimd", lambda: nc.gpsimd.dma_start(
        out=bes[0:16, :], in_=ctx.ins["L0_ffn_be2"]),
        r=["hT_dead"], w=["bes", ("wslot", 0)], dma=True, tag="ldbe2")
    p.op("gpsimd", lambda: nc.gpsimd.dma_start(
        out=bes[16:17, :], in_=ctx.ins["L0_ffn_bs2"][None, :]),
        w=["bes", ("wslot", 0)], dma=True, tag="ldbs2")
    wbx = ctx.sb["work_b"]
    for t in range(NT):
        p.op("sync", lambda t=t: nc.sync.dma_start(
            out=ctx.sb["xstage"][:, :],
            in_=ctx.ins["xspill"][128 * t:128 * t + 128, :]),
            r=["xspilled", "xstage"], w=["xstage"], dma=True, tag="ldxf")
        for half in range(2):
            bank = psA if half == 0 else psB
            bkey = "psA" if half == 0 else "psB"
            p.op("tensor", lambda t=t, half=half, bank=bank: nc.tensor.matmul(
                bank[0:128, 0:512], ctT[0:17, 128 * t:128 * t + 128],
                bes[0:17, 512 * half:512 * half + 512],
                start=True, stop=True),
                r=[("ctT", t), "ctTregion", "bes"], w=[bkey], tag="besmm")
            wreg = "wb01" if half == 0 else "wb23"
            woff = 0 if half == 0 else 1024
            p.op("vector", lambda t=t, half=half, bank=bank, woff=woff:
                 nc.vector.tensor_add(
                     wbx[:, woff:woff + 512], bank[0:128, 0:512],
                     yac[:, 1024 * t + 512 * half:1024 * t + 512 * half + 512]),
                 r=[bkey, ("y", t)], w=[wreg], tag="ybes")
            p.op("gpsimd", lambda half=half, woff=woff: nc.gpsimd.tensor_mul(
                wbx[:, woff + 512:woff + 1024], wbx[:, woff:woff + 512],
                ctx.sb["modt"][:, 2 * D + 512 * half:2 * D + 512 * half + 512]),
                r=[wreg, ("modt", 2)], w=[wreg], tag="ygb")
            p.op("vector", lambda t=t, half=half, woff=woff:
                 nc.vector.tensor_add(
                     x_res[:, 1024 * t + 512 * half:
                           1024 * t + 512 * half + 512],
                     ctx.sb["xstage"][:, 512 * half:512 * half + 512],
                     wbx[:, woff + 512:woff + 1024]),
                 r=[wreg, "xstage", "hT_dead"], w=[("x", t)], tag="xnew")



def _xov(ctx, c0, c1, f32=False):
    if f32:
        return ctx.sb["x_res"][:, c0:c1]
    return ctx.sb["x_res"].bitcast(BF16)[:, c0:c1]


def emit_l1_weights(ctx):
    nc, p = ctx.nc, ctx.prog
    p.op("sync", lambda: nc.sync.dma_start(
        out=ctx.ins["xspillC"].rearrange("(t p) f -> p t f", p=128),
        in_=ctx.sb["x_res"][:, :].rearrange("p (t f) -> p t f", f=D)),
        r=[("x", t) for t in range(NT)], w=["xspC"], dma=True, tag="spillC")
    psT = ctx.ps["psT"]
    names = ["L1_attn_Wq", "L1_attn_Wk", "L1_attn_Wv", "L1_attn_Wg",
             "L1_attn_Wo"]
    stg = _xov(ctx, 40960, 49152).rearrange("p (c f) -> p c f", f=1024)
    for wi, name in enumerate(names):
        p.op("gpsimd", lambda name=name: nc.gpsimd.dma_start(
            out=stg, in_=ctx.ins[name].rearrange("(c p) f -> p c f", p=128)),
            r=["xspC", "wstg"], w=["wstg"], dma=True, tag="ldL1w")
        dstv = _xov(ctx, 8192 * wi, 8192 * (wi + 1)).rearrange(
            "p (c f) -> p c f", f=1024)
        for ot in range(8):
            for it in range(8):
                sl = (2 * ot) % 4
                p.op("tensor", lambda ot=ot, it=it, sl=sl: nc.tensor.transpose(
                    psT[0:128, 128 * sl:128 * sl + 128],
                    stg[:, ot, 128 * it:128 * it + 128],
                    ctx.sb["ident_bf"][:, :]),
                    r=["wstg", "ident_bf", "xspC"], w=["psT"], tag="l1wtr")
                eng = "vector" if it % 2 == 0 else "scalar"
                fn = nc.vector.tensor_copy if it % 2 == 0 else nc.scalar.copy
                p.op(eng, lambda ot=ot, it=it, sl=sl, fn=fn, dstv=dstv: fn(
                    dstv[:, it, 128 * ot:128 * ot + 128],
                    psT[0:128, 128 * sl:128 * sl + 128]),
                    r=["psT"], w=[("l1w", min(ot, 7))], tag="l1wtre")
        p.marker(ctx, r=[("l1w", i) for i in range(8)], w=[("wT", wi)],
                 tag="l1wdone")


def emit_l1_consts(ctx):
    nc, p = ctx.nc, ctx.prog
    for cname, sname, key in (("c_triu_incl", "ctrilu", "trilu"),
                              ("c_e127", "ce127", "e127"),
                              ("c_maskT", "cmaskT", "maskT")):
        p.op("sync", lambda cname=cname, sname=sname: nc.sync.dma_start(
            out=ctx.sb[sname][:, :], in_=ctx.ins[cname]),
            w=[key], dma=True, tag="ldc1")
    for cname, sname, key in (("c_m32bd", "cm32bd", "m32bd"),
                              ("c_m32off", "cm32off", "m32off")):
        p.op("gpsimd", lambda cname=cname, sname=sname: nc.gpsimd.dma_start(
            out=ctx.sb[sname][:, :], in_=ctx.ins[cname]),
            w=[key], dma=True, tag="ldc2")


L1WS_KEYS = (["Dw", "NTh", "t1h", "TTm", "Pm", "Ptm", "Rm", "um", "zk", "ym",
              "Vp", "o32", "Sf", "gt16", "kqd", "bkq", "nbe", "xo"] +
             [("o", h) for h in range(NH)] +
             [("albe", t) for t in range(NT)] +
             [("fm", w, s) for w in range(2) for s in range(8)])


def emit_l1_attn(ctx, ntiles=NT):
    nc, p = ctx.nc, ctx.prog
    st = ctx.sb["stats"]
    psA, psB, psC, psD = (ctx.ps[k] for k in ("psA", "psB", "psC", "psD"))
    psE, psF, psT = ctx.ps["psE"], ctx.ps["psF"], ctx.ps["psT"]
    wbf = ctx.sb["work_bf"]
    hTw = ctx.sb["hTw"]
    F = lambda a, b: ctx.sb["x_res"][:, a:b]
    albeF = F(25600, 26624)
    Dw = F(26624, 28672)
    NTh = F(28672, 28800)
    t1h = F(28800, 28928)
    TTm = F(28928, 29056)
    Pm = F(29056, 29184)
    Ptm = F(29184, 29312)
    Rm = F(29312, 29440)
    um = F(29440, 29504)
    zk = F(29504, 29568)
    ym = F(29568, 29632)
    Vp = F(29632, 29696)
    o32 = F(29696, 30720)
    Sf = F(30720, 31232)
    gt16 = F(31232, 31360)
    kqd = F(31360, 31376)
    bkq = F(31376, 31392)
    nbe = F(31392, 31408)
    xo = F(31424, 32448)
    ywb = ctx.sb["work_b"].bitcast(BF16)[:, 0:64]
    wab = ctx.sb["bemat"].bitcast(BF16)[:, 0:256]
    qfmb = ctx.sb["biast"]
    Sb = ctx.sb["kvsb"]

    p.marker(ctx, r=["xspC", ("biast", 0), ("biast", 1), "gwT", "wab",
                     ("wT", 0), ("wT", 1), ("wT", 2), ("wT", 3), ("wT", 4)],
             w=list(L1WS_KEYS) + ["onesbf"], tag="cfence")

    p.op("vector", lambda: nc.vector.tensor_copy(
        ctx.sb["gwT"][0:1, 0:128], ctx.sb["ones_row"][0:1, :]),
        r=["ones_row", "onesbf"], w=["onesbf"], tag="onesbf")
    for which, wname in ((0, "L1_attn_Wa"), (1, "L1_attn_Wb")):
        p.op("sync", lambda wname=wname: nc.sync.dma_start(
            out=ctx.sb["work_b"][0:16, 0:1024], in_=ctx.ins[wname]),
            r=["wab"], w=["wb01", "wb23"], dma=True, tag="ldWab")
        for ci in range(8):
            p.op("tensor", lambda ci=ci: nc.tensor.transpose(
                psF[0:128, 0:16],
                ctx.sb["work_b"][0:16, 128 * ci:128 * ci + 128],
                ctx.sb["ident_f32"][0:16, 0:16]),
                r=["wb01", "wb23", "ident_f32"], w=["psF"], tag="wabtr")
            p.op("vector", lambda ci=ci, which=which: nc.vector.tensor_copy(
                wab[:, 128 * which + 16 * ci:128 * which + 16 * ci + 16],
                psF[0:128, 0:16]),
                r=["psF"], w=["wab"], tag="wabtre")
    p.op("sync", lambda: nc.sync.dma_start(
        out=ctx.sb["mrows"][0:1, 0:16], in_=ctx.ins["L1_attn_ba"][None, :]),
        w=["mstage"], dma=True, tag="ldba")
    bcast_row(ctx, ctx.sb["mrows"][0:1, 0:16], ctx.sb["gb16"][:, 0:16],
              ["mstage"], "bab", bank="psD", tag="bab")
    p.op("sync", lambda: nc.sync.dma_start(
        out=ctx.sb["mrows"][0:1, 0:16], in_=ctx.ins["L1_attn_bb"][None, :]),
        r=["bab"], w=["mstage"], dma=True, tag="ldbb")
    bcast_row(ctx, ctx.sb["mrows"][0:1, 0:16], ctx.sb["cmb"][:, 0:16],
              ["mstage"], "bbb", bank="psD", tag="bbb")
    p.op("vector", lambda: nc.vector.memset(Sf, 0.0), r=["Sf"], w=["Sf"],
         tag="S0")
    p.op("vector", lambda: nc.vector.memset(Sb[:, :], 0.0),
         r=["kvsb"], w=["Sb"], tag="S0b")

    WQ, WK, WV, WG, WO = 0, 8192, 16384, 24576, 32768

    def wsl1(base, ci, f0, fn):
        o = base + 1024 * ci + f0
        return ctx.sb["x_res"].bitcast(BF16)[:, o:o + fn]

    for t in range(ntiles):
        p.op("sync", lambda t=t: nc.sync.dma_start(
            out=ctx.sb["xstage"][:, :],
            in_=ctx.ins["xspillC"][128 * t:128 * t + 128, :]),
            r=["xspC", "xstage"], w=["xstage"], dma=True, tag="ldxc")
        emit_h_tile(ctx, t, MOD_SLOT["ca1p"], MOD_SLOT["sa"], hTw, "hTw",
                    xt_ap=ctx.sb["xstage"][:, :], xt_key="xstage",
                    tag=f"hC{t}_")
        for name, base, dstq, extra in (("q", WQ, 0, "l2"), ("k", WK, 1, "l2"),
                                        ("v", WV, 2, "v"), ("g", WG, 3, "og")):
            for half in range(2):
                bank, bkey = (psA, "psA") if half == 0 else (psB, "psB")
                for ci in range(FT):
                    p.op("tensor", lambda half=half, ci=ci, bank=bank,
                         base=base: nc.tensor.matmul(
                             bank[0:128, 0:512],
                             hTw[:, 128 * ci:128 * ci + 128],
                             wsl1(base, ci, half * 512, 512),
                             start=(ci == 0), stop=(ci == FT - 1)),
                         r=[("hTw", ci), ("wT", base // 8192)], w=[bkey],
                         tag="prj" + name)
                dsl = wbf[:, 1024 * dstq + 512 * half:
                          1024 * dstq + 512 * half + 512]
                if extra == "l2":
                    p.op("vector", lambda bank=bank, half=half:
                         nc.vector.tensor_copy(
                             Dw[:, 512 * half:512 * half + 512],
                             bank[0:128, 0:512]),
                         r=[bkey, "Dw"], w=["Dw"], tag="l2ev")
                    p.op("scalar", lambda half=half:
                         nc.scalar.activation(
                             out=Dw[:, 1024 + 512 * half:1536 + 512 * half],
                             in_=Dw[:, 512 * half:512 * half + 512],
                             func=AF.Square),
                         r=["Dw"], w=["Dw"], tag="l2sq")
                    p.op("vector", lambda half=half: nc.vector.tensor_reduce(
                        st[:, 16 + 8 * half:24 + 8 * half],
                        Dw[:, 1024 + 512 * half:1536 + 512 * half].rearrange(
                            "p (h d) -> p h d", d=HD), AX.X, ALU.add),
                        r=["Dw"], w=[("st", 16)], tag="l2ss")
                    p.op("scalar", lambda half=half: nc.scalar.activation(
                        out=st[:, 32 + 8 * half:40 + 8 * half],
                        in_=st[:, 16 + 8 * half:24 + 8 * half], func=AF.Ln),
                        r=[("st", 16)], w=[("st", 32)], tag="l2ln")
                    p.op("scalar", lambda half=half: nc.scalar.activation(
                        out=st[:, 16 + 8 * half:24 + 8 * half],
                        in_=st[:, 32 + 8 * half:40 + 8 * half], func=AF.Exp,
                        scale=0.5),
                        r=[("st", 32)], w=[("st", 16)], tag="l2ex")
                    p.op("vector", lambda half=half: nc.vector.tensor_scalar(
                        out=st[:, 16 + 8 * half:24 + 8 * half],
                        in0=st[:, 16 + 8 * half:24 + 8 * half],
                        scalar1=1e-6, scalar2=None, op0=ALU.max),
                        r=[("st", 16)], w=[("st", 16)], tag="l2mx")
                    p.op("vector", lambda half=half: nc.vector.reciprocal(
                        st[:, 32 + 8 * half:40 + 8 * half],
                        st[:, 16 + 8 * half:24 + 8 * half]),
                        r=[("st", 16)], w=[("st", 32)], tag="l2rc")
                    p.op("gpsimd", lambda half=half, dsl=dsl,
                         dstq=dstq: nc.gpsimd.tensor_tensor(
                             out=dsl.rearrange("p (h d) -> p h d", d=HD),
                             in0=Dw[:, 512 * half:512 * half + 512].rearrange(
                                 "p (h d) -> p h d", d=HD),
                             in1=st[:, 32 + 8 * half:40 + 8 * half].rearrange(
                                 "p (h o) -> p h o", o=1).broadcast_to(
                                 [128, 8, HD]),
                             op=ALU.mult),
                         r=["Dw", ("st", 32)], w=[("wbf", dstq)], tag="l2n")
                elif extra == "v":
                    p.op("vector", lambda bank=bank, dsl=dsl, bkey=bkey:
                         nc.vector.tensor_copy(dsl, bank[0:128, 0:512]),
                         r=[bkey], w=[("wbf", dstq)], tag="vev")
                else:
                    p.op("scalar", lambda bank=bank, dsl=dsl, bkey=bkey:
                         nc.scalar.activation(out=dsl, in_=bank[0:128, 0:512],
                                              func=AF.Silu),
                         r=[bkey], w=[("wbf", dstq)], tag="ogev")
        qtm = wbf[:, 0:1024]
        ktm = wbf[:, 1024:2048]
        vtm = wbf[:, 2048:3072]
        og = wbf[:, 3072:4096]
        p.op("vector", lambda: nc.vector.tensor_mul(
            Dw[:, 0:1024], qtm, ktm), r=[("wbf", 0), ("wbf", 1), "Dw"],
            w=["Dw"], tag="kq")
        p.op("vector", lambda: nc.vector.tensor_reduce(
            kqd, Dw[:, 0:1024].rearrange("p (h d) -> p h d", d=HD),
            AX.X, ALU.add), r=["Dw", "kqd"], w=["kqd"], tag="kqd")
        for which in range(2):
            for ci in range(FT):
                p.op("tensor", lambda which=which, ci=ci: nc.tensor.matmul(
                    psE[0:128, 16 * which:16 * which + 16],
                    hTw[:, 128 * ci:128 * ci + 128],
                    wab[:, 128 * which + 16 * ci:128 * which + 16 * ci + 16],
                    start=(ci == 0 and which == 0), stop=(ci == FT - 1)),
                    r=[("hTw", ci), "wab"], w=["psE"], tag="abmm")
        p.op("vector", lambda: nc.vector.tensor_add(
            Dw[:, 0:16], psE[0:128, 0:16], ctx.sb["gb16"][:, 0:16]),
            r=["psE", "bab", "Dw"], w=["Dw"], tag="aadd")
        p.op("vector", lambda: nc.vector.tensor_add(
            Dw[:, 16:32], psE[0:128, 16:32], ctx.sb["cmb"][:, 0:16]),
            r=["psE", "bbb", "Dw"], w=["Dw"], tag="badd")
        p.op("scalar", lambda t=t: nc.scalar.activation(
            out=albeF[:, 32 * t:32 * t + 32], in_=Dw[:, 0:32],
            func=AF.Sigmoid),
            r=["Dw"], w=[("albe", t)], tag="absig")
        al_t = albeF[:, 32 * t:32 * t + 16]
        be_t = albeF[:, 32 * t + 16:32 * t + 32]
        p.op("scalar", lambda: nc.scalar.activation(
            out=st[:, 16:32], in_=Dw[:, 0:16], func=AF.Exp, scale=-1.0),
            r=["Dw"], w=[("st", 16)], tag="lga1")
        p.op("vector", lambda: nc.vector.tensor_scalar(
            out=st[:, 16:32], in0=st[:, 16:32], scalar1=1.0, scalar2=None,
            op0=ALU.add), r=[("st", 16)], w=[("st", 16)], tag="lga2")
        p.op("scalar", lambda: nc.scalar.activation(
            out=st[:, 16:32], in_=st[:, 16:32], func=AF.Ln),
            r=[("st", 16)], w=[("st", 16)], tag="lga3")
        p.op("tensor", lambda: nc.tensor.matmul(
            psF[0:128, 0:16], ctx.sb["ctrilu"][:, :], st[:, 16:32],
            start=True, stop=True),
            r=[("st", 16), "trilu"], w=["psF"], tag="gmm")
        p.op("vector", lambda: nc.vector.tensor_copy(
            st[:, 80:96], psF[0:128, 0:16]),
            r=["psF"], w=[("st", 80)], tag="gev")
        p.op("scalar", lambda: nc.scalar.activation(
            out=st[:, 48:64], in_=st[:, 80:96], func=AF.Exp),
            r=[("st", 80)], w=[("st", 48)], tag="Gex")
        p.op("tensor", lambda: nc.tensor.matmul(
            psF[0:128, 16:32], ctx.sb["ce127"][:, :], st[:, 48:64],
            start=True, stop=True),
            r=[("st", 48), "e127"], w=["psF"], tag="gcmm")
        p.op("vector", lambda: nc.vector.tensor_copy(
            st[:, 96:112], psF[0:128, 16:32]),
            r=["psF"], w=[("st", 96)], tag="gcev")
        p.op("vector", lambda: nc.vector.reciprocal(
            st[:, 64:80], st[:, 48:64]),
            r=[("st", 48)], w=[("st", 64)], tag="recG")
        p.op("vector", lambda be_t=be_t, t=t: nc.vector.tensor_mul(
            st[:, 112:128], be_t, st[:, 96:112]),
            r=[("albe", t), ("st", 96)], w=[("st", 112)], tag="wv1")
        p.op("vector", lambda: nc.vector.tensor_mul(
            st[:, 112:128], st[:, 112:128], st[:, 64:80]),
            r=[("st", 112), ("st", 64)], w=[("st", 112)], tag="wv2")
        p.op("vector", lambda be_t=be_t, t=t: nc.vector.tensor_mul(
            bkq, be_t, kqd), r=[("albe", t), "kqd", "bkq"], w=["bkq"],
            tag="bkq")
        p.op("vector", lambda be_t=be_t, t=t: nc.vector.tensor_scalar(
            out=nbe, in0=be_t, scalar1=-1.0, scalar2=None, op0=ALU.mult),
            r=[("albe", t), "nbe"], w=["nbe"], tag="nbe")
        p.op("tensor", lambda: nc.tensor.transpose(
            psF[0:16, 32:160], st[:, 80:96], ctx.sb["ident_f32"][:, :]),
            r=[("st", 80), "ident_f32"], w=["psF"], tag="gtr")
        p.op("vector", lambda: nc.vector.tensor_copy(
            gt16[0:16, :], psF[0:16, 32:160]),
            r=["psF", "gt16"], w=["gt16"], tag="gtev")
        for qd in range(4):
            if qd % 2 == 0:
                gh = qd // 2
                p.op("sync", lambda gh=gh: nc.sync.dma_start(
                    out=ctx.sb["mrows"][0:1, 0:1024].rearrange(
                        "o (h i) -> o h i", i=128),
                    in_=gt16[8 * gh:8 * gh + 8, :]),
                    r=["gt16", "mstage"], w=["mstage"], dma=True,
                    tag="growdma")
            p.op("tensor", lambda qd=qd: nc.tensor.matmul(
                psC[0:128, 0:512], ctx.sb["ones_row"][0:1, :],
                ctx.sb["mrows"][0:1, 512 * (qd % 2):512 * (qd % 2) + 512],
                start=True, stop=True),
                r=["mstage", "ones_row"], w=["psC"], tag="xbc")
            dwq = Dw[:, 512 * qd:512 * qd + 512].rearrange(
                "p (h i) -> p h i", i=128)
            p.op("vector", lambda qd=qd, dwq=dwq: nc.vector.tensor_tensor(
                out=dwq,
                in0=psC[0:128, 0:512].rearrange("p (h i) -> p h i", i=128),
                in1=st[:, 80 + 4 * qd:84 + 4 * qd].rearrange(
                    "p (h o) -> p h o", o=1).broadcast_to([128, 4, 128]),
                op=ALU.subtract),
                r=["psC", ("st", 80), "Dw"], w=["Dw"], tag="m1a")
            p.op("gpsimd", lambda qd=qd, dwq=dwq: nc.gpsimd.tensor_tensor(
                out=dwq, in0=dwq,
                in1=ctx.sb["cmaskT"][:, :].rearrange(
                    "p (o i) -> p o i", o=1).broadcast_to([128, 4, 128]),
                op=ALU.add),
                r=["Dw", "maskT"], w=["Dw"], tag="m1c")
            p.op("scalar", lambda qd=qd: nc.scalar.activation(
                out=Dw[:, 512 * qd:512 * qd + 512],
                in_=Dw[:, 512 * qd:512 * qd + 512], func=AF.Exp),
                r=["Dw"], w=["Dw"], tag="m1e")
        p.op("gpsimd", lambda: nc.gpsimd.tensor_tensor(
            out=Dw[:, 0:2048].rearrange("p (h i) -> p h i", i=128),
            in0=Dw[:, 0:2048].rearrange("p (h i) -> p h i", i=128),
            in1=nbe.rearrange("p (h o) -> p h o", o=1).broadcast_to(
                [128, 16, 128]),
            op=ALU.mult),
            r=["Dw", "nbe"], w=["Dw"], tag="m1f")

        for h in range(NH):
            pb = 64 * (h % 2)
            sub = h // 2
            if h % 2 == 0:
                for which, srcq in ((0, 0), (1, 1)):
                    p.op("tensor", lambda sub=sub, which=which, srcq=srcq:
                         nc.tensor.transpose(
                             psT[0:128, 256 * which:256 * which + 128],
                             wbf[:, 1024 * srcq + 128 * sub:
                                 1024 * srcq + 128 * sub + 128],
                             ctx.sb["ident_bf"][:, :]),
                         r=[("wbf", srcq), "ident_bf"], w=["psT"],
                         tag="qktr")
                    p.op("scalar", lambda sub=sub, which=which:
                         nc.scalar.copy(
                             qfmb[:, 1024 * which + 128 * sub:
                                  1024 * which + 128 * sub + 128],
                             psT[0:128, 256 * which:256 * which + 128]),
                         r=["psT"], w=[("fm", which, sub)], tag="qktre")
            qfm = qfmb[pb:pb + 64, 128 * sub:128 * sub + 128]
            kfm = qfmb[pb:pb + 64, 1024 + 128 * sub:1024 + 128 * sub + 128]
            Dh = Dw[:, 128 * h:128 * h + 128]
            Ssl_b = Sb[pb:pb + 64, 64 * sub:64 * sub + 64]
            p.op("tensor", lambda kfm=kfm, Ssl_b=Ssl_b: nc.tensor.matmul(
                psA[0:128, 0:64], kfm, Ssl_b, start=True, stop=True),
                r=[("fm", 1, sub), "Sb"], w=["psA"], tag="ks0")
            p.op("vector", lambda h=h: nc.vector.tensor_scalar(
                out=um, in0=psA[0:128, 0:64], scalar1=st[:, 48 + h:49 + h],
                scalar2=None, op0=ALU.mult),
                r=["psA", ("st", 48), "um"], w=["um"], tag="vp1")
            p.op("vector", lambda h=h: nc.vector.tensor_sub(
                Vp, vtm[:, 64 * h:64 * h + 64], um),
                r=[("wbf", 2), "um", "Vp"], w=["Vp"], tag="vp2")
            p.op("tensor", lambda kfm=kfm: nc.tensor.matmul(
                psC[0:128, 0:128], kfm, kfm, start=True, stop=True),
                r=[("fm", 1, sub)], w=["psC"], tag="kk")
            p.op("tensor", lambda kfm=kfm, qfm=qfm: nc.tensor.matmul(
                psD[0:128, 0:128], kfm, qfm, start=True, stop=True),
                r=[("fm", 0, sub), ("fm", 1, sub)], w=["psD"], tag="kq2")
            p.op("vector", lambda Dh=Dh: nc.vector.tensor_mul(
                NTh, psC[0:128, 0:128], Dh),
                r=["psC", "Dw", "NTh"], w=["NTh"], tag="nth")
            p.op("vector", lambda Dh=Dh: nc.vector.tensor_mul(
                t1h, psD[0:128, 0:128], Dh),
                r=["psD", "Dw", "t1h"], w=["t1h"], tag="t1h")
            p.op("gpsimd", lambda: nc.gpsimd.tensor_mul(
                Pm, NTh, ctx.sb["cm32bd"][:, :]),
                r=["NTh", "m32bd", "Pm"], w=["Pm"], tag="p0")
            p.op("gpsimd", lambda: nc.gpsimd.tensor_mul(
                Rm, NTh, ctx.sb["cm32off"][:, :]),
                r=["NTh", "m32off", "Rm"], w=["Rm"], tag="rmask")
            p.op("vector", lambda: nc.vector.tensor_add(
                TTm, Pm, ctx.sb["ident_f32"][:, :]),
                r=["Pm", "ident_f32", "TTm"], w=["TTm"], tag="tt0")
            for lev in range(4):
                p.op("vector", lambda: nc.vector.transpose(Ptm, Pm),
                     r=["Pm", "Ptm"], w=["Ptm"], tag="ptr")
                p.op("tensor", lambda: nc.tensor.matmul(
                    psE[0:128, 128:256], Ptm, Pm, start=True, stop=True),
                    r=["Ptm", "Pm"], w=["psE"], tag="psq")
                p.op("vector", lambda: nc.vector.tensor_copy(
                    Pm, psE[0:128, 128:256]),
                    r=["psE", "Pm"], w=["Pm"], tag="pev")
                p.op("vector", lambda: nc.vector.transpose(Ptm, Pm),
                     r=["Pm", "Ptm"], w=["Ptm"], tag="ptr2")
                p.op("tensor", lambda: nc.tensor.matmul(
                    psE[0:128, 128:256], Ptm, TTm, start=True, stop=True),
                    r=["Ptm", "TTm"], w=["psE"], tag="ttmm")
                p.op("vector", lambda: nc.vector.tensor_add(
                    TTm, TTm, psE[0:128, 128:256]),
                    r=["TTm", "psE"], w=["TTm"], tag="ttadd")
            p.op("tensor", lambda: nc.tensor.matmul(
                psB[0:128, 0:64], TTm, Vp, start=True, stop=True),
                r=["TTm", "Vp"], w=["psB"], tag="umm")
            p.op("vector", lambda: nc.vector.tensor_copy(
                um, psB[0:128, 0:64]), r=["psB", "um"], w=["um"], tag="uev")
            p.op("vector", lambda: nc.vector.tensor_copy(ym, um),
                 r=["um", "ym"], w=["ym"], tag="y0")
            for it in range(3):
                p.op("tensor", lambda: nc.tensor.matmul(
                    psB[0:128, 64:128], Rm, um, start=True, stop=True),
                    r=["Rm", "um"], w=["psB"], tag="rmm")
                p.op("vector", lambda: nc.vector.tensor_copy(
                    zk, psB[0:128, 64:128]), r=["psB", "zk"], w=["zk"],
                    tag="zev")
                p.op("tensor", lambda: nc.tensor.matmul(
                    psB[0:128, 0:64], TTm, zk, start=True, stop=True),
                    r=["TTm", "zk"], w=["psB"], tag="tzmm")
                p.op("vector", lambda: nc.vector.tensor_copy(
                    um, psB[0:128, 0:64]), r=["psB", "um"], w=["um"],
                    tag="zev2")
                p.op("vector", lambda: nc.vector.tensor_add(ym, ym, um),
                     r=["ym", "um"], w=["ym"], tag="yacc")
            osl = o32[:, 64 * h:64 * h + 64]
            p.op("tensor", lambda qfm=qfm, Ssl_b=Ssl_b: nc.tensor.matmul(
                psA[0:128, 64:128], qfm, Ssl_b, start=True, stop=True),
                r=[("fm", 0, sub), "Sb"], w=["psA"], tag="qs0")
            p.op("vector", lambda osl=osl, h=h: nc.vector.tensor_scalar(
                out=osl, in0=psA[0:128, 64:128], scalar1=st[:, 48 + h:49 + h],
                scalar2=None, op0=ALU.mult),
                r=["psA", ("st", 48), ("o", h)], w=[("o", h)], tag="ocr")
            p.op("tensor", lambda: nc.tensor.matmul(
                psB[0:128, 64:128], t1h, ym, start=True, stop=True),
                r=["t1h", "ym"], w=["psB"], tag="atty")
            p.op("vector", lambda osl=osl: nc.vector.tensor_sub(
                osl, osl, psB[0:128, 64:128]),
                r=[("o", h), "psB"], w=[("o", h)], tag="oatt")
            p.op("vector", lambda osl=osl, h=h: nc.vector.scalar_tensor_tensor(
                out=osl, in0=ym, scalar=bkq[:, h:h + 1], in1=osl,
                op0=ALU.mult, op1=ALU.add),
                r=["ym", "bkq", ("o", h)], w=[("o", h)], tag="odiag")
            p.op("vector", lambda h=h: nc.vector.tensor_scalar(
                out=zk, in0=ym, scalar1=st[:, 112 + h:113 + h], scalar2=None,
                op0=ALU.mult),
                r=["ym", ("st", 112), "zk"], w=["zk"], tag="yw")
            p.op("vector", lambda: nc.vector.tensor_copy(ywb, zk),
                 r=["zk", "wb01"], w=["wb01"], tag="ywb")
            p.op("tensor", lambda h=h, pb=pb: nc.tensor.matmul(
                psE[pb:pb + 64, 0:64], wbf[:, 1024 + 64 * h:1024 + 64 * h + 64],
                ywb, start=True, stop=True),
                r=[("wbf", 1), "wb01"], w=["psE"], tag="smm")
            Ssl = Sf[pb:pb + 64, 64 * sub:64 * sub + 64]
            p.op("vector", lambda Ssl=Ssl, h=h, pb=pb: nc.vector.tensor_scalar(
                out=Ssl, in0=Ssl, scalar1=st[pb:pb + 64, 96 + h:97 + h],
                scalar2=None, op0=ALU.mult),
                r=["Sf", ("st", 96)], w=["Sf"], tag="sdec")
            p.op("vector", lambda Ssl=Ssl, pb=pb: nc.vector.tensor_add(
                Ssl, Ssl, psE[pb:pb + 64, 0:64]),
                r=["Sf", "psE"], w=["Sf"], tag="supd")
            p.op("vector", lambda Ssl=Ssl, Ssl_b=Ssl_b: nc.vector.tensor_copy(
                Ssl_b, Ssl), r=["Sf", "Sb"], w=["Sb"], tag="sbf")
        p.op("vector", lambda: nc.vector.tensor_mul(
            wbf[:, 0:1024], o32[:, 0:1024], og),
            r=[("o", h2) for h2 in range(NH)] + [("wbf", 3), ("wbf", 0)],
            w=[("wbf", 0)], tag="oog")
        for c in range(FT):
            p.op("tensor", lambda c=c: nc.tensor.transpose(
                psT[0:128, 128 * (c % 4):128 * (c % 4) + 128],
                wbf[:, 128 * c:128 * c + 128], ctx.sb["ident_bf"][:, :]),
                r=[("wbf", 0), "ident_bf"], w=["psT"], tag="otr")
            eng = "vector" if c % 2 == 0 else "scalar"
            fn = nc.vector.tensor_copy if c % 2 == 0 else nc.scalar.copy
            p.op(eng, lambda c=c, fn=fn: fn(
                wbf[:, 1024 + 128 * c:1024 + 128 * c + 128],
                psT[0:128, 128 * (c % 4):128 * (c % 4) + 128]),
                r=["psT"], w=[("wbf", 1)], tag="otre")
        for half in range(2):
            bank, bkey = (psA, "psA") if half == 0 else (psB, "psB")
            for ci in range(FT):
                p.op("tensor", lambda half=half, ci=ci, bank=bank:
                     nc.tensor.matmul(
                         bank[0:128, 0:512],
                         wbf[:, 1024 + 128 * ci:1024 + 128 * ci + 128],
                         wsl1(WO, ci, half * 512, 512),
                         start=(ci == 0), stop=(ci == FT - 1)),
                     r=[("wbf", 1), ("wT", 4)], w=[bkey], tag="womm")
            p.op("vector", lambda half=half, bank=bank, bkey=bkey:
                 nc.vector.tensor_mul(
                     xo[:, 512 * half:512 * half + 512], bank[0:128, 0:512],
                     ctx.sb["modt"][:, 2 * D + 512 * half:
                                    2 * D + 512 * half + 512]),
                 r=[bkey, ("modt", 2), "xo"], w=["xo"], tag="xga")
            p.op("vector", lambda half=half: nc.vector.tensor_add(
                xo[:, 512 * half:512 * half + 512],
                xo[:, 512 * half:512 * half + 512],
                ctx.sb["xstage"][:, 512 * half:512 * half + 512]),
                r=["xo", "xstage"], w=["xo"], tag="xadd")
        p.op("sync", lambda t=t: nc.sync.dma_start(
            out=ctx.ins["x3"][128 * t:128 * t + 128, :], in_=xo),
            r=["xo"], dma=True, tag="stx3")
        if t == 0 and "d_st" in ctx.ins:
            p.op("sync", lambda: nc.sync.dma_start(
                out=ctx.ins["d_st"], in_=st[:, :]),
                r=[("st", 16), ("st", 32), ("st", 48), ("st", 64), ("st", 80),
                   ("st", 96), ("st", 112)], dma=True, tag="dbg")
            p.op("sync", lambda: nc.sync.dma_start(
                out=ctx.ins["d_Dw"], in_=Dw), r=["Dw"], dma=True, tag="dbg")
            p.op("gpsimd", lambda: nc.gpsimd.dma_start(
                out=ctx.ins["d_qk"], in_=wbf[:, 0:4096]),
                r=[("wbf", 0), ("wbf", 1), ("wbf", 2), ("wbf", 3)],
                dma=True, tag="dbg")
            p.op("sync", lambda: nc.sync.dma_start(
                out=ctx.ins["d_o"], in_=o32), r=[("o", hh) for hh in range(NH)],
                dma=True, tag="dbg")
            p.op("sync", lambda: nc.sync.dma_start(
                out=ctx.ins["d_S"], in_=Sf), r=["Sf"], dma=True, tag="dbg")
            p.op("gpsimd", lambda: nc.gpsimd.dma_start(
                out=ctx.ins["d_grow"],
                in_=ctx.sb["mrows"].bitcast(BF16)[0:1, 0:2048]),
                r=["mstage"], dma=True, tag="dbg")



def emit_quant(ctx, src_ap, src_key, dst_ap, dst_key, work0, wkey, tag="qz"):
    nc, p = ctx.nc, ctx.prog
    v3 = lambda ap: ap.rearrange("p (b k) -> p b k", k=32)
    mx = work0[:, 1024:1056]
    rs = work0[:, 1056:1088]
    u = work0[:, 0:1024]
    p.op("vector", lambda: nc.vector.tensor_reduce(
        mx, v3(src_ap), AX.X, ALU.max, apply_absolute_value=True),
        r=[src_key, wkey], w=[wkey], tag=tag + "mx")
    p.op("vector", lambda: nc.vector.tensor_scalar(
        out=mx, in0=mx, scalar1=1.0 / 7.5, scalar2=1e-30,
        op0=ALU.mult, op1=ALU.max),
        r=[wkey], w=[wkey], tag=tag + "sc")
    p.op("vector", lambda: nc.vector.reciprocal(rs, mx),
         r=[wkey], w=[wkey], tag=tag + "rc")
    p.op("gpsimd", lambda: nc.gpsimd.tensor_tensor(
        out=v3(u), in0=v3(src_ap),
        in1=rs.rearrange("p (b o) -> p b o", o=1).broadcast_to([128, 32, 32]),
        op=ALU.mult),
        r=[src_key, wkey], w=[wkey], tag=tag + "u")
    ex = work0[:, 1088:2112]
    p.op("vector", lambda: nc.vector.tensor_single_scalar(
        out=ex.bitcast(U32), in_=u.bitcast(U32), scalar=0x7F800000,
        op=ALU.bitwise_and),
        r=[wkey], w=[wkey], tag=tag + "ex")
    p.op("vector", lambda: nc.vector.tensor_scalar(
        out=ex, in0=ex, scalar1=1.0, scalar2=float(3 * 2 ** 19),
        op0=ALU.max, op1=ALU.mult),
        r=[wkey], w=[wkey], tag=tag + "S")
    p.op("vector", lambda: nc.vector.tensor_add(u, u, ex),
         r=[wkey], w=[wkey], tag=tag + "t")
    p.op("vector", lambda: nc.vector.tensor_sub(u, u, ex),
         r=[wkey], w=[wkey], tag=tag + "q")
    p.op("gpsimd", lambda: nc.gpsimd.tensor_tensor(
        out=v3(dst_ap), in0=v3(u),
        in1=mx.rearrange("p (b o) -> p b o", o=1).broadcast_to([128, 32, 32]),
        op=ALU.mult),
        r=[wkey], w=[dst_key], tag=tag + "d")


def emit_l1_ffn(ctx):
    nc, p = ctx.nc, ctx.prog
    psA, psB, psT = ctx.ps["psA"], ctx.ps["psB"], ctx.ps["psT"]
    F = lambda a, b: ctx.sb["x_res"][:, a:b]
    hF = F(0, 1024)
    qw = F(1024, 3136)
    wstage = F(4096, 5120)
    wq = ctx.sb["x_res"].bitcast(BF16)[:, 16384:24576]
    wqT = ctx.sb["x_res"].bitcast(BF16)[:, 24576:32768]
    xd = ctx.sb["work_bf"][:, 0:1024]
    xdT = ctx.sb["work_bf"][:, 1024:2048]
    xo = F(31424, 32448)
    wbx = ctx.sb["work_b"]
    p.marker(ctx, r=[],
             w=list(L1WS_KEYS) + ["hF", "qw", "wstage", "wq", "wqT",
                ("wT", 0), ("wT", 1), ("wT", 2), ("wT", 3),
                ("biast", 0), ("biast", 1)], tag="dfence")
    emit_bias_tile(ctx, "L1_ffn_bias", 0)
    for ci in range(8):
        p.op("sync", lambda ci=ci: nc.sync.dma_start(
            out=wstage, in_=ctx.ins["L1_ffn_weight"][128 * ci:128 * ci + 128, :]),
            r=["wstage"], w=["wstage"], dma=True, tag="ldWd")
        emit_quant(ctx, wstage, "wstage", wq[:, 1024 * ci:1024 * ci + 1024],
                   "wq", qw, "qw", tag=f"qw{ci}")
    for ot in range(8):
        for it in range(8):
            sl = (2 * it) % 4
            p.op("tensor", lambda ot=ot, it=it, sl=sl: nc.tensor.transpose(
                psT[0:128, 128 * sl:128 * sl + 128],
                wq[:, 1024 * ot + 128 * it:1024 * ot + 128 * it + 128],
                ctx.sb["ident_bf"][:, :]),
                r=["wq", "ident_bf"], w=["psT"], tag="wdtr")
            eng = "vector" if it % 2 == 0 else "scalar"
            fn = nc.vector.tensor_copy if it % 2 == 0 else nc.scalar.copy
            p.op(eng, lambda ot=ot, it=it, sl=sl, fn=fn: fn(
                wqT[:, 1024 * it + 128 * ot:1024 * it + 128 * ot + 128],
                psT[0:128, 128 * sl:128 * sl + 128]),
                r=["psT"], w=["wqT"], tag="wdtre")
    for t in range(NT):
        p.op("sync", lambda t=t: nc.sync.dma_start(
            out=ctx.sb["xstage"][:, :],
            in_=ctx.ins["x3"][128 * t:128 * t + 128, :]),
            r=["xstage"], w=["xstage"], dma=True, tag="ldxd")
        emit_h_tile(ctx, t, MOD_SLOT["cb1p"], MOD_SLOT["sb"], hF, "hF",
                    transpose="f32", xt_ap=ctx.sb["xstage"][:, :],
                    xt_key="xstage", tag=f"hD{t}_")
        emit_quant(ctx, hF, "hF", xd, ("wbf", 0), qw, "qw", tag="qx")
        for c in range(FT):
            p.op("tensor", lambda c=c: nc.tensor.transpose(
                psT[0:128, 128 * (c % 4):128 * (c % 4) + 128],
                xd[:, 128 * c:128 * c + 128], ctx.sb["ident_bf"][:, :]),
                r=[("wbf", 0), "ident_bf"], w=["psT"], tag="xdtr")
            eng = "vector" if c % 2 == 0 else "scalar"
            fn = nc.vector.tensor_copy if c % 2 == 0 else nc.scalar.copy
            p.op(eng, lambda c=c, fn=fn: fn(
                xdT[:, 128 * c:128 * c + 128],
                psT[0:128, 128 * (c % 4):128 * (c % 4) + 128]),
                r=["psT"], w=[("wbf", 1)], tag="xdtre")
        for half in range(2):
            bank, bkey = (psA, "psA") if half == 0 else (psB, "psB")
            for ci in range(FT):
                p.op("tensor", lambda half=half, ci=ci, bank=bank:
                     nc.tensor.matmul(
                         bank[0:128, 0:512],
                         xdT[:, 128 * ci:128 * ci + 128],
                         wqT[:, 1024 * ci + 512 * half:
                             1024 * ci + 512 * half + 512],
                         start=(ci == 0), stop=(ci == FT - 1)),
                     r=[("wbf", 1), "wqT"], w=[bkey], tag="ffmm")
            woff = 0 if half == 0 else 1024
            wreg = "wb01" if half == 0 else "wb23"
            p.op("vector", lambda half=half, bank=bank, woff=woff:
                 nc.vector.tensor_add(
                     wbx[:, woff:woff + 512], bank[0:128, 0:512],
                     ctx.sb["biast"][:, 512 * half:512 * half + 512]),
                 r=[bkey, ("biast", 0)], w=[wreg], tag="ffb")
            p.op("gpsimd", lambda half=half, woff=woff: nc.gpsimd.tensor_mul(
                wbx[:, woff + 512:woff + 1024], wbx[:, woff:woff + 512],
                ctx.sb["modt"][:, 2 * D + 512 * half:2 * D + 512 * half + 512]),
                r=[wreg, ("modt", 2)], w=[wreg], tag="ffg")
            p.op("vector", lambda half=half, woff=woff: nc.vector.tensor_add(
                xo[:, 512 * half:512 * half + 512],
                ctx.sb["xstage"][:, 512 * half:512 * half + 512],
                wbx[:, woff + 512:woff + 1024]),
                r=[wreg, "xstage", "xo"], w=["xo"], tag="ffx")
        p.op("sync", lambda t=t: nc.sync.dma_start(
            out=ctx.ins["out"][128 * t:128 * t + 128, :], in_=xo),
            r=["xo"], dma=True, tag="stout")



FUSED = True
NCORES = 8


def emit_global_fence(ctx, tag="gfence"):
    p = ctx.prog
    keys = set(p.last_w) | set(p.readers)
    p.marker(ctx, r=[], w=list(keys), tag=tag)


def _wspec():
    s = {}
    for w in ("Wq", "Wk", "Wv", "Wo"):
        s[f"L0_attn_{w}"] = ((1024, 1024), F32, "ExternalInput")
    for b in ("bq", "bk", "bv", "bo"):
        s[f"L0_attn_{b}"] = ((1024,), F32, "ExternalInput")
    s.update({
        "L0_ffn_gate_w": ((16, 1024), F32, "ExternalInput"),
        "L0_ffn_gate_b": ((16,), F32, "ExternalInput"),
        "L0_ffn_We1": ((16, 1024, 512), F32, "ExternalInput"),
        "L0_ffn_be1": ((16, 512), F32, "ExternalInput"),
        "L0_ffn_We2": ((16, 512, 1024), F32, "ExternalInput"),
        "L0_ffn_be2": ((16, 1024), F32, "ExternalInput"),
        "L0_ffn_We3": ((16, 1024, 512), F32, "ExternalInput"),
        "L0_ffn_be3": ((16, 512), F32, "ExternalInput"),
        "L0_ffn_ws1": ((1024, 1024), F32, "ExternalInput"),
        "L0_ffn_bs1": ((1024,), F32, "ExternalInput"),
        "L0_ffn_ws2": ((1024, 1024), F32, "ExternalInput"),
        "L0_ffn_bs2": ((1024,), F32, "ExternalInput"),
        "L0_ffn_ws3": ((1024, 1024), F32, "ExternalInput"),
        "L0_ffn_bs3": ((1024,), F32, "ExternalInput"),
    })
    for w in ("Wq", "Wk", "Wv", "Wg", "Wo"):
        s[f"L1_attn_{w}"] = ((1024, 1024), F32, "ExternalInput")
    for w in ("Wa", "Wb"):
        s[f"L1_attn_{w}"] = ((16, 1024), F32, "ExternalInput")
    for b in ("ba", "bb"):
        s[f"L1_attn_{b}"] = ((16,), F32, "ExternalInput")
    s["L1_ffn_weight"] = ((1024, 1024), F32, "ExternalInput")
    s["L1_ffn_bias"] = ((1024,), F32, "ExternalInput")
    s["L0_c"] = ((1024,), F32, "ExternalInput")
    s["L0_wmod"] = ((1024, 6144), F32, "ExternalInput")
    s["L1_c"] = ((1024,), F32, "ExternalInput")
    s["L1_wmod"] = ((1024, 6144), F32, "ExternalInput")
    return s


_WS = _wspec()


def _pick(*names):
    s = {}
    for n in names:
        s[n] = _WS[n]
    return s


def _stageA_spec():
    s = {"x": ((T, D), F32, "ExternalInput"),
         "out": ((T, D), F32, "ExternalOutput")}
    s.update(_pick("L0_c", "L0_wmod", *[f"L0_attn_{w}" for w in
                                        ("Wq", "Wk", "Wv", "Wo",
                                         "bq", "bk", "bv", "bo")]))
    s.update(const_ins_spec())
    return s


def _stageB_spec():
    s = {"x": ((T, D), F32, "ExternalInput"),
         "out": ((T, D), F32, "ExternalOutput"),
         "xspill": ((T, D), F32, "Internal")}
    s.update(_pick("L0_c", "L0_wmod",
                   *[k for k in _WS if k.startswith("L0_ffn")]))
    s.update(const_ins_spec())
    return s


def _stageC_spec():
    s = {"x": ((T, D), F32, "ExternalInput"),
         "x3": ((T, D), F32, "ExternalOutput"),
         "xspillC": ((T, D), F32, "Internal")}
    s.update(_pick("L1_c", "L1_wmod",
                   *[k for k in _WS if k.startswith("L1_attn")]))
    s.update(const_ins_spec())
    return s


def _stageD_spec():
    s = {"x3": ((T, D), F32, "ExternalInput"),
         "out": ((T, D), F32, "ExternalOutput")}
    s.update(_pick("L1_c", "L1_wmod", "L1_ffn_weight", "L1_ffn_bias"))
    s.update(const_ins_spec())
    return s


def _fused_spec():
    s = {"x": ((T, D), F32, "ExternalInput"),
         "out": ((T, D), F32, "ExternalOutput"),
         "xspill": ((T, D), F32, "Internal"),
         "xspillC": ((T, D), F32, "Internal"),
         "x3": ((T, D), F32, "Internal")}
    s.update(_WS)
    s.update(const_ins_spec())
    return s


def _emitA(ctx):
    load_consts(ctx)
    load_x(ctx)
    emit_mod(ctx, 0, 0)
    emit_l0_attn(ctx)
    store_x(ctx)


def _emitB(ctx):
    load_consts(ctx)
    load_x(ctx)
    emit_mod(ctx, 0, 1)
    emit_moe(ctx)
    store_x(ctx)


def _emitC(ctx):
    load_consts(ctx)
    emit_l1_consts(ctx)
    load_x(ctx)
    emit_mod(ctx, 1, 0)
    emit_l1_weights(ctx)
    emit_l1_attn(ctx)


def _emitD(ctx):
    load_consts(ctx)
    emit_mod(ctx, 1, 1)
    emit_l1_ffn(ctx)


def _emitF(ctx):
    load_consts(ctx)
    load_x(ctx)
    emit_mod(ctx, 0, 0)
    emit_l0_attn(ctx)
    emit_mod(ctx, 0, 1)
    emit_moe(ctx)
    emit_global_fence(ctx)
    emit_l1_consts(ctx)
    emit_mod(ctx, 1, 0)
    emit_l1_weights(ctx)
    emit_l1_attn(ctx)
    emit_mod(ctx, 1, 1)
    emit_l1_ffn(ctx)


_PROG_CACHE = {}


def _get_prog(name, specf, emitf):
    if name not in _PROG_CACHE:
        _PROG_CACHE[name] = build_program(specf(), emitf)
    return _PROG_CACHE[name][0]


def _launch(nc, spec, per_core):
    res = run_bass_kernel_spmd(nc, per_core, core_ids=list(range(NCORES)))
    return res


def kernel(**inputs):
    ins = {k: np.asarray(v, np.float32) for k, v in inputs.items()}
    consts = make_const_arrays()
    x = np.ascontiguousarray(ins["L0_attn_hidden_states"].reshape(NCORES, T, D))

    def wmaps(prefix_keys, extra):
        maps = []
        for b in range(NCORES):
            m = {}
            for k in prefix_keys:
                v = ins[k]
                m[k] = np.ascontiguousarray(v[b]) if k in ("L0_c", "L1_c") \
                    else v
            m.update(extra(b))
            m.update(consts)
            maps.append(m)
        return maps

    if FUSED:
        nc = _get_prog("F", _fused_spec, _emitF)
        keys = [k for k in _WS]
        res = _launch(nc, None, wmaps(keys, lambda b: {"x": x[b]}))
        out = np.stack([np.asarray(res.results[b]["out"])
                        for b in range(NCORES)])
        return out.reshape(NCORES, T, D).astype(np.float32)

    ncA = _get_prog("A", _stageA_spec, _emitA)
    kA = ["L0_c", "L0_wmod"] + [f"L0_attn_{w}" for w in
                                ("Wq", "Wk", "Wv", "Wo", "bq", "bk", "bv",
                                 "bo")]
    resA = _launch(ncA, None, wmaps(kA, lambda b: {"x": x[b]}))
    x1 = [np.asarray(resA.results[b]["out"]) for b in range(NCORES)]

    ncB = _get_prog("B", _stageB_spec, _emitB)
    kB = ["L0_c", "L0_wmod"] + [k for k in _WS if k.startswith("L0_ffn")]
    resB = _launch(ncB, None, wmaps(kB, lambda b: {"x": x1[b]}))
    x2 = [np.asarray(resB.results[b]["out"]) for b in range(NCORES)]

    ncC = _get_prog("C", _stageC_spec, _emitC)
    kC = ["L1_c", "L1_wmod"] + [k for k in _WS if k.startswith("L1_attn")]
    resC = _launch(ncC, None, wmaps(kC, lambda b: {"x": x2[b]}))
    x3 = [np.asarray(resC.results[b]["x3"]) for b in range(NCORES)]

    ncD = _get_prog("D", _stageD_spec, _emitD)
    kD = ["L1_c", "L1_wmod", "L1_ffn_weight", "L1_ffn_bias"]
    resD = _launch(ncD, None, wmaps(kD, lambda b: {"x3": x3[b]}))
    out = np.stack([np.asarray(resD.results[b]["out"])
                    for b in range(NCORES)])
    return out.reshape(NCORES, T, D).astype(np.float32)
